# version 07
# baseline (speedup 1.0000x reference)
import numpy as np
import concourse.bass as bass
import concourse.mybir as mybir
from concourse.bass_utils import run_bass_kernel_spmd
from concourse.alu_op_type import AluOpType as OP

F32 = mybir.dt.float32
BF16 = mybir.dt.bfloat16
AF = mybir.ActivationFunctionType

S = 2048
D = 1024
H = 8
DH = 64
HD = 512
E = 8
HID = 256
SC = S // 128
DC = D // 128
KC = 16
NPASS = 39
YTS = 2056

SEM_NAMES = [
    "ldg", "ldy", "lsy", "sto", "xrv",
    "t", "v", "s",
    "snkt", "snkv", "lgt", "lgs", "avt", "avs", "vt",
    "wet", "wef", "yet", "yes", "yev",
]


RSAFE = 0
TSAFE = 1


def build_program(upto=None, taps=()):
    nc = bass.Bass("TRN2", target_bir_lowering=False, debug=False)

    dr = {}
    def din(name, shape, dt=F32):
        dr[name] = nc.dram_tensor("i_" + name, shape, dt, kind="ExternalInput").ap()
    din("x", [S, D])
    din("wq", [D, HD]); din("wk", [D, HD]); din("wv", [D, HD])
    din("bqc", [128, 4]); din("bkc", [128, 4]); din("bvb", [128, HD])
    din("wo", [HD, D]); din("bob", [128, D]); din("bor", [1, D])
    din("dwc", [128, DC * 3]); din("dwbc", [128, DC])
    din("pwt", [D, D]); din("pwbc", [128, DC])
    din("rwt", [D, E]); din("rbb", [128, E])
    din("ewt", [E, D, D])
    din("ebr", [E, D])
    din("owt", [D, D]); din("obr", [1, D])
    din("nwc", [128, DC]); din("nwb", [128, D])
    din("fdt", [D, HID]); din("fdbr", [1, HID])
    din("fut", [HID, D]); din("fubr", [1, D])
    din("ident", [128, 128])
    din("ones1", [1, 128])
    din("esel", [8, E * 128])
    yspill = nc.dram_tensor("yspill", [S, D], F32, kind="Internal").ap()
    out_h = nc.dram_tensor("out", [S, D], F32, kind="ExternalOutput").ap()
    dbg_h = {}
    for (dn, sn, shp, dt) in taps:
        dbg_h[dn] = nc.dram_tensor(dn, shp, dt, kind="ExternalOutput").ap()

    ctx = {}
    RSAFE = globals()["RSAFE"]
    TSAFE = globals()["TSAFE"]

    def emit(me, eng):
        c = {n: 0 for n in SEM_NAMES}
        sems = ctx["sems"]
        sb = ctx

        def inc(name, instr, k=1):
            c[name] += k
            if instr is not None and instr is not False:
                instr.then_inc(sems[name], k)

        def wait(name, val):
            if eng is not None:
                eng.wait_ge(sems[name], val)

        T = me == "tensor"; V = me == "vector"; SS = me == "scalar"
        G = me == "gpsimd"; SY = me == "sync"

        def g_(cond, instr_fn):
            return instr_fn() if cond else None

        def vdrain():
            if V:
                eng.drain()

        def wload(dst_ap, src_dr, o):
            return eng.dma_start(
                out=dst_ap.rearrange("p (c o) -> p c o", o=o),
                in_=src_dr.rearrange("(c p) o -> p c o", p=128))

        class _Stop(Exception):
            pass

        def tapstop():
            if SY:
                for nm2 in ("t", "v", "s", "snkv", "lgs", "avs", "yev",
                            "yes", "vt"):
                    if c[nm2] > 0:
                        wait(nm2, c[nm2])
                ns = 0
                for (dn, sn, shp, dt) in taps:
                    ap = sb[sn] if isinstance(sn, str) else sn(sb)
                    ap = ap[:, :]
                    i2 = eng.dma_start(out=dbg_h[dn][:, :] if len(shp) == 2
                                       else dbg_h[dn][:], in_=ap)
                    i2.then_inc(sems["sto"], 16)
                    ns += 16
                wait("sto", c["sto"] + ns)
            raise _Stop()

        i = g_(G, lambda: wload(sb["xbf"], dr["x"], D))
        inc("ldg", i, 16)
        i = g_(G, lambda: eng.dma_start(out=sb["ident"][:, :], in_=dr["ident"][:, :]))
        inc("ldg", i, 16)
        i = g_(G, lambda: eng.dma_start(out=sb["ones1"][:, :], in_=dr["ones1"][:, :]))
        inc("ldg", i, 16)
        for wname, dst in (("wq", "wq_sb"), ("wk", "wk_sb"), ("wv", "wv_sb")):
            i = g_(G, lambda wname=wname, dst=dst: wload(sb[dst], dr[wname], HD))
            inc("ldg", i, 16)
        for nm in ("bqc", "bkc", "bvb"):
            i = g_(G, lambda nm=nm: eng.dma_start(out=sb[nm][:, :], in_=dr[nm][:, :]))
            inc("ldg", i, 16)
        LD_ATTN = c["ldg"]

        i = g_(G, lambda: eng.dma_start(out=sb["bor_sb"][:, :], in_=dr["bor"][:, :]))
        inc("ldg", i, 16)
        for nm in ("dwc", "dwbc", "pwbc", "rbb", "nwc"):
            i = g_(G, lambda nm=nm: eng.dma_start(out=sb[nm][:, :], in_=dr[nm][:, :]))
            inc("ldg", i, 16)
        i = g_(G, lambda: wload(sb["rwt_sb"], dr["rwt"], E))
        inc("ldg", i, 16)
        i = g_(G, lambda: eng.dma_start(out=sb["ebr2"][:, :], in_=dr["ebr"][:, :]))
        inc("ldg", i, 16)
        i = g_(G, lambda: eng.dma_start(out=sb["esel"][:, :], in_=dr["esel"][:, :]))
        inc("ldg", i, 16)
        for nm, dst in (("obr", "obr_sb"), ("fdbr", "fdbr_sb"), ("fubr", "fubr_sb")):
            i = g_(G, lambda nm=nm, dst=dst: eng.dma_start(
                out=sb[dst][:, :], in_=dr[nm][:, :]))
            inc("ldg", i, 16)
        LD_SM2 = c["ldg"]

        if T:
            wait("ldg", 48)
        v0 = c["v"]
        for sc in range(SC):
            for half in range(2):
                idx = 2 * sc + half
                ps = sb["CpB"][:, 1024 * half:1024 * half + 512]
                if T and idx >= 2:
                    wait("v", v0 + idx - 1)
                lastt = None
                for q in range(4):
                    dc = 4 * half + q
                    lastt = g_(T, lambda q=q, dc=dc, sc=sc, ps=ps: eng.transpose(
                        ps[:, 128 * q:128 * q + 128],
                        sb["xbf"][:, 1024 * sc + 128 * dc:1024 * sc + 128 * dc + 128],
                        sb["ident"][:, :]))
                inc("t", lastt)
                if V:
                    wait("t", c["t"])
                i = g_(V, lambda sc=sc, half=half, ps=ps: eng.tensor_copy(
                    sb["xT"].rearrange("p (dc s) -> p dc s", dc=DC)
                    [:, 4 * half:4 * half + 4, 128 * sc:128 * sc + 128],
                    ps.rearrange("p (q r) -> p q r", q=4)))
                inc("v", i)
        XT_V = c["v"]

        if T:
            wait("ldg", LD_ATTN)
            wait("v", XT_V)
        s0 = c["s"]
        for wi, (wsb, qdst, bcol) in enumerate(
                (("wq_sb", "qt", "bqc"), ("wk_sb", "kt", "bkc"))):
            for oc in range(4):
                idx = 4 * wi + oc
                if T and idx >= 1:
                    wait("s", s0 + idx)
                last = None
                for n4 in range(4):
                    for dc in range(DC):
                        last = g_(T, lambda n4=n4, dc=dc, oc=oc, wsb=wsb: eng.matmul(
                            sb["Lp"][:, 512 * n4:512 * n4 + 512],
                            sb[wsb][:, 512 * dc + 128 * oc:512 * dc + 128 * oc + 128],
                            sb["xT"][:, 2048 * dc + 512 * n4:2048 * dc + 512 * n4 + 512],
                            start=(dc == 0), stop=(dc == DC - 1)))
                inc("t", last)
                if SS:
                    wait("t", c["t"])
                i = g_(SS, lambda oc=oc, qdst=qdst, bcol=bcol: eng.activation(
                    sb[qdst][:, 2048 * oc:2048 * oc + 2048], sb["Lp"][:, :],
                    AF.Identity, bias=sb[bcol][:, oc:oc + 1], scale=1.0))
                inc("s", i)
        vv0 = c["v"]
        for sc in range(SC):
            ph = sc % 2
            ps = sb["Cp"][:, 512 * ph:512 * ph + 512]
            if T and sc >= 2:
                wait("v", vv0 + sc - 1)
            last = None
            for dc in range(DC):
                last = g_(T, lambda dc=dc, sc=sc, ps=ps: eng.matmul(
                    ps,
                    sb["xT"][:, 2048 * dc + 128 * sc:2048 * dc + 128 * sc + 128],
                    sb["wv_sb"][:, 512 * dc:512 * dc + 512],
                    start=(dc == 0), stop=(dc == DC - 1)))
            inc("t", last)
            if V:
                wait("t", c["t"])
            i = g_(V, lambda sc=sc, ps=ps: eng.tensor_tensor(
                sb["vna"][:, 512 * sc:512 * sc + 512], ps, sb["bvb"][:, :], OP.add))
            inc("v", i)
        PROJ_V = c["v"]; PROJ_S = c["s"]
        if upto == 1:
            tapstop()

        head_lgs = {}
        def snk_groups(parity):
            if parity == 0:
                return [sb["Cp"][:, 0:4], sb["Cp"][:, 512:516],
                        sb["S0"][:, 0:4], sb["S1"][:, 0:4]]
            return [sb["Lp"][:, 0:4], sb["Lp"][:, 512:516],
                    sb["Lp"][:, 1024:1028], sb["Lp"][:, 1536:1540]]

        def emit_logits2(h, transposed, av_h=None, p0_h=None):
            oc = h // 2
            p0 = 64 * (h % 2)
            src_l, src_r = ("kt", "qt") if transposed else ("qt", "kt")
            dstbuf = "Tbuf" if transposed else "Abuf"
            LT = c["lgt"]; LS = c["lgs"]
            if av_h is not None:
                AVB = c["avs"]
                hpav = av_h % 2
            if p0_h is not None:
                hp0 = p0_h % 2
                groups0 = snk_groups(0)
                vbase0 = c["snkv"]
                if T:
                    wait("avs", c["avs"])
            last0 = None
            for qc in range(KC):
                if av_h is not None:
                    phav = qc % 2
                    psav = sb["Cp"][:, 512 * phav:512 * phav + 64]
                    if T:
                        if qc == 0:
                            wait("vt", c["vt"])
                        if qc >= 2:
                            wait("avs", AVB + qc - 1)
                    lastav = None
                    for kc in range(KC):
                        lastav = g_(T, lambda kc=kc, qc=qc, psav=psav: eng.matmul(
                            psav,
                            sb["Tbuf"][:, 2048 * kc + 128 * qc:2048 * kc + 128 * qc + 128],
                            sb["vt"][:, 64 * kc:64 * kc + 64],
                            start=(kc == 0), stop=(kc == KC - 1)))
                    inc("avt", lastav)
                    if V:
                        wait("avt", c["avt"])
                    i = g_(V, lambda av_h=av_h, qc=qc, hpav=hpav, psav=psav:
                           eng.tensor_scalar(
                               sb["obuf"][:, 512 * qc + 64 * av_h:512 * qc + 64 * av_h + 64],
                               psav, sb["wcol_f"][hpav][:, qc:qc + 1], None, OP.mult))
                    inc("avs", i)
                if T and qc >= 1:
                    wait("lgs", LS + 2 * (qc - 1) + 1)
                lastmm = None
                for n4 in (0, 1):
                    lastmm = g_(T, lambda n4=n4, qc=qc, oc=oc, p0=p0,
                                src_l=src_l, src_r=src_r: eng.matmul(
                        sb["Lp"][:, 512 * n4:512 * n4 + 512],
                        sb[src_l][p0:p0 + 64,
                                  2048 * oc + 128 * qc:2048 * oc + 128 * qc + 128],
                        sb[src_r][p0:p0 + 64,
                                  2048 * oc + 512 * n4:2048 * oc + 512 * n4 + 512],
                        start=True, stop=True))
                inc("lgt", lastmm)
                if T and qc >= 1:
                    wait("lgs", LS + 2 * (qc - 1) + 2)
                for n4 in (2, 3):
                    lastmm = g_(T, lambda n4=n4, qc=qc, oc=oc, p0=p0,
                                src_l=src_l, src_r=src_r: eng.matmul(
                        sb["Lp"][:, 512 * n4:512 * n4 + 512],
                        sb[src_l][p0:p0 + 64,
                                  2048 * oc + 128 * qc:2048 * oc + 128 * qc + 128],
                        sb[src_r][p0:p0 + 64,
                                  2048 * oc + 512 * n4:2048 * oc + 512 * n4 + 512],
                        start=True, stop=True))
                inc("lgt", lastmm)
                if SS:
                    wait("lgt", LT + 2 * qc + 1)
                    if transposed and qc == 0 and c["avt"] > 0:
                        wait("avt", c["avt"])
                i = g_(SS, lambda h=h, qc=qc, transposed=transposed, dstbuf=dstbuf:
                       eng.activation(
                           sb[dstbuf][:, 2048 * qc:2048 * qc + 1024],
                           sb["Lp"][:, 0:1024], AF.Exp, scale=0.125,
                           accum_out=(None if transposed
                                      else sb["wdenA"][h % 2][:, qc:qc + 1])))
                inc("lgs", i)
                if SS:
                    wait("lgt", LT + 2 * qc + 2)
                i = g_(SS, lambda h=h, qc=qc, transposed=transposed, dstbuf=dstbuf:
                       eng.activation(
                           sb[dstbuf][:, 2048 * qc + 1024:2048 * qc + 2048],
                           sb["Lp"][:, 1024:2048], AF.Exp, scale=0.125,
                           accum_out=(None if transposed
                                      else sb["wdenB"][h % 2][:, qc:qc + 1])))
                inc("lgs", i)
                if p0_h is not None:
                    j = qc
                    gidx = j // 4
                    reg = groups0[gidx]
                    for cc in range(16):
                        if T and j == 0 and cc % 4 == 0:
                            wait("snkv", vbase0 - 4 + (cc // 4) + 1)
                        last0 = g_(T, lambda j=j, cc=cc, reg=reg, hp0=hp0: eng.matmul(
                            reg[:, j % 4:j % 4 + 1],
                            sb["Abuf"][:, 2048 * cc + 128 * j:2048 * cc + 128 * j + 128],
                            sb["wcol_b"][hp0][:, cc:cc + 1],
                            start=(j % 4 == 0 and cc == 0),
                            stop=(j % 4 == 3 and cc == 15),
                            skip_group_check=True))
                    if j % 4 == 3:
                        inc("snkt", last0)
                        if V:
                            wait("snkt", c["snkt"])
                        i = g_(V, lambda hp0=hp0, gidx=gidx, reg=reg: eng.reciprocal(
                            sb["ucol_f"][hp0][:, 4 * gidx:4 * gidx + 4], reg[:, 0:4]))
                        if gidx >= 1:
                            i2 = g_(V, lambda hp0=hp0, g2=gidx - 1: eng.tensor_copy(
                                sb["ucol_b"][hp0][:, 4 * g2:4 * g2 + 4],
                                sb["ucol_f"][hp0][:, 4 * g2:4 * g2 + 4]))
                            inc("snkv", i2)
            if p0_h is not None:
                i3 = g_(V, lambda hp0=hp0: eng.tensor_copy(
                    sb["ucol_b"][hp0][:, 12:16], sb["ucol_f"][hp0][:, 12:16]))
                inc("snkv", i3)

        def emit_w1(h):
            hp = h % 2
            base = c["lgs"]
            def _a(g):
                return g_(V, lambda: eng.tensor_tensor(
                    sb["wsum"][hp][:, 4 * g:4 * g + 4],
                    sb["wdenA"][hp][:, 4 * g:4 * g + 4],
                    sb["wdenB"][hp][:, 4 * g:4 * g + 4], OP.add))
            def _r(g):
                return g_(V, lambda: eng.reciprocal(
                    sb["wcol_f"][hp][:, 4 * g:4 * g + 4],
                    sb["wsum"][hp][:, 4 * g:4 * g + 4]))
            def _c(g):
                return g_(V, lambda: eng.tensor_copy(
                    sb["wcol_b"][hp][:, 4 * g:4 * g + 4],
                    sb["wcol_f"][hp][:, 4 * g:4 * g + 4]))
            if V:
                wait("lgs", base - 32 + 8)
            _a(0)
            if V:
                wait("lgs", base - 32 + 16)
            _a(1)
            _r(0)
            if V:
                wait("lgs", base - 32 + 24)
            _a(2)
            _r(1)
            inc("snkv", _c(0))
            if V:
                wait("lgs", base - 32 + 32)
            _a(3)
            _r(2)
            inc("snkv", _c(1))
            _r(3)
            inc("snkv", _c(2))
            inc("snkv", _c(3))

        def emit_pass(h, p):
            hp = h % 2
            col = (p % 2 == 0)
            mat = "Abuf" if col else "Tbuf"
            src_b = "wcol_b" if col else "ucol_b"
            dst_f = "ucol_f" if col else "wcol_f"
            dst_b = "ucol_b" if col else "wcol_b"
            groups = snk_groups(p % 2)
            vbase = c["snkv"]
            if T and p == 1:
                wait("lgs", head_lgs[h])
            last = None
            for j in range(16):
                gidx = j // 4
                reg = groups[gidx]
                for cc in range(16):
                    if T and j == 0 and cc % 4 == 0:
                        wait("snkv", vbase - 4 + (cc // 4) + 1)
                    last = g_(T, lambda j=j, cc=cc, reg=reg, mat=mat,
                              src_b=src_b, hp=hp, gidx=gidx: eng.matmul(
                        reg[:, j % 4:j % 4 + 1],
                        sb[mat][:, 2048 * cc + 128 * j:2048 * cc + 128 * j + 128],
                        sb[src_b][hp][:, cc:cc + 1],
                        start=(j % 4 == 0 and cc == 0),
                        stop=(j % 4 == 3 and cc == 15),
                        skip_group_check=True))
                if j % 4 == 3:
                    inc("snkt", last)
                    if V:
                        wait("snkt", c["snkt"])
                    i = g_(V, lambda hp=hp, gidx=gidx, reg=reg, dst_f=dst_f:
                           eng.reciprocal(
                               sb[dst_f][hp][:, 4 * gidx:4 * gidx + 4],
                               reg[:, 0:4]))
                    if gidx >= 1:
                        i2 = g_(V, lambda hp=hp, g2=gidx - 1, dst_f=dst_f,
                                dst_b=dst_b: eng.tensor_copy(
                                    sb[dst_b][hp][:, 4 * g2:4 * g2 + 4],
                                    sb[dst_f][hp][:, 4 * g2:4 * g2 + 4]))
                        inc("snkv", i2)
            i3 = g_(V, lambda hp=hp, dst_f=dst_f, dst_b=dst_b: eng.tensor_copy(
                sb[dst_b][hp][:, 12:16], sb[dst_f][hp][:, 12:16]))
            inc("snkv", i3)

        def emit_vtilde(h):
            hp = h % 2
            last = None
            for kc in range(KC):
                last = g_(V, lambda kc=kc, h=h, hp=hp: eng.tensor_scalar(
                    sb["vt"][:, 64 * kc:64 * kc + 64],
                    sb["vna"][:, 512 * kc + 64 * h:512 * kc + 64 * h + 64],
                    sb["ucol_f"][hp][:, kc:kc + 1], None, OP.mult))
            inc("vt", last)

        def emit_av(h):
            hp = h % 2
            av0 = c["avs"]
            for qc in range(KC):
                ph = qc % 2
                ps = sb["Cp"][:, 512 * ph:512 * ph + 64]
                if T:
                    if qc == 0:
                        wait("vt", c["vt"])
                    if qc >= 2:
                        wait("avs", av0 + qc - 1)
                last = None
                for kc in range(KC):
                    last = g_(T, lambda kc=kc, qc=qc, ps=ps: eng.matmul(
                        ps,
                        sb["Tbuf"][:, 2048 * kc + 128 * qc:2048 * kc + 128 * qc + 128],
                        sb["vt"][:, 64 * kc:64 * kc + 64],
                        start=(kc == 0), stop=(kc == KC - 1)))
                inc("avt", last)
                if V:
                    wait("avt", c["avt"])
                i = g_(V, lambda h=h, qc=qc, hp=hp, ps=ps: eng.tensor_scalar(
                    sb["obuf"][:, 512 * qc + 64 * h:512 * qc + 64 * h + 64],
                    ps, sb["wcol_f"][hp][:, qc:qc + 1], None, OP.mult))
                inc("avs", i)

        if T:
            wait("s", PROJ_S)
            wait("v", PROJ_V)
        emit_logits2(0, False)
        emit_w1(0)
        emit_logits2(0, True, p0_h=0)
        head_lgs[0] = c["lgs"]
        for h in range(H):
            for p in range(1, NPASS):
                emit_pass(h, p)
            if upto == 21 and h == 0:
                tapstop()
            if upto == 210 + h:
                tapstop()
            emit_vtilde(h)
            if h + 1 >= H:
                emit_av(h)
            if upto == 2 and h == 0:
                emit_av(h)
                tapstop()
            if h + 1 < H:
                if TSAFE & 1:
                    emit_av(h)
                if SS:
                    wait("snkt", c["snkt"])
                emit_logits2(h + 1, False,
                             av_h=(None if (TSAFE & 1) else h))
                emit_w1(h + 1)
                emit_logits2(h + 1, True,
                             p0_h=(None if (TSAFE & 2) else h + 1))
                head_lgs[h + 1] = c["lgs"]
                if TSAFE & 2:
                    emit_pass(h + 1, 0)
                if upto == 220 + h:
                    tapstop()

        if T:
            wait("avs", c["avs"])
            wait("ldg", LD_ATTN)
        v3 = c["v"]
        idx3 = 0
        for gq in (2, 3, 1, 0):
            for hc in range(4):
                ph = idx3 % 2
                ps = sb["CpB"][:, 1024 * ph:1024 * ph + 512]
                if T and idx3 >= 2:
                    wait("v", v3 + idx3 - 1)
                lastt = None
                for q in range(4):
                    sc = 4 * gq + q
                    lastt = g_(T, lambda sc=sc, hc=hc, q=q, ps=ps: eng.transpose(
                        ps[:, 128 * q:128 * q + 128],
                        sb["obuf"][:, 512 * sc + 128 * hc:512 * sc + 128 * hc + 128],
                        sb["ident"][:, :]))
                inc("t", lastt)
                if V:
                    wait("t", c["t"])
                i = g_(V, lambda hc=hc, gq=gq, ps=ps: eng.tensor_copy(
                    sb["ot"][:, 2048 * hc + 512 * gq:2048 * hc + 512 * gq + 512], ps))
                inc("v", i)
                idx3 += 1
        OT_V = c["v"]

        for wc in range(4):
            if G and wc == 0:
                wait("v", v3 + 4)
            if G and wc == 2:
                wait("v", v3 + 8)
            i = g_(G, lambda wc=wc: eng.dma_start(
                out=sb["wo_sb"][:, 1024 * wc:1024 * wc + 1024],
                in_=dr["wo"][128 * wc:128 * wc + 128, :]))
            inc("ldg", i, 16)
        LD_WO = c["ldg"]
        if G:
            wait("v", v3 + 12)
        i = g_(G, lambda: wload(sb["fut_sb"], dr["fut"], D))
        inc("ldg", i, 16)
        if G:
            wait("v", OT_V)
        i = g_(G, lambda: wload(sb["fdt_sb"], dr["fdt"], HID))
        inc("ldg", i, 16)
        LD_FDT = c["ldg"]
        if G:
            wait("avt", 16 * H)
        i = g_(G, lambda: wload(sb["wet1"], dr["ewt"][0], D))
        inc("wet", i, 16)
        if upto == 29:
            tapstop()

        lsy3 = c["lsy"]
        s_ybf = {}; s_evac = {}; s_sqrt = {}
        v_stt = {}; v_eps = {}; v_mb = {}; t_wo = {}
        v_conv = {}; v_x1t = {}

        def xct_sl(dc, lo, ln):
            buf = sb["xct_a"] if dc < 4 else sb["xct_b"]
            base = 2048 * (dc % 4)
            return buf[:, base + lo:base + lo + ln]

        if V:
            wait("avt", 16 * H)
        i = g_(V, lambda: eng.memset(
            sb["yt"].rearrange("p (dc s) -> p dc s", s=YTS)[:, :, 0:2], 0.0))
        inc("v", i)
        vdrain()

        def ph3_stageA(sc):
            ph = sc % 2
            psW = sb["Lp"][:, 1024 * ph:1024 * ph + 1024]
            t1 = sb["t1f"][sc % 4]
            if SY and sc == 0:
                wait("snkt", c["snkt"])
            if SY and sc >= 4:
                wait("s", s_ybf[sc - 4])
            i = g_(SY, lambda sc=sc, t1=t1: eng.dma_start(
                out=t1[:, :], in_=dr["x"][128 * sc:128 * sc + 128, :]))
            inc("lsy", i, 16)
            if T:
                if sc == 0:
                    wait("v", OT_V)
                    wait("ldg", LD_WO)
                if sc >= 2:
                    wait("v", v_stt[sc - 2])
            last = None
            for n2 in range(2):
                last = g_(T, lambda n2=n2, psW=psW: eng.matmul(
                    psW[:, 512 * n2:512 * n2 + 512], sb["ones1"][:, :],
                    sb["bor_sb"][0:1, 512 * n2:512 * n2 + 512],
                    start=True, stop=False, skip_group_check=True))
                for hcc in range(4):
                    last = g_(T, lambda n2=n2, hcc=hcc, sc=sc, psW=psW: eng.matmul(
                        psW[:, 512 * n2:512 * n2 + 512],
                        sb["ot"][:, 2048 * hcc + 128 * sc:2048 * hcc + 128 * sc + 128],
                        sb["wo_sb"][:, 1024 * hcc + 512 * n2:1024 * hcc + 512 * n2 + 512],
                        start=False, stop=(hcc == 3), skip_group_check=True))
            inc("t", last)
            t_wo[sc] = c["t"]
            rcb = 4 * ph
            if V:
                wait("t", t_wo[sc])
                wait("lsy", lsy3 + (16 if sc == 0 else 32 * sc))
            i = g_(V, lambda t1=t1, psW=psW: eng.scalar_tensor_tensor(
                t1[:, :], psW, 1.0, t1[:, :], OP.mult, OP.add))
            inc("v", i)
            v_stt[sc] = c["v"]
            vdrain()
            i = g_(V, lambda t1=t1, ph=ph: eng.bn_stats(
                sb["bns"][:, 12 * ph:12 * ph + 6], t1[:, 0:512]))
            i = g_(V, lambda t1=t1, ph=ph: eng.bn_stats(
                sb["bns"][:, 12 * ph + 6:12 * ph + 12], t1[:, 512:1024]))
            vdrain()
            i = g_(V, lambda ph=ph: eng.bn_aggr(
                sb["bnagg"][:, 2 * ph:2 * ph + 2],
                sb["bns"][:, 12 * ph:12 * ph + 12]))
            vdrain()
            i = g_(V, lambda rcb=rcb, ph=ph: eng.tensor_scalar(
                sb["rc"][:, rcb:rcb + 1], sb["bnagg"][:, 2 * ph + 1:2 * ph + 2],
                1.0, 1e-5, OP.mult, OP.add))
            inc("v", i)
            v_eps[sc] = c["v"]

        def ph3_stageB(sc):
            rcb = 4 * (sc % 2)
            if SS:
                wait("v", v_eps[sc])
            i = g_(SS, lambda rcb=rcb: eng.activation(
                sb["rc"][:, rcb + 1:rcb + 2], sb["rc"][:, rcb:rcb + 1], AF.Sqrt))
            inc("s", i)
            s_sqrt[sc] = c["s"]

        def ph3_stageC(sc):
            ph = sc % 2
            rcb = 4 * ph
            if V:
                wait("s", s_sqrt[sc])
            i = g_(V, lambda rcb=rcb: eng.reciprocal(
                sb["rc"][:, rcb + 2:rcb + 3], sb["rc"][:, rcb + 1:rcb + 2]))
            vdrain()
            i = g_(V, lambda rcb=rcb, ph=ph: eng.tensor_scalar(
                sb["rc"][:, rcb + 3:rcb + 4], sb["bnagg"][:, 2 * ph:2 * ph + 1],
                sb["rc"][:, rcb + 2:rcb + 3], -1.0, OP.mult, OP.mult))
            inc("v", i)
            v_mb[sc] = c["v"]

        def ph3_stageD(sc):
            ph = sc % 2
            rcb = 4 * ph
            t1 = sb["t1f"][sc % 4]
            if SS:
                wait("v", v_mb[sc])
                if sc >= 2:
                    wait("lsy", lsy3 + 32 * (sc - 2) + 48)
            i = g_(SS, lambda ph=ph, t1=t1, rcb=rcb: eng.activation(
                sb["yf32"][ph][:, :], t1[:, :], AF.Identity,
                scale=sb["rc"][:, rcb + 2:rcb + 3], bias=sb["rc"][:, rcb + 3:rcb + 4]))
            inc("s", i)
            i = g_(SS, lambda sc=sc, t1=t1, rcb=rcb: eng.activation(
                sb["ybf"][sc % 4][:, :], t1[:, :], AF.Identity,
                scale=sb["rc"][:, rcb + 2:rcb + 3], bias=sb["rc"][:, rcb + 3:rcb + 4]))
            inc("s", i)
            s_ybf[sc] = c["s"]
            if SY:
                wait("s", s_ybf[sc])
            i = g_(SY, lambda sc=sc, ph=ph: eng.dma_start(
                out=yspill[128 * sc:128 * sc + 128, :], in_=sb["yf32"][ph][:, :]))
            inc("lsy", i, 16)

        def ph3_back(sc):
            yb = sb["ybf"][sc % 4]
            for half in range(2):
                ps = sb["CpB"][:, 1024 * half:1024 * half + 512]
                if T:
                    wait("s", max(s_ybf[sc], s_evac.get((sc - 1, half), 0)))
                lastt = None
                for q in range(4):
                    dc = 4 * half + q
                    lastt = g_(T, lambda yb=yb, dc=dc, q=q, ps=ps: eng.transpose(
                        ps[:, 128 * q:128 * q + 128],
                        yb[:, 128 * dc:128 * dc + 128],
                        sb["ident"][:, :]))
                inc("t", lastt)
                if SS:
                    wait("t", c["t"])
                i = g_(SS, lambda sc=sc, half=half, ps=ps: eng.activation(
                    sb["yt"].rearrange("p (dc s) -> p dc s", s=YTS)
                    [:, 4 * half:4 * half + 4, 2 + 128 * sc:2 + 128 * sc + 128],
                    ps.rearrange("p (q r) -> p q r", q=4), AF.Copy))
                inc("s", i)
                s_evac[(sc, half)] = c["s"]

        def conv0_part(dc):
            ca = sb["cacc0"]
            y0 = YTS * dc
            if V and dc == 0:
                wait("ldg", LD_SM2)
                wait("s", s_evac[(3, 1)])
            vdrain()
            i = g_(V, lambda ca=ca, y0=y0, dc=dc: eng.tensor_scalar(
                ca[:, :], sb["yt"][:, y0:y0 + 512],
                sb["dwc"][:, 3 * dc:3 * dc + 1], sb["dwbc"][:, dc:dc + 1],
                OP.mult, OP.add))
            vdrain()
            i = g_(V, lambda ca=ca, y0=y0, dc=dc: eng.scalar_tensor_tensor(
                ca[:, :], sb["yt"][:, y0 + 1:y0 + 1 + 512],
                sb["dwc"][:, 3 * dc + 1:3 * dc + 2], ca[:, :], OP.mult, OP.add))
            vdrain()
            i = g_(V, lambda ca=ca, y0=y0, dc=dc: eng.scalar_tensor_tensor(
                xct_sl(dc, 0, 512),
                sb["yt"][:, y0 + 2:y0 + 2 + 512],
                sb["dwc"][:, 3 * dc + 2:3 * dc + 3], ca[:, :], OP.mult, OP.add))
            if dc == DC - 1:
                inc("v", i)
                v_conv[0] = c["v"]

        for sc in range(SC):
            ph3_stageA(sc)
            ph3_stageB(sc)
            if sc >= 1:
                ph3_stageC(sc - 1)
                ph3_stageD(sc - 1)
            if sc >= 8:
                conv0_part(sc - 8)
            if sc >= 3:
                ph3_back(sc - 3)
            if upto == 31 and sc == 3:
                tapstop()
        ph3_stageC(SC - 1)
        ph3_stageD(SC - 1)
        ph3_back(SC - 3)
        ph3_back(SC - 2)
        ph3_back(SC - 1)
        LSY3_END = c["lsy"]
        PH3_V = c["v"]
        if upto == 3:
            tapstop()

        LD_PWT0 = c["ldg"]
        pwt_pos = {}
        for pi, wc in enumerate((0, 2, 4, 6, 1, 3, 5, 7)):
            if G and pi == 0:
                wait("t", t_wo[7])
            if G and pi == 4:
                wait("t", t_wo[SC - 1])
            i = g_(G, lambda wc=wc: eng.dma_start(
                out=sb["pwt_sb"][:, 1024 * wc:1024 * wc + 1024],
                in_=dr["pwt"][128 * wc:128 * wc + 128, :]))
            inc("ldg", i, 16)
            pwt_pos[wc] = c["ldg"]
        LD_PWT = c["ldg"]

        cacc = sb["trash"].bitcast(F32)

        def conv_group(n4):
            lo = 512 * n4
            if V:
                wait("s", s_evac[(4 * n4 + 3, 1)])
            i = None
            for dc in range(DC):
                y0 = YTS * dc + lo
                vdrain()
                i = g_(V, lambda y0=y0, dc=dc: eng.tensor_scalar(
                    cacc[:, :], sb["yt"][:, y0:y0 + 512],
                    sb["dwc"][:, 3 * dc:3 * dc + 1], sb["dwbc"][:, dc:dc + 1],
                    OP.mult, OP.add))
                vdrain()
                i = g_(V, lambda y0=y0, dc=dc: eng.scalar_tensor_tensor(
                    cacc[:, :], sb["yt"][:, y0 + 1:y0 + 1 + 512],
                    sb["dwc"][:, 3 * dc + 1:3 * dc + 2], cacc[:, :],
                    OP.mult, OP.add))
                vdrain()
                i = g_(V, lambda y0=y0, dc=dc, lo=lo: eng.scalar_tensor_tensor(
                    xct_sl(dc, lo, 512),
                    sb["yt"][:, y0 + 2:y0 + 2 + 512],
                    sb["dwc"][:, 3 * dc + 2:3 * dc + 3], cacc[:, :],
                    OP.mult, OP.add))
            inc("v", i)
            v_conv[n4] = c["v"]

        if upto == 41:
            conv_group(1); conv_group(2); conv_group(3)
            tapstop()

        s4 = c["s"]
        for n4 in range(4):
            if n4 + 1 < 4:
                conv_group(n4 + 1)
            for ec in range(DC):
                idx = 8 * n4 + ec
                ph = idx % 2
                ps = sb["Lp"][:, 512 * ph:512 * ph + 512]
                if T:
                    if ec == 0:
                        wait("v", v_conv[n4])
                    if idx >= 2:
                        wait("s", s4 + idx - 1)
                last = None
                for dc in range(DC):
                    if T and n4 == 0:
                        wait("ldg", pwt_pos[dc])
                    last = g_(T, lambda dc=dc, ec=ec, n4=n4, ps=ps: eng.matmul(
                        ps,
                        sb["pwt_sb"][:, 1024 * dc + 128 * ec:1024 * dc + 128 * ec + 128],
                        xct_sl(dc, 512 * n4, 512),
                        start=(dc == 0), stop=(dc == DC - 1)))
                inc("t", last)
                if SS:
                    wait("t", c["t"])
                i = g_(SS, lambda ec=ec, n4=n4, ps=ps: eng.activation(
                    sb["xc2t"][:, 2048 * ec + 512 * n4:2048 * ec + 512 * n4 + 512],
                    ps, AF.Silu, bias=sb["pwbc"][:, ec:ec + 1]))
                inc("s", i)
                if V:
                    wait("s", c["s"])
                i = g_(V, lambda ec=ec, n4=n4: eng.tensor_tensor(
                    sb["x1t"][:, 2048 * ec + 512 * n4:2048 * ec + 512 * n4 + 512],
                    sb["xc2t"][:, 2048 * ec + 512 * n4:2048 * ec + 512 * n4 + 512],
                    sb["yt"][:, YTS * ec + 2 + 512 * n4:YTS * ec + 2 + 512 * n4 + 512],
                    OP.add))
                inc("v", i)
                if ec == DC - 1:
                    v_x1t[n4] = c["v"]
        PW_V = c["v"]
        PW_S = c["s"]
        t_pwend = c["t"]
        if upto == 42:
            tapstop()

        if G:
            wait("t", t_pwend)
        i = g_(G, lambda: wload(sb["wet0"], dr["ewt"][1], D))
        inc("wet", i, 16)

        if V:
            wait("t", t_pwend)
        for pc in range(4):
            i = g_(V, lambda pc=pc: eng.memset(sb["combP"][pc][:, :], 0.0))
            if pc == 3:
                inc("v", i)
        CB0_V = c["v"]

        rtr_inline = True
        T5R = c["t"]

        def comb_sl(sc):
            return sb["combP"][sc // 4][:, 1024 * (sc % 4):1024 * (sc % 4) + 1024]

        def router_mm(sc):
            ph = sc % 2
            ps = (sb["S0"] if ph == 0 else sb["S1"])[:, 0:8]
            if T:
                if sc == 0:
                    wait("v", PW_V)
                if sc >= 2:
                    wait("v", CB0_V + 3 * (sc - 2) + 1)
            last = None
            for dc in range(DC):
                last = g_(T, lambda dc=dc, sc=sc, ps=ps: eng.matmul(
                    ps,
                    sb["x1t"][:, 2048 * dc + 128 * sc:2048 * dc + 128 * sc + 128],
                    sb["rwt_sb"][:, 8 * dc:8 * dc + 8],
                    start=(dc == 0), stop=(dc == DC - 1)))
            inc("t", last)

        s4r = c["s"]
        for sc in range(SC):
            ph = sc % 2
            ps = (sb["S0"] if ph == 0 else sb["S1"])[:, 0:8]
            if rtr_inline:
                router_mm(sc)
            if V:
                wait("t", T5R + sc + 1)
            i = g_(V, lambda ps=ps: eng.tensor_tensor(
                sb["rlg"][:, :], ps, sb["rbb"][:, :], OP.add))
            inc("v", i)
            vdrain()
            if RSAFE & 1:
                i = g_(V, lambda: eng.tensor_copy(sb["rmx"][:, :], sb["rlg"][:, :]))
            else:
                i = g_(V, lambda: eng.max(sb["rmx"][:, :], sb["rlg"][:, :]))
            vdrain()
            i = g_(V, lambda: eng.tensor_tensor(
                sb["rc"][:, 8:9], sb["rmx"][:, 0:1], sb["rmx"][:, 1:2], OP.subtract))
            inc("v", i)
            if SS:
                wait("v", c["v"])
            i = g_(SS, lambda: eng.activation(sb["rc"][:, 9:10], sb["rc"][:, 8:9],
                                              AF.Identity if (RSAFE & 4) else AF.Sigmoid))
            inc("s", i)
            if V:
                wait("s", c["s"])
            i = g_(V, lambda: eng.tensor_scalar(
                sb["rc"][:, 10:11], sb["rc"][:, 9:10], -1.0, 1.0, OP.mult, OP.add))
            if RSAFE & 2:
                i = g_(V, lambda: eng.tensor_copy(sb["ris"][:, 0:8], sb["rlg"][:, :]))
                i = g_(V, lambda: eng.tensor_copy(sb["ris"][:, 8:16], sb["rlg"][:, :]))
            else:
                i = g_(V, lambda: eng.tensor_scalar(
                    sb["ris"][:, 0:8], sb["rlg"][:, :], sb["rmx"][:, 0:1],
                    sb["rc"][:, 9:10], OP.is_equal, OP.mult))
                i = g_(V, lambda: eng.tensor_scalar(
                    sb["ris"][:, 8:16], sb["rlg"][:, :], sb["rmx"][:, 1:2],
                    sb["rc"][:, 10:11], OP.is_equal, OP.mult))
            vdrain()
            i = g_(V, lambda sc=sc: eng.tensor_tensor(
                sb["rw"][:, 8 * sc:8 * sc + 8], sb["ris"][:, 0:8],
                sb["ris"][:, 8:16], OP.add))
            inc("v", i)
            vdrain()
        if upto == 4:
            tapstop()

        for e in range(E):
            for sc in range(SC):
                it = e * SC + sc
                ph = it % 2
                ps = sb["Lp"][:, 1024 * ph:1024 * ph + 1024]
                if T:
                    if it == 0:
                        wait("s", PW_S)
                        wait("ldg", LD_SM2)
                    if e == 0 and sc % 4 == 0:
                        wait("v", v_x1t[sc // 4])
                    if sc == 0:
                        wait("wet", 16 * (e + 1))
                    if it >= 2:
                        wait("yes", it - 1)
                last = None
                for n2 in range(2):
                    last = g_(T, lambda e=e, n2=n2, ps=ps: eng.matmul(
                        ps[:, 512 * n2:512 * n2 + 512],
                        sb["esel"][:, 128 * e:128 * e + 128],
                        sb["ebr2"][:, 512 * n2:512 * n2 + 512],
                        start=True, stop=False, skip_group_check=True))
                    for dc in range(DC):
                        last = g_(T, lambda e=e, dc=dc, sc=sc, n2=n2, ps=ps: eng.matmul(
                            ps[:, 512 * n2:512 * n2 + 512],
                            sb["x1t"][:, 2048 * dc + 128 * sc:2048 * dc + 128 * sc + 128],
                            sb["wet"][e % 2][:, 1024 * dc + 512 * n2:
                                             1024 * dc + 512 * n2 + 512],
                            start=False, stop=(dc == DC - 1), skip_group_check=True))
                inc("yet", last)
                if e == 0 and not rtr_inline:
                    router_mm(sc)
                if SS:
                    wait("yet", c["yet"])
                    if it >= 2:
                        wait("yev", it - 1)
                i = g_(SS, lambda ph=ph, ps=ps: eng.activation(
                    sb["zbuf"][ph][:, :], ps, AF.Silu))
                inc("yes", i)
                if True:
                    if V:
                        wait("yes", c["yes"])
                    i = g_(V, lambda e=e, sc=sc, ph=ph: eng.scalar_tensor_tensor(
                        comb_sl(sc),
                        sb["zbuf"][ph][:, :], sb["rw"][:, 8 * sc + e:8 * sc + e + 1],
                        comb_sl(sc), OP.mult, OP.add))
                    inc("yev", i)
            if e + 2 < E:
                if G:
                    wait("yet", 16 * (e + 1))
                i = g_(G, lambda e=e: wload(sb["wet"][e % 2], dr["ewt"][e + 2], D))
                inc("wet", i, 16)
        EXP_V = c["yev"]

        if upto == 5:
            tapstop()
        LD_OWT0 = c["ldg"]
        for wc in range(DC):
            if G and wc == 0:
                wait("yet", 16 * E)
            i = g_(G, lambda wc=wc: eng.dma_start(
                out=sb["owt_sb"][:, 1024 * wc:1024 * wc + 1024],
                in_=dr["owt"][128 * wc:128 * wc + 128, :]))
            inc("ldg", i, 16)
        LD_OWT = c["ldg"]

        t6 = c["t"]
        for sc in range(SC):
            if V:
                if sc == 0:
                    wait("yev", EXP_V)
                if sc >= 2:
                    wait("t", t6 + 2 * (sc - 2) + 2)
            i = g_(V, lambda sc=sc: eng.tensor_copy(
                sb["cbf"][sc % 2][:, :], comb_sl(sc)))
            inc("v", i)
            for half in range(2):
                ps = sb["CpB"][:, 1024 * half:1024 * half + 512]
                if T:
                    wait("v", c["v"])
                lastt = None
                for q in range(4):
                    dc = 4 * half + q
                    lastt = g_(T, lambda sc=sc, dc=dc, q=q, ps=ps: eng.transpose(
                        ps[:, 128 * q:128 * q + 128],
                        sb["cbf"][sc % 2][:, 128 * dc:128 * dc + 128],
                        sb["ident"][:, :]))
                inc("t", lastt)
                if V:
                    wait("t", c["t"])
                i = g_(V, lambda sc=sc, half=half, ps=ps: eng.tensor_copy(
                    sb["combT"].rearrange("p (dc s) -> p dc s", dc=DC)
                    [:, 4 * half:4 * half + 4, 128 * sc:128 * sc + 128],
                    ps.rearrange("p (q r) -> p q r", q=4)))
                inc("v", i)
        CBT_V = c["v"]

        s6 = c["s"]
        v6 = c["v"]
        for sc in range(SC):
            ph = sc % 2
            psM = sb["Lp"][:, 1024 * ph:1024 * ph + 1024]
            hp = sb["hpf"][ph]
            rcb = 16 + 4 * ph
            if G:
                if sc == 0:
                    wait("lsy", LSY3_END)
                if sc < 2:
                    wait("v", CBT_V)
                else:
                    wait("s", s6 + 3 * (sc - 2) + 3)
            i = g_(G, lambda hp=hp, sc=sc: eng.dma_start(
                out=hp[:, :], in_=yspill[128 * sc:128 * sc + 128, :]))
            inc("ldy", i, 16)
            if T:
                if sc == 0:
                    wait("v", CBT_V)
                if sc >= 2:
                    wait("v", v6 + 3 * (sc - 2) + 1)
            last = None
            for n2 in range(2):
                last = g_(T, lambda n2=n2, psM=psM: eng.matmul(
                    psM[:, 512 * n2:512 * n2 + 512], sb["ones1"][:, :],
                    sb["obr_sb"][0:1, 512 * n2:512 * n2 + 512],
                    start=True, stop=False, skip_group_check=True))
                for dc in range(DC):
                    if T and sc == 0 and n2 == 0:
                        wait("ldg", LD_OWT0 + 16 * (dc + 1))
                    last = g_(T, lambda dc=dc, sc=sc, n2=n2, psM=psM: eng.matmul(
                        psM[:, 512 * n2:512 * n2 + 512],
                        sb["combT"][:, 2048 * dc + 128 * sc:2048 * dc + 128 * sc + 128],
                        sb["owt_sb"][:, 1024 * dc + 512 * n2:1024 * dc + 512 * n2 + 512],
                        start=False, stop=(dc == DC - 1), skip_group_check=True))
            for dc in range(DC):
                last = g_(T, lambda dc=dc, sc=sc, psM=psM: eng.matmul(
                    psM[:, 128 * dc:128 * dc + 128],
                    sb["xc2t"][:, 2048 * dc + 128 * sc:2048 * dc + 128 * sc + 128],
                    sb["ident"][:, :],
                    start=False, stop=(dc == DC - 1), skip_group_check=True))
            inc("t", last)
            if V:
                wait("t", c["t"])
                wait("ldy", c["ldy"])
            i = g_(V, lambda hp=hp, psM=psM: eng.scalar_tensor_tensor(
                hp[:, :], psM, 1.0, hp[:, :], OP.mult, OP.add))
            inc("v", i)
            if SS:
                wait("v", c["v"])
            i = g_(SS, lambda hp=hp, rcb=rcb: eng.activation(
                sb["trash"][:, :], hp[:, :], AF.Square,
                accum_out=sb["rc"][:, rcb:rcb + 1]))
            inc("s", i)
            if V:
                wait("s", c["s"])
            i = g_(V, lambda rcb=rcb: eng.tensor_scalar(
                sb["rc"][:, rcb + 1:rcb + 2], sb["rc"][:, rcb:rcb + 1],
                1.0 / D, 1e-6, OP.mult, OP.add))
            inc("v", i)
            if SS:
                wait("v", c["v"])
            i = g_(SS, lambda rcb=rcb: eng.activation(
                sb["rc"][:, rcb + 2:rcb + 3], sb["rc"][:, rcb + 1:rcb + 2], AF.Sqrt))
            inc("s", i)
            if V:
                wait("s", c["s"])
            i = g_(V, lambda rcb=rcb: eng.reciprocal(
                sb["rc"][:, rcb + 3:rcb + 4], sb["rc"][:, rcb + 2:rcb + 3]))
            inc("v", i)
            if SS:
                wait("v", c["v"])
            i = g_(SS, lambda sc=sc, hp=hp, rcb=rcb: eng.activation(
                sb["t2bf"][:, 1024 * sc:1024 * sc + 1024], hp[:, :],
                AF.Identity, scale=sb["rc"][:, rcb + 3:rcb + 4]))
            inc("s", i)
        RMS_S = c["s"]

        v6t = c["v"]
        for sc in range(SC):
            for half in range(2):
                idx = 2 * sc + half
                ps = sb["CpB"][:, 1024 * half:1024 * half + 512]
                if T:
                    if idx == 0:
                        wait("s", RMS_S)
                    if idx >= 2:
                        wait("v", v6t + idx - 1)
                lastt = None
                for q in range(4):
                    dc = 4 * half + q
                    lastt = g_(T, lambda sc=sc, dc=dc, q=q, ps=ps: eng.transpose(
                        ps[:, 128 * q:128 * q + 128],
                        sb["t2bf"][:, 1024 * sc + 128 * dc:1024 * sc + 128 * dc + 128],
                        sb["ident"][:, :]))
                inc("t", lastt)
                if V:
                    wait("t", c["t"])
                i = g_(V, lambda sc=sc, half=half, ps=ps: eng.tensor_copy(
                    sb["t2t"].rearrange("p (dc s) -> p dc s", dc=DC)
                    [:, 4 * half:4 * half + 4, 128 * sc:128 * sc + 128],
                    ps.rearrange("p (q r) -> p q r", q=4)))
                inc("v", i)
        if V:
            wait("ldg", LD_FDT)
        last = None
        for dc in range(DC):
            last = g_(V, lambda dc=dc: eng.tensor_scalar(
                sb["fdt_sb"][:, 256 * dc:256 * dc + 256],
                sb["fdt_sb"][:, 256 * dc:256 * dc + 256],
                sb["nwc"][:, dc:dc + 1], None, OP.mult))
        inc("v", last)
        FD_V = c["v"]

        if upto == 6:
            tapstop()
        s7 = c["s"]
        for sc in range(SC):
            ph = sc % 2
            psF = (sb["S0"] if ph == 0 else sb["S1"])[:, 0:256]
            if T:
                if sc == 0:
                    wait("v", FD_V)
                if sc >= 2:
                    wait("s", s7 + sc - 1)
            last = g_(T, lambda psF=psF: eng.matmul(
                psF, sb["ones1"][:, :], sb["fdbr_sb"][0:1, :],
                start=True, stop=False, skip_group_check=True))
            for dc in range(DC):
                last = g_(T, lambda dc=dc, sc=sc, psF=psF: eng.matmul(
                    psF,
                    sb["t2t"][:, 2048 * dc + 128 * sc:2048 * dc + 128 * sc + 128],
                    sb["fdt_sb"][:, 256 * dc:256 * dc + 256],
                    start=False, stop=(dc == DC - 1), skip_group_check=True))
            inc("t", last)
            if SS:
                wait("t", c["t"])
            i = g_(SS, lambda sc=sc, psF=psF: eng.activation(
                sb["ff1"][:, 256 * sc:256 * sc + 256], psF, AF.Gelu))
            inc("s", i)
        FF1_S = c["s"]

        v7 = c["v"]
        for sc in range(SC):
            ph = sc % 2
            ps = sb["CpB"][:, 1024 * ph:1024 * ph + 256]
            if T:
                if sc == 0:
                    wait("s", FF1_S)
                if sc >= 2:
                    wait("v", v7 + sc - 1)
            lastt = None
            for q in range(2):
                lastt = g_(T, lambda sc=sc, q=q, ps=ps: eng.transpose(
                    ps[:, 128 * q:128 * q + 128],
                    sb["ff1"][:, 256 * sc + 128 * q:256 * sc + 128 * q + 128],
                    sb["ident"][:, :]))
            inc("t", lastt)
            if V:
                wait("t", c["t"])
            i = g_(V, lambda sc=sc, ps=ps: eng.tensor_copy(
                sb["ff1t"].rearrange("p (hc s) -> p hc s", hc=2)
                [:, :, 128 * sc:128 * sc + 128],
                ps.rearrange("p (q r) -> p q r", q=2)))
            inc("v", i)
        FF1T_V = c["v"]

        if V:
            wait("t", c["t"])
        lastd = None
        for dc in range(DC):
            lastd = g_(V, lambda dc=dc: eng.tensor_scalar(
                sb["nwdiag"][:, 128 * dc:128 * dc + 128], sb["ident"][:, :],
                sb["nwc"][:, dc:dc + 1], None, OP.mult))
        inc("v", lastd)
        NWD_V = c["v"]

        s7b = c["s"]
        v7b = c["v"]
        for sc in range(SC):
            ph = sc % 2
            psZ = sb["Lp"][:, 1024 * ph:1024 * ph + 1024]
            zf = sb["hpf"][ph]
            rcb = 24 + 4 * ph
            if G:
                if sc < 2:
                    wait("s", RMS_S)
                else:
                    wait("s", s7b + 2 * (sc - 2) + 2)
            i = g_(G, lambda zf=zf, sc=sc: eng.dma_start(
                out=zf[:, :], in_=yspill[128 * sc:128 * sc + 128, :]))
            inc("ldy", i, 16)
            if T:
                if sc == 0:
                    wait("v", NWD_V)
                if sc >= 2:
                    wait("v", v7b + 3 * (sc - 2) + 1)
            last = None
            for n2 in range(2):
                last = g_(T, lambda n2=n2, psZ=psZ: eng.matmul(
                    psZ[:, 512 * n2:512 * n2 + 512], sb["ones1"][:, :],
                    sb["fubr_sb"][0:1, 512 * n2:512 * n2 + 512],
                    start=True, stop=False, skip_group_check=True))
                for hcc in range(2):
                    last = g_(T, lambda hcc=hcc, sc=sc, n2=n2, psZ=psZ: eng.matmul(
                        psZ[:, 512 * n2:512 * n2 + 512],
                        sb["ff1t"][:, 2048 * hcc + 128 * sc:2048 * hcc + 128 * sc + 128],
                        sb["fut_sb"][:, 1024 * hcc + 512 * n2:1024 * hcc + 512 * n2 + 512],
                        start=False, stop=(hcc == 1), skip_group_check=True))
            for dc in range(DC):
                last = g_(T, lambda dc=dc, sc=sc, psZ=psZ: eng.matmul(
                    psZ[:, 128 * dc:128 * dc + 128],
                    sb["t2t"][:, 2048 * dc + 128 * sc:2048 * dc + 128 * sc + 128],
                    sb["nwdiag"][:, 128 * dc:128 * dc + 128],
                    start=False, stop=(dc == DC - 1), skip_group_check=True))
            inc("t", last)
            if V:
                wait("t", c["t"])
                wait("ldy", c["ldy"])
            i = g_(V, lambda zf=zf, psZ=psZ: eng.scalar_tensor_tensor(
                zf[:, :], psZ, 1.0, zf[:, :], OP.mult, OP.add))
            inc("v", i)
            vdrain()
            i = g_(V, lambda zf=zf: eng.bn_stats(sb["bns"][:, 0:6], zf[:, 0:512]))
            i = g_(V, lambda zf=zf: eng.bn_stats(sb["bns"][:, 6:12], zf[:, 512:1024]))
            vdrain()
            i = g_(V, lambda: eng.bn_aggr(sb["bnagg"][:, 0:2], sb["bns"][:, 0:12]))
            vdrain()
            i = g_(V, lambda rcb=rcb: eng.tensor_scalar(
                sb["rc"][:, rcb:rcb + 1], sb["bnagg"][:, 1:2], 1.0, 1e-5,
                OP.mult, OP.add))
            inc("v", i)
            if SS:
                wait("v", c["v"])
            i = g_(SS, lambda rcb=rcb: eng.activation(
                sb["rc"][:, rcb + 1:rcb + 2], sb["rc"][:, rcb:rcb + 1], AF.Sqrt))
            inc("s", i)
            if V:
                wait("s", c["s"])
            i = g_(V, lambda rcb=rcb: eng.reciprocal(
                sb["rc"][:, rcb + 2:rcb + 3], sb["rc"][:, rcb + 1:rcb + 2]))
            vdrain()
            i = g_(V, lambda rcb=rcb: eng.tensor_scalar(
                sb["rc"][:, rcb + 3:rcb + 4], sb["bnagg"][:, 0:1],
                sb["rc"][:, rcb + 2:rcb + 3], -1.0, OP.mult, OP.mult))
            inc("v", i)
            if SS:
                wait("v", c["v"])
                if sc >= 2:
                    wait("sto", 16 * (sc - 1))
            i = g_(SS, lambda ph=ph, zf=zf, rcb=rcb: eng.activation(
                sb["ostg"][ph][:, :], zf[:, :], AF.Identity,
                scale=sb["rc"][:, rcb + 2:rcb + 3], bias=sb["rc"][:, rcb + 3:rcb + 4]))
            inc("s", i)
            if SY:
                wait("s", c["s"])
            i = g_(SY, lambda sc=sc, ph=ph: eng.dma_start(
                out=out_h[128 * sc:128 * sc + 128, :], in_=sb["ostg"][ph][:, :]))
            inc("sto", i, 16)

        if upto == 7:
            tapstop()
        if SY:
            wait("sto", c["sto"])

    import contextlib
    with nc.Block() as block, contextlib.ExitStack() as st:
        sems = {}
        for n in SEM_NAMES:
            sems[n] = st.enter_context(nc.semaphore(n))
        ctx["sems"] = sems

        def sbuf(name, shape, dt):
            ctx[name] = st.enter_context(nc.sbuf_tensor(name, shape, dt))
            return ctx[name]

        def psum(name, shape):
            ctx[name] = st.enter_context(nc.psum_tensor(name, shape, F32))

        sbuf("ident", [128, 128], BF16)
        sbuf("ones1", [1, 128], BF16)
        sbuf("bqc", [128, 4], F32); sbuf("bkc", [128, 4], F32)
        sbuf("bvb", [128, HD], F32)
        sbuf("dwc", [128, DC * 3], F32); sbuf("dwbc", [128, DC], F32)
        sbuf("pwbc", [128, DC], F32); sbuf("rbb", [128, E], F32)
        sbuf("nwc", [128, DC], F32)
        sbuf("ebr2", [8, D], BF16)
        sbuf("esel", [8, E * 128], BF16)
        sbuf("obr_sb", [1, D], BF16)
        sbuf("bor_sb", [1, D], BF16)
        sbuf("fdbr_sb", [1, HID], BF16); sbuf("fubr_sb", [1, D], BF16)
        sbuf("rwt_sb", [128, DC * E], BF16)
        sbuf("rc", [128, 32], F32)
        sbuf("bns", [128, 24], F32); sbuf("bnagg", [128, 4], F32)
        sbuf("rlg", [128, 8], F32); sbuf("rmx", [128, 8], F32)
        sbuf("ris", [128, 16], F32)
        sbuf("rw", [128, 8 * SC], F32)
        sbuf("vt", [128, KC * DH], BF16)
        for par in range(2):
            sbuf(f"wdenA{par}", [128, 16], F32)
            sbuf(f"wdenB{par}", [128, 16], F32)
            sbuf(f"wsum{par}", [128, 16], F32)
            sbuf(f"wcol_f{par}", [128, 16], F32)
            sbuf(f"ucol_f{par}", [128, 16], F32)
            sbuf(f"wcol_b{par}", [128, 16], BF16)
            sbuf(f"ucol_b{par}", [128, 16], BF16)
        for nm in ("wdenA", "wdenB", "wsum", "wcol_f", "ucol_f",
                   "wcol_b", "ucol_b"):
            ctx[nm] = [ctx[f"{nm}0"], ctx[f"{nm}1"]]
        sbuf("yf32_0", [128, D], F32); sbuf("yf32_1", [128, D], F32)
        ctx["yf32"] = [ctx["yf32_0"], ctx["yf32_1"]]

        AR1 = sbuf("AR1", [128, 32768], BF16)
        AR2 = sbuf("AR2", [128, 32832], BF16)
        AR5 = sbuf("AR5", [128, 8192], BF16)
        AR6 = sbuf("AR6", [128, 8192], BF16)
        AR7 = sbuf("AR7", [128, 8192], BF16)

        a1 = AR1.ap(); a2 = AR2.ap(); a5 = AR5.ap(); a6 = AR6.ap(); a7 = AR7.ap()
        ctx["wq_sb"] = a1[:, 0:4096]
        ctx["wk_sb"] = a1[:, 4096:8192]
        ctx["wv_sb"] = a1[:, 8192:12288]
        ctx["Abuf"] = a1[:, 0:32768]
        ctx["convacc"] = [a1[:, 0:4096].bitcast(F32), a1[:, 0:4096].bitcast(F32)]
        ctx["t1f"] = [a1[:, 16384:18432].bitcast(F32),
                      a1[:, 18432:20480].bitcast(F32),
                      a1[:, 20480:22528].bitcast(F32),
                      a1[:, 22528:24576].bitcast(F32)]
        ctx["ybf"] = [a1[:, 28672:29696], a1[:, 29696:30720],
                      a1[:, 30720:31744], a1[:, 31744:32768]]
        ctx["cacc0"] = a1[:, 24576:25600].bitcast(F32)
        ctx["x1t"] = a1[:, 0:16384]
        ctx["xc2t"] = a1[:, 16384:32768]
        ctx["combT"] = a1[:, 0:16384]
        ctx["ff1"] = a1[:, 0:4096]
        ctx["ff1t"] = a1[:, 4096:8192]
        ctx["nwdiag"] = a1[:, 8192:9216]
        ctx["xbf"] = a2[:, 0:16384]
        ctx["xT"] = a2[:, 16384:32768]
        ctx["Tbuf"] = a2[:, 0:32768]
        ctx["yt"] = a2[:, 0:16448]
        ctx["wet1"] = a2[:, 16448:24640]
        ctx["xct_a"] = a2[:, 24640:32832]
        ctx["combP"] = None
        ctx["cbf"] = [a2[:, 16448:17472], a2[:, 17472:18496]]
        ctx["t2bf"] = a2[:, 16448:32832]
        ctx["t2t"] = a2[:, 0:16384]
        ctx["qt"] = a5[:, 0:8192]
        ctx["ot"] = a5[:, 0:8192]
        ctx["pwt_sb"] = a5[:, 0:8192]
        ctx["hpf"] = [a5[:, 0:2048].bitcast(F32), a5[:, 2048:4096].bitcast(F32)]
        ctx["ostg"] = [a5[:, 4096:6144].bitcast(F32), a5[:, 6144:8192].bitcast(F32)]
        ctx["kt"] = a6[:, 0:8192]
        ctx["xct_b"] = a6[:, 0:8192]
        ctx["wet0"] = a6[:, 0:8192]
        ctx["owt_sb"] = a6[:, 0:8192]
        ctx["vna"] = a7[:, 0:8192]
        ctx["obuf"] = a7[:, 0:8192]
        ctx["fdt_sb"] = a7[:, 0:2048]
        ctx["fut_sb"] = a7[:, 2048:4096]
        ctx["wo_sb"] = a7[:, 4096:8192]
        ctx["zbuf"] = [a7[:, 4096:5120], a7[:, 5120:6144]]
        ctx["tmpf"] = a7[:, 4096:6144].bitcast(F32)
        ctx["trash"] = a7[:, 6144:7168]
        ctx["combP"] = [a5[:, 0:8192].bitcast(F32),
                        a2[:, 0:8192].bitcast(F32),
                        a2[:, 8192:16384].bitcast(F32),
                        a2[:, 24640:32832].bitcast(F32)]
        ctx["wet"] = [ctx["wet1"], ctx["wet0"]]

        psum("Lp", [128, 2048])
        psum("Cp", [128, 1024])
        psum("S0", [128, 512])
        psum("S1", [128, 512])
        ctx["CpB"] = ctx["Cp"][:, :].bitcast(BF16)

        def _run(me, eng):
            try:
                emit(me, eng)
            except Exception as ex:
                if type(ex).__name__ != "_Stop":
                    raise

        @block.tensor
        def _(eng):
            _run("tensor", eng)

        @block.vector
        def _(eng):
            _run("vector", eng)

        @block.scalar
        def _(eng):
            _run("scalar", eng)

        @block.gpsimd
        def _(eng):
            _run("gpsimd", eng)

        @block.sync
        def _(eng):
            _run("sync", eng)

    return nc


def _prep_core_inputs(b, inp):
    f = lambda a: np.ascontiguousarray(a, dtype=np.float32)
    d = {
        "x": f(np.asarray(inp["L0_attn_x"])[b]),
        "wq": f(inp["L0_attn_Wq"]), "wk": f(inp["L0_attn_Wk"]),
        "wv": f(inp["L0_attn_Wv"]),
        "bqc": f(np.asarray(inp["L0_attn_bq"]).reshape(4, 128).T),
        "bkc": f(np.asarray(inp["L0_attn_bk"]).reshape(4, 128).T),
        "bvb": f(np.broadcast_to(np.asarray(inp["L0_attn_bv"]), (128, HD))),
        "wo": f(inp["L0_attn_Wo"]),
        "bob": f(np.broadcast_to(np.asarray(inp["L0_attn_bo"]), (128, D))),
        "bor": f(np.asarray(inp["L0_attn_bo"]).reshape(1, D)),
        "dwc": f(np.asarray(inp["L0_ffn_dw_kernel"])
                 .reshape(DC, 128, 3).transpose(1, 0, 2).reshape(128, DC * 3)),
        "dwbc": f(np.asarray(inp["L0_ffn_dw_bias"]).reshape(DC, 128).T),
        "pwt": f(np.asarray(inp["L0_ffn_pw_w"]).T),
        "pwbc": f(np.asarray(inp["L0_ffn_pw_b"]).reshape(DC, 128).T),
        "rwt": f(np.asarray(inp["L0_ffn_router_w"]).T),
        "rbb": f(np.broadcast_to(np.asarray(inp["L0_ffn_router_b"]), (128, E))),
        "ewt": f(np.asarray(inp["L0_ffn_expert_w"]).transpose(0, 2, 1)),
        "ebr": f(inp["L0_ffn_expert_b"]),
        "owt": f(np.asarray(inp["L0_ffn_out_w"]).T),
        "obr": f(np.asarray(inp["L0_ffn_out_b"]).reshape(1, D)),
        "nwc": f(np.asarray(inp["L0_ffn_norm_w"]).reshape(DC, 128).T),
        "nwb": f(np.broadcast_to(np.asarray(inp["L0_ffn_norm_w"]), (128, D))),
        "fdt": f(np.asarray(inp["L0_ffn_ff_down_w"]).T),
        "fdbr": f(np.asarray(inp["L0_ffn_ff_down_b"]).reshape(1, HID)),
        "fut": f(np.asarray(inp["L0_ffn_ff_up_w"]).T),
        "fubr": f(np.asarray(inp["L0_ffn_ff_up_b"]).reshape(1, D)),
        "ident": np.eye(128, dtype=np.float32),
        "ones1": np.ones((1, 128), np.float32),
        "esel": np.kron(np.eye(E, dtype=np.float32), np.ones((1, 128), np.float32)),
    }
    return {"i_" + k: v for k, v in d.items()}


_NC_CACHE = {}


def kernel(**inputs):
    if "prog" not in _NC_CACHE:
        _NC_CACHE["prog"] = build_program()
    nc = _NC_CACHE["prog"]
    in_maps = [_prep_core_inputs(b, inputs) for b in range(8)]
    res = run_bass_kernel_spmd(nc, in_maps, core_ids=list(range(8)))
    return np.stack(
        [np.asarray(res.results[b]["out"], np.float32) for b in range(8)], axis=0)


# version 08
# speedup vs baseline: 1.0018x; 1.0018x over previous
import numpy as np
import concourse.bass as bass
import concourse.mybir as mybir
from concourse.bass_utils import run_bass_kernel_spmd
from concourse.alu_op_type import AluOpType as OP

F32 = mybir.dt.float32
BF16 = mybir.dt.bfloat16
AF = mybir.ActivationFunctionType

S = 2048
D = 1024
H = 8
DH = 64
HD = 512
E = 8
HID = 256
SC = S // 128
DC = D // 128
KC = 16
NPASS = 39
YTS = 2056

SEM_NAMES = [
    "ldg", "ldy", "lsy", "sto", "xrv",
    "t", "v", "s",
    "snkt", "snkv", "lgt", "lgs", "avt", "avs", "vt",
    "wet", "wef", "yet", "yes", "yev",
]


RSAFE = 0
TSAFE = 1


def build_program(upto=None, taps=()):
    nc = bass.Bass("TRN2", target_bir_lowering=False, debug=False)

    dr = {}
    def din(name, shape, dt=F32):
        dr[name] = nc.dram_tensor("i_" + name, shape, dt, kind="ExternalInput").ap()
    din("x", [S, D])
    din("wq", [D, HD]); din("wk", [D, HD]); din("wv", [D, HD])
    din("bqc", [128, 4]); din("bkc", [128, 4]); din("bvb", [128, HD])
    din("wo", [HD, D]); din("bob", [128, D]); din("bor", [1, D])
    din("dwc", [128, DC * 3]); din("dwbc", [128, DC])
    din("pwt", [D, D]); din("pwbc", [128, DC])
    din("rwt", [D, E]); din("rbb", [128, E])
    din("ewt", [E, D, D])
    din("ebr", [E, D])
    din("owt", [D, D]); din("obr", [1, D])
    din("nwc", [128, DC]); din("nwb", [128, D])
    din("fdt", [D, HID]); din("fdbr", [1, HID])
    din("fut", [HID, D]); din("fubr", [1, D])
    din("ident", [128, 128])
    din("ones1", [1, 128])
    din("esel", [8, E * 128])
    yspill = nc.dram_tensor("yspill", [S, D], F32, kind="Internal").ap()
    out_h = nc.dram_tensor("out", [S, D], F32, kind="ExternalOutput").ap()
    dbg_h = {}
    for (dn, sn, shp, dt) in taps:
        dbg_h[dn] = nc.dram_tensor(dn, shp, dt, kind="ExternalOutput").ap()

    ctx = {}
    RSAFE = globals()["RSAFE"]
    TSAFE = globals()["TSAFE"]

    def emit(me, eng):
        c = {n: 0 for n in SEM_NAMES}
        sems = ctx["sems"]
        sb = ctx

        def inc(name, instr, k=1):
            c[name] += k
            if instr is not None and instr is not False:
                instr.then_inc(sems[name], k)

        def wait(name, val):
            if eng is not None:
                eng.wait_ge(sems[name], val)

        T = me == "tensor"; V = me == "vector"; SS = me == "scalar"
        G = me == "gpsimd"; SY = me == "sync"

        def g_(cond, instr_fn):
            return instr_fn() if cond else None

        def vdrain():
            if V:
                eng.drain()

        def wload(dst_ap, src_dr, o):
            return eng.dma_start(
                out=dst_ap.rearrange("p (c o) -> p c o", o=o),
                in_=src_dr.rearrange("(c p) o -> p c o", p=128))

        class _Stop(Exception):
            pass

        def tapstop():
            if SY:
                for nm2 in ("t", "v", "s", "snkv", "lgs", "avs", "yev",
                            "yes", "vt"):
                    if c[nm2] > 0:
                        wait(nm2, c[nm2])
                ns = 0
                for (dn, sn, shp, dt) in taps:
                    ap = sb[sn] if isinstance(sn, str) else sn(sb)
                    ap = ap[:, :]
                    i2 = eng.dma_start(out=dbg_h[dn][:, :] if len(shp) == 2
                                       else dbg_h[dn][:], in_=ap)
                    i2.then_inc(sems["sto"], 16)
                    ns += 16
                wait("sto", c["sto"] + ns)
            raise _Stop()

        i = g_(G, lambda: wload(sb["xbf"], dr["x"], D))
        inc("ldg", i, 16)
        i = g_(G, lambda: eng.dma_start(out=sb["ident"][:, :], in_=dr["ident"][:, :]))
        inc("ldg", i, 16)
        i = g_(G, lambda: eng.dma_start(out=sb["ones1"][:, :], in_=dr["ones1"][:, :]))
        inc("ldg", i, 16)
        for wname, dst in (("wq", "wq_sb"), ("wk", "wk_sb"), ("wv", "wv_sb")):
            i = g_(G, lambda wname=wname, dst=dst: wload(sb[dst], dr[wname], HD))
            inc("ldg", i, 16)
        for nm in ("bqc", "bkc", "bvb"):
            i = g_(G, lambda nm=nm: eng.dma_start(out=sb[nm][:, :], in_=dr[nm][:, :]))
            inc("ldg", i, 16)
        LD_ATTN = c["ldg"]

        i = g_(G, lambda: eng.dma_start(out=sb["bor_sb"][:, :], in_=dr["bor"][:, :]))
        inc("ldg", i, 16)
        for nm in ("dwc", "dwbc", "pwbc", "rbb", "nwc"):
            i = g_(G, lambda nm=nm: eng.dma_start(out=sb[nm][:, :], in_=dr[nm][:, :]))
            inc("ldg", i, 16)
        i = g_(G, lambda: wload(sb["rwt_sb"], dr["rwt"], E))
        inc("ldg", i, 16)
        i = g_(G, lambda: eng.dma_start(out=sb["ebr2"][:, :], in_=dr["ebr"][:, :]))
        inc("ldg", i, 16)
        i = g_(G, lambda: eng.dma_start(out=sb["esel"][:, :], in_=dr["esel"][:, :]))
        inc("ldg", i, 16)
        for nm, dst in (("obr", "obr_sb"), ("fdbr", "fdbr_sb"), ("fubr", "fubr_sb")):
            i = g_(G, lambda nm=nm, dst=dst: eng.dma_start(
                out=sb[dst][:, :], in_=dr[nm][:, :]))
            inc("ldg", i, 16)
        LD_SM2 = c["ldg"]

        if T:
            wait("ldg", 48)
        v0 = c["v"]
        for sc in range(SC):
            for half in range(2):
                idx = 2 * sc + half
                ps = sb["CpB"][:, 1024 * half:1024 * half + 512]
                if T and idx >= 2:
                    wait("v", v0 + idx - 1)
                lastt = None
                for q in range(4):
                    dc = 4 * half + q
                    lastt = g_(T, lambda q=q, dc=dc, sc=sc, ps=ps: eng.transpose(
                        ps[:, 128 * q:128 * q + 128],
                        sb["xbf"][:, 1024 * sc + 128 * dc:1024 * sc + 128 * dc + 128],
                        sb["ident"][:, :]))
                inc("t", lastt)
                if V:
                    wait("t", c["t"])
                i = g_(V, lambda sc=sc, half=half, ps=ps: eng.tensor_copy(
                    sb["xT"].rearrange("p (dc s) -> p dc s", dc=DC)
                    [:, 4 * half:4 * half + 4, 128 * sc:128 * sc + 128],
                    ps.rearrange("p (q r) -> p q r", q=4)))
                inc("v", i)
        XT_V = c["v"]

        if T:
            wait("ldg", LD_ATTN)
            wait("v", XT_V)
        s0 = c["s"]
        for wi, (wsb, qdst, bcol) in enumerate(
                (("wq_sb", "qt", "bqc"), ("wk_sb", "kt", "bkc"))):
            for oc in range(4):
                idx = 4 * wi + oc
                if T and idx >= 1:
                    wait("s", s0 + idx)
                last = None
                for n4 in range(4):
                    for dc in range(DC):
                        last = g_(T, lambda n4=n4, dc=dc, oc=oc, wsb=wsb: eng.matmul(
                            sb["Lp"][:, 512 * n4:512 * n4 + 512],
                            sb[wsb][:, 512 * dc + 128 * oc:512 * dc + 128 * oc + 128],
                            sb["xT"][:, 2048 * dc + 512 * n4:2048 * dc + 512 * n4 + 512],
                            start=(dc == 0), stop=(dc == DC - 1)))
                inc("t", last)
                if SS:
                    wait("t", c["t"])
                i = g_(SS, lambda oc=oc, qdst=qdst, bcol=bcol: eng.activation(
                    sb[qdst][:, 2048 * oc:2048 * oc + 2048], sb["Lp"][:, :],
                    AF.Identity, bias=sb[bcol][:, oc:oc + 1], scale=1.0))
                inc("s", i)
        vv0 = c["v"]
        for sc in range(SC):
            ph = sc % 2
            ps = sb["Cp"][:, 512 * ph:512 * ph + 512]
            if T and sc >= 2:
                wait("v", vv0 + sc - 1)
            last = None
            for dc in range(DC):
                last = g_(T, lambda dc=dc, sc=sc, ps=ps: eng.matmul(
                    ps,
                    sb["xT"][:, 2048 * dc + 128 * sc:2048 * dc + 128 * sc + 128],
                    sb["wv_sb"][:, 512 * dc:512 * dc + 512],
                    start=(dc == 0), stop=(dc == DC - 1)))
            inc("t", last)
            if V:
                wait("t", c["t"])
            i = g_(V, lambda sc=sc, ps=ps: eng.tensor_tensor(
                sb["vna"][:, 512 * sc:512 * sc + 512], ps, sb["bvb"][:, :], OP.add))
            inc("v", i)
        PROJ_V = c["v"]; PROJ_S = c["s"]
        if upto == 1:
            tapstop()

        head_lgs = {}
        def snk_groups(parity):
            if parity == 0:
                return [sb["Cp"][:, 0:4], sb["Cp"][:, 512:516],
                        sb["S0"][:, 0:4], sb["S1"][:, 0:4]]
            return [sb["Lp"][:, 0:4], sb["Lp"][:, 512:516],
                    sb["Lp"][:, 1024:1028], sb["Lp"][:, 1536:1540]]

        def emit_logits2(h, transposed, av_h=None, p0_h=None):
            oc = h // 2
            p0 = 64 * (h % 2)
            src_l, src_r = ("kt", "qt") if transposed else ("qt", "kt")
            dstbuf = "Tbuf" if transposed else "Abuf"
            LT = c["lgt"]; LS = c["lgs"]
            if av_h is not None:
                AVB = c["avs"]
                hpav = av_h % 2
            if p0_h is not None:
                hp0 = p0_h % 2
                groups0 = snk_groups(0)
                vbase0 = c["snkv"]
                if T:
                    wait("avs", c["avs"])
            last0 = None
            for qc in range(KC):
                if av_h is not None:
                    phav = qc % 2
                    psav = sb["Cp"][:, 512 * phav:512 * phav + 64]
                    if T:
                        if qc == 0:
                            wait("vt", c["vt"])
                        if qc >= 2:
                            wait("avs", AVB + qc - 1)
                    lastav = None
                    for kc in range(KC):
                        lastav = g_(T, lambda kc=kc, qc=qc, psav=psav: eng.matmul(
                            psav,
                            sb["Tbuf"][:, 2048 * kc + 128 * qc:2048 * kc + 128 * qc + 128],
                            sb["vt"][:, 64 * kc:64 * kc + 64],
                            start=(kc == 0), stop=(kc == KC - 1)))
                    inc("avt", lastav)
                    if V:
                        wait("avt", c["avt"])
                    i = g_(V, lambda av_h=av_h, qc=qc, hpav=hpav, psav=psav:
                           eng.tensor_scalar(
                               sb["obuf"][:, 512 * qc + 64 * av_h:512 * qc + 64 * av_h + 64],
                               psav, sb["wcol_f"][hpav][:, qc:qc + 1], None, OP.mult))
                    inc("avs", i)
                if T and qc >= 1:
                    wait("lgs", LS + 2 * (qc - 1) + 1)
                lastmm = None
                for n4 in (0, 1):
                    lastmm = g_(T, lambda n4=n4, qc=qc, oc=oc, p0=p0,
                                src_l=src_l, src_r=src_r: eng.matmul(
                        sb["Lp"][:, 512 * n4:512 * n4 + 512],
                        sb[src_l][p0:p0 + 64,
                                  2048 * oc + 128 * qc:2048 * oc + 128 * qc + 128],
                        sb[src_r][p0:p0 + 64,
                                  2048 * oc + 512 * n4:2048 * oc + 512 * n4 + 512],
                        start=True, stop=True))
                inc("lgt", lastmm)
                if T and qc >= 1:
                    wait("lgs", LS + 2 * (qc - 1) + 2)
                for n4 in (2, 3):
                    lastmm = g_(T, lambda n4=n4, qc=qc, oc=oc, p0=p0,
                                src_l=src_l, src_r=src_r: eng.matmul(
                        sb["Lp"][:, 512 * n4:512 * n4 + 512],
                        sb[src_l][p0:p0 + 64,
                                  2048 * oc + 128 * qc:2048 * oc + 128 * qc + 128],
                        sb[src_r][p0:p0 + 64,
                                  2048 * oc + 512 * n4:2048 * oc + 512 * n4 + 512],
                        start=True, stop=True))
                inc("lgt", lastmm)
                if SS:
                    wait("lgt", LT + 2 * qc + 1)
                    if transposed and qc == 0 and c["avt"] > 0:
                        wait("avt", c["avt"])
                i = g_(SS, lambda h=h, qc=qc, transposed=transposed, dstbuf=dstbuf:
                       eng.activation(
                           sb[dstbuf][:, 2048 * qc:2048 * qc + 1024],
                           sb["Lp"][:, 0:1024], AF.Exp, scale=0.125,
                           accum_out=(None if transposed
                                      else sb["wdenA"][h % 2][:, qc:qc + 1])))
                inc("lgs", i)
                if SS:
                    wait("lgt", LT + 2 * qc + 2)
                i = g_(SS, lambda h=h, qc=qc, transposed=transposed, dstbuf=dstbuf:
                       eng.activation(
                           sb[dstbuf][:, 2048 * qc + 1024:2048 * qc + 2048],
                           sb["Lp"][:, 1024:2048], AF.Exp, scale=0.125,
                           accum_out=(None if transposed
                                      else sb["wdenB"][h % 2][:, qc:qc + 1])))
                inc("lgs", i)
                if p0_h is not None:
                    j = qc
                    gidx = j // 4
                    reg = groups0[gidx]
                    for cc in range(16):
                        if T and j == 0 and cc % 4 == 0:
                            wait("snkv", vbase0 - 4 + (cc // 4) + 1)
                        last0 = g_(T, lambda j=j, cc=cc, reg=reg, hp0=hp0: eng.matmul(
                            reg[:, j % 4:j % 4 + 1],
                            sb["Abuf"][:, 2048 * cc + 128 * j:2048 * cc + 128 * j + 128],
                            sb["wcol_b"][hp0][:, cc:cc + 1],
                            start=(j % 4 == 0 and cc == 0),
                            stop=(j % 4 == 3 and cc == 15),
                            skip_group_check=True))
                    if j % 4 == 3:
                        inc("snkt", last0)
                        if V:
                            wait("snkt", c["snkt"])
                        i = g_(V, lambda hp0=hp0, gidx=gidx, reg=reg: eng.reciprocal(
                            sb["ucol_f"][hp0][:, 4 * gidx:4 * gidx + 4], reg[:, 0:4]))
                        if gidx >= 1:
                            i2 = g_(V, lambda hp0=hp0, g2=gidx - 1: eng.tensor_copy(
                                sb["ucol_b"][hp0][:, 4 * g2:4 * g2 + 4],
                                sb["ucol_f"][hp0][:, 4 * g2:4 * g2 + 4]))
                            inc("snkv", i2)
            if p0_h is not None:
                i3 = g_(V, lambda hp0=hp0: eng.tensor_copy(
                    sb["ucol_b"][hp0][:, 12:16], sb["ucol_f"][hp0][:, 12:16]))
                inc("snkv", i3)

        def emit_w1(h):
            hp = h % 2
            base = c["lgs"]
            def _a(g):
                return g_(V, lambda: eng.tensor_tensor(
                    sb["wsum"][hp][:, 4 * g:4 * g + 4],
                    sb["wdenA"][hp][:, 4 * g:4 * g + 4],
                    sb["wdenB"][hp][:, 4 * g:4 * g + 4], OP.add))
            def _r(g):
                return g_(V, lambda: eng.reciprocal(
                    sb["wcol_f"][hp][:, 4 * g:4 * g + 4],
                    sb["wsum"][hp][:, 4 * g:4 * g + 4]))
            def _c(g):
                return g_(V, lambda: eng.tensor_copy(
                    sb["wcol_b"][hp][:, 4 * g:4 * g + 4],
                    sb["wcol_f"][hp][:, 4 * g:4 * g + 4]))
            if V:
                wait("lgs", base - 32 + 8)
            _a(0)
            if V:
                wait("lgs", base - 32 + 16)
            _a(1)
            _r(0)
            if V:
                wait("lgs", base - 32 + 24)
            _a(2)
            _r(1)
            inc("snkv", _c(0))
            if V:
                wait("lgs", base - 32 + 32)
            _a(3)
            _r(2)
            inc("snkv", _c(1))
            _r(3)
            inc("snkv", _c(2))
            inc("snkv", _c(3))

        def emit_pass(h, p):
            hp = h % 2
            col = (p % 2 == 0)
            mat = "Abuf" if col else "Tbuf"
            src_b = "wcol_b" if col else "ucol_b"
            dst_f = "ucol_f" if col else "wcol_f"
            dst_b = "ucol_b" if col else "wcol_b"
            groups = snk_groups(p % 2)
            vbase = c["snkv"]
            if T and p == 1:
                wait("lgs", head_lgs[h])
            last = None
            for j in range(16):
                gidx = j // 4
                reg = groups[gidx]
                for cc in range(16):
                    if T and j == 0 and cc % 4 == 0:
                        wait("snkv", vbase - 4 + (cc // 4) + 1)
                    last = g_(T, lambda j=j, cc=cc, reg=reg, mat=mat,
                              src_b=src_b, hp=hp, gidx=gidx: eng.matmul(
                        reg[:, j % 4:j % 4 + 1],
                        sb[mat][:, 2048 * cc + 128 * j:2048 * cc + 128 * j + 128],
                        sb[src_b][hp][:, cc:cc + 1],
                        start=(j % 4 == 0 and cc == 0),
                        stop=(j % 4 == 3 and cc == 15),
                        skip_group_check=True))
                if j % 4 == 3:
                    inc("snkt", last)
                    if V:
                        wait("snkt", c["snkt"])
                    i = g_(V, lambda hp=hp, gidx=gidx, reg=reg, dst_f=dst_f:
                           eng.reciprocal(
                               sb[dst_f][hp][:, 4 * gidx:4 * gidx + 4],
                               reg[:, 0:4]))
                    if gidx >= 1:
                        i2 = g_(V, lambda hp=hp, g2=gidx - 1, dst_f=dst_f,
                                dst_b=dst_b: eng.tensor_copy(
                                    sb[dst_b][hp][:, 4 * g2:4 * g2 + 4],
                                    sb[dst_f][hp][:, 4 * g2:4 * g2 + 4]))
                        inc("snkv", i2)
            i3 = g_(V, lambda hp=hp, dst_f=dst_f, dst_b=dst_b: eng.tensor_copy(
                sb[dst_b][hp][:, 12:16], sb[dst_f][hp][:, 12:16]))
            inc("snkv", i3)

        def emit_vtilde(h):
            hp = h % 2
            last = None
            for kc in range(KC):
                last = g_(V, lambda kc=kc, h=h, hp=hp: eng.tensor_scalar(
                    sb["vt"][:, 64 * kc:64 * kc + 64],
                    sb["vna"][:, 512 * kc + 64 * h:512 * kc + 64 * h + 64],
                    sb["ucol_f"][hp][:, kc:kc + 1], None, OP.mult))
            inc("vt", last)

        def emit_av(h):
            hp = h % 2
            av0 = c["avs"]
            for qc in range(KC):
                ph = qc % 2
                ps = sb["Cp"][:, 512 * ph:512 * ph + 64]
                if T:
                    if qc == 0:
                        wait("vt", c["vt"])
                    if qc >= 2:
                        wait("avs", av0 + qc - 1)
                last = None
                for kc in range(KC):
                    last = g_(T, lambda kc=kc, qc=qc, ps=ps: eng.matmul(
                        ps,
                        sb["Tbuf"][:, 2048 * kc + 128 * qc:2048 * kc + 128 * qc + 128],
                        sb["vt"][:, 64 * kc:64 * kc + 64],
                        start=(kc == 0), stop=(kc == KC - 1)))
                inc("avt", last)
                if V:
                    wait("avt", c["avt"])
                i = g_(V, lambda h=h, qc=qc, hp=hp, ps=ps: eng.tensor_scalar(
                    sb["obuf"][:, 512 * qc + 64 * h:512 * qc + 64 * h + 64],
                    ps, sb["wcol_f"][hp][:, qc:qc + 1], None, OP.mult))
                inc("avs", i)

        if T:
            wait("s", PROJ_S)
            wait("v", PROJ_V)
        emit_logits2(0, False)
        emit_w1(0)
        emit_logits2(0, True, p0_h=0)
        head_lgs[0] = c["lgs"]
        for h in range(H):
            for p in range(1, NPASS):
                emit_pass(h, p)
            if upto == 21 and h == 0:
                tapstop()
            if upto == 210 + h:
                tapstop()
            emit_vtilde(h)
            if h + 1 >= H:
                emit_av(h)
            if upto == 2 and h == 0:
                emit_av(h)
                tapstop()
            if h + 1 < H:
                if TSAFE & 1:
                    emit_av(h)
                if SS:
                    wait("snkt", c["snkt"])
                emit_logits2(h + 1, False,
                             av_h=(None if (TSAFE & 1) else h))
                emit_w1(h + 1)
                emit_logits2(h + 1, True,
                             p0_h=(None if (TSAFE & 2) else h + 1))
                head_lgs[h + 1] = c["lgs"]
                if TSAFE & 2:
                    emit_pass(h + 1, 0)
                if upto == 220 + h:
                    tapstop()

        if T:
            wait("avs", c["avs"])
            wait("ldg", LD_ATTN)
        v3 = c["v"]
        idx3 = 0
        for gq in (2, 3, 1, 0):
            for hc in range(4):
                ph = idx3 % 2
                ps = sb["CpB"][:, 1024 * ph:1024 * ph + 512]
                if T and idx3 >= 2:
                    wait("v", v3 + idx3 - 1)
                lastt = None
                for q in range(4):
                    sc = 4 * gq + q
                    lastt = g_(T, lambda sc=sc, hc=hc, q=q, ps=ps: eng.transpose(
                        ps[:, 128 * q:128 * q + 128],
                        sb["obuf"][:, 512 * sc + 128 * hc:512 * sc + 128 * hc + 128],
                        sb["ident"][:, :]))
                inc("t", lastt)
                if V:
                    wait("t", c["t"])
                i = g_(V, lambda hc=hc, gq=gq, ps=ps: eng.tensor_copy(
                    sb["ot"][:, 2048 * hc + 512 * gq:2048 * hc + 512 * gq + 512], ps))
                inc("v", i)
                idx3 += 1
        OT_V = c["v"]

        for wc in range(4):
            if G and wc == 0:
                wait("v", v3 + 4)
            if G and wc == 2:
                wait("v", v3 + 8)
            i = g_(G, lambda wc=wc: eng.dma_start(
                out=sb["wo_sb"][:, 1024 * wc:1024 * wc + 1024],
                in_=dr["wo"][128 * wc:128 * wc + 128, :]))
            inc("ldg", i, 16)
        LD_WO = c["ldg"]
        if G:
            wait("v", v3 + 12)
        i = g_(G, lambda: wload(sb["fut_sb"], dr["fut"], D))
        inc("ldg", i, 16)
        if G:
            wait("v", OT_V)
        i = g_(G, lambda: wload(sb["fdt_sb"], dr["fdt"], HID))
        inc("ldg", i, 16)
        LD_FDT = c["ldg"]
        if G:
            wait("avt", 16 * H)
        i = g_(G, lambda: wload(sb["wet1"], dr["ewt"][0], D))
        inc("wet", i, 16)
        if upto == 29:
            tapstop()

        lsy3 = c["lsy"]
        s_ybf = {}; s_evac = {}; s_sqrt = {}
        v_stt = {}; v_eps = {}; v_mb = {}; t_wo = {}
        v_conv = {}; v_x1t = {}

        def xct_sl(dc, lo, ln):
            buf = sb["xct_a"] if dc < 4 else sb["xct_b"]
            base = 2048 * (dc % 4)
            return buf[:, base + lo:base + lo + ln]

        if V:
            wait("avt", 16 * H)
        i = g_(V, lambda: eng.memset(
            sb["yt"].rearrange("p (dc s) -> p dc s", s=YTS)[:, :, 0:2], 0.0))
        inc("v", i)
        vdrain()

        def ph3_stageA(sc):
            ph = sc % 2
            psW = sb["Lp"][:, 1024 * ph:1024 * ph + 1024]
            t1 = sb["t1f"][sc % 4]
            if SY and sc == 0:
                wait("snkt", c["snkt"])
            if SY and sc >= 4:
                wait("s", s_ybf[sc - 4])
            i = g_(SY, lambda sc=sc, t1=t1: eng.dma_start(
                out=t1[:, :], in_=dr["x"][128 * sc:128 * sc + 128, :]))
            inc("lsy", i, 16)
            if T:
                if sc == 0:
                    wait("v", OT_V)
                    wait("ldg", LD_WO)
                if sc >= 2:
                    wait("v", v_stt[sc - 2])
            last = None
            for n2 in range(2):
                last = g_(T, lambda n2=n2, psW=psW: eng.matmul(
                    psW[:, 512 * n2:512 * n2 + 512], sb["ones1"][:, :],
                    sb["bor_sb"][0:1, 512 * n2:512 * n2 + 512],
                    start=True, stop=False, skip_group_check=True))
                for hcc in range(4):
                    last = g_(T, lambda n2=n2, hcc=hcc, sc=sc, psW=psW: eng.matmul(
                        psW[:, 512 * n2:512 * n2 + 512],
                        sb["ot"][:, 2048 * hcc + 128 * sc:2048 * hcc + 128 * sc + 128],
                        sb["wo_sb"][:, 1024 * hcc + 512 * n2:1024 * hcc + 512 * n2 + 512],
                        start=False, stop=(hcc == 3), skip_group_check=True))
            inc("t", last)
            t_wo[sc] = c["t"]
            rcb = 4 * ph
            if V:
                wait("t", t_wo[sc])
                wait("lsy", lsy3 + (16 if sc == 0 else 32 * sc))
            i = g_(V, lambda t1=t1, psW=psW: eng.scalar_tensor_tensor(
                t1[:, :], psW, 1.0, t1[:, :], OP.mult, OP.add))
            inc("v", i)
            v_stt[sc] = c["v"]
            vdrain()
            i = g_(V, lambda t1=t1, ph=ph: eng.bn_stats(
                sb["bns"][:, 12 * ph:12 * ph + 6], t1[:, 0:512]))
            i = g_(V, lambda t1=t1, ph=ph: eng.bn_stats(
                sb["bns"][:, 12 * ph + 6:12 * ph + 12], t1[:, 512:1024]))
            vdrain()
            i = g_(V, lambda ph=ph: eng.bn_aggr(
                sb["bnagg"][:, 2 * ph:2 * ph + 2],
                sb["bns"][:, 12 * ph:12 * ph + 12]))
            vdrain()
            i = g_(V, lambda rcb=rcb, ph=ph: eng.tensor_scalar(
                sb["rc"][:, rcb:rcb + 1], sb["bnagg"][:, 2 * ph + 1:2 * ph + 2],
                1.0, 1e-5, OP.mult, OP.add))
            inc("v", i)
            v_eps[sc] = c["v"]

        def ph3_stageB(sc):
            rcb = 4 * (sc % 2)
            if SS:
                wait("v", v_eps[sc])
            i = g_(SS, lambda rcb=rcb: eng.activation(
                sb["rc"][:, rcb + 1:rcb + 2], sb["rc"][:, rcb:rcb + 1], AF.Sqrt))
            inc("s", i)
            s_sqrt[sc] = c["s"]

        def ph3_stageC(sc):
            ph = sc % 2
            rcb = 4 * ph
            if V:
                wait("s", s_sqrt[sc])
            i = g_(V, lambda rcb=rcb: eng.reciprocal(
                sb["rc"][:, rcb + 2:rcb + 3], sb["rc"][:, rcb + 1:rcb + 2]))
            vdrain()
            i = g_(V, lambda rcb=rcb, ph=ph: eng.tensor_scalar(
                sb["rc"][:, rcb + 3:rcb + 4], sb["bnagg"][:, 2 * ph:2 * ph + 1],
                sb["rc"][:, rcb + 2:rcb + 3], -1.0, OP.mult, OP.mult))
            inc("v", i)
            v_mb[sc] = c["v"]

        def ph3_stageD(sc):
            ph = sc % 2
            rcb = 4 * ph
            t1 = sb["t1f"][sc % 4]
            if SS:
                wait("v", v_mb[sc])
                if sc >= 2:
                    wait("lsy", lsy3 + 32 * (sc - 2) + 48)
            i = g_(SS, lambda ph=ph, t1=t1, rcb=rcb: eng.activation(
                sb["yf32"][ph][:, :], t1[:, :], AF.Identity,
                scale=sb["rc"][:, rcb + 2:rcb + 3], bias=sb["rc"][:, rcb + 3:rcb + 4]))
            inc("s", i)
            i = g_(SS, lambda sc=sc, t1=t1, rcb=rcb: eng.activation(
                sb["ybf"][sc % 4][:, :], t1[:, :], AF.Identity,
                scale=sb["rc"][:, rcb + 2:rcb + 3], bias=sb["rc"][:, rcb + 3:rcb + 4]))
            inc("s", i)
            s_ybf[sc] = c["s"]
            if SY:
                wait("s", s_ybf[sc])
            i = g_(SY, lambda sc=sc, ph=ph: eng.dma_start(
                out=yspill[128 * sc:128 * sc + 128, :], in_=sb["yf32"][ph][:, :]))
            inc("lsy", i, 16)

        def ph3_back(sc):
            yb = sb["ybf"][sc % 4]
            for half in range(2):
                ps = sb["CpB"][:, 1024 * half:1024 * half + 512]
                if T:
                    wait("s", max(s_ybf[sc], s_evac.get((sc - 1, half), 0)))
                lastt = None
                for q in range(4):
                    dc = 4 * half + q
                    lastt = g_(T, lambda yb=yb, dc=dc, q=q, ps=ps: eng.transpose(
                        ps[:, 128 * q:128 * q + 128],
                        yb[:, 128 * dc:128 * dc + 128],
                        sb["ident"][:, :]))
                inc("t", lastt)
                if SS:
                    wait("t", c["t"])
                i = g_(SS, lambda sc=sc, half=half, ps=ps: eng.activation(
                    sb["yt"].rearrange("p (dc s) -> p dc s", s=YTS)
                    [:, 4 * half:4 * half + 4, 2 + 128 * sc:2 + 128 * sc + 128],
                    ps.rearrange("p (q r) -> p q r", q=4), AF.Copy))
                inc("s", i)
                s_evac[(sc, half)] = c["s"]

        def conv0_part(dc):
            ca = sb["cacc0"]
            y0 = YTS * dc
            if V and dc == 0:
                wait("ldg", LD_SM2)
                wait("s", s_evac[(3, 1)])
            vdrain()
            i = g_(V, lambda ca=ca, y0=y0, dc=dc: eng.tensor_scalar(
                ca[:, :], sb["yt"][:, y0:y0 + 512],
                sb["dwc"][:, 3 * dc:3 * dc + 1], sb["dwbc"][:, dc:dc + 1],
                OP.mult, OP.add))
            vdrain()
            i = g_(V, lambda ca=ca, y0=y0, dc=dc: eng.scalar_tensor_tensor(
                ca[:, :], sb["yt"][:, y0 + 1:y0 + 1 + 512],
                sb["dwc"][:, 3 * dc + 1:3 * dc + 2], ca[:, :], OP.mult, OP.add))
            vdrain()
            i = g_(V, lambda ca=ca, y0=y0, dc=dc: eng.scalar_tensor_tensor(
                xct_sl(dc, 0, 512),
                sb["yt"][:, y0 + 2:y0 + 2 + 512],
                sb["dwc"][:, 3 * dc + 2:3 * dc + 3], ca[:, :], OP.mult, OP.add))
            if dc == DC - 1:
                inc("v", i)
                v_conv[0] = c["v"]

        for sc in range(SC):
            ph3_stageA(sc)
            ph3_stageB(sc)
            if sc >= 1:
                ph3_stageC(sc - 1)
                ph3_stageD(sc - 1)
            if sc >= 8:
                conv0_part(sc - 8)
            if sc >= 3:
                ph3_back(sc - 3)
            if upto == 31 and sc == 3:
                tapstop()
        ph3_stageC(SC - 1)
        ph3_stageD(SC - 1)
        ph3_back(SC - 3)
        ph3_back(SC - 2)
        ph3_back(SC - 1)
        LSY3_END = c["lsy"]
        PH3_V = c["v"]
        if upto == 3:
            tapstop()

        LD_PWT0 = c["ldg"]
        pwt_pos = {}
        for pi, wc in enumerate((0, 2, 4, 6, 1, 3, 5, 7)):
            if G and pi == 0:
                wait("t", t_wo[7])
            if G and pi == 4:
                wait("t", t_wo[SC - 1])
            i = g_(G, lambda wc=wc: eng.dma_start(
                out=sb["pwt_sb"][:, 1024 * wc:1024 * wc + 1024],
                in_=dr["pwt"][128 * wc:128 * wc + 128, :]))
            inc("ldg", i, 16)
            pwt_pos[wc] = c["ldg"]
        LD_PWT = c["ldg"]

        cacc = sb["trash"].bitcast(F32)

        def conv_group(n4):
            lo = 512 * n4
            if V:
                wait("s", s_evac[(4 * n4 + 3, 1)])
            i = None
            for dc in range(DC):
                y0 = YTS * dc + lo
                vdrain()
                i = g_(V, lambda y0=y0, dc=dc: eng.tensor_scalar(
                    cacc[:, :], sb["yt"][:, y0:y0 + 512],
                    sb["dwc"][:, 3 * dc:3 * dc + 1], sb["dwbc"][:, dc:dc + 1],
                    OP.mult, OP.add))
                vdrain()
                i = g_(V, lambda y0=y0, dc=dc: eng.scalar_tensor_tensor(
                    cacc[:, :], sb["yt"][:, y0 + 1:y0 + 1 + 512],
                    sb["dwc"][:, 3 * dc + 1:3 * dc + 2], cacc[:, :],
                    OP.mult, OP.add))
                vdrain()
                i = g_(V, lambda y0=y0, dc=dc, lo=lo: eng.scalar_tensor_tensor(
                    xct_sl(dc, lo, 512),
                    sb["yt"][:, y0 + 2:y0 + 2 + 512],
                    sb["dwc"][:, 3 * dc + 2:3 * dc + 3], cacc[:, :],
                    OP.mult, OP.add))
            inc("v", i)
            v_conv[n4] = c["v"]

        if upto == 41:
            conv_group(1); conv_group(2); conv_group(3)
            tapstop()

        s4 = c["s"]
        for n4 in range(4):
            if n4 + 1 < 4:
                conv_group(n4 + 1)
            for ec in range(DC):
                idx = 8 * n4 + ec
                ph = idx % 2
                ps = sb["Lp"][:, 512 * ph:512 * ph + 512]
                if T:
                    if ec == 0:
                        wait("v", v_conv[n4])
                    if idx >= 2:
                        wait("s", s4 + idx - 1)
                last = None
                for dc in range(DC):
                    if T and n4 == 0:
                        wait("ldg", pwt_pos[dc])
                    last = g_(T, lambda dc=dc, ec=ec, n4=n4, ps=ps: eng.matmul(
                        ps,
                        sb["pwt_sb"][:, 1024 * dc + 128 * ec:1024 * dc + 128 * ec + 128],
                        xct_sl(dc, 512 * n4, 512),
                        start=(dc == 0), stop=(dc == DC - 1)))
                inc("t", last)
                if SS:
                    wait("t", c["t"])
                i = g_(SS, lambda ec=ec, n4=n4, ps=ps: eng.activation(
                    sb["xc2t"][:, 2048 * ec + 512 * n4:2048 * ec + 512 * n4 + 512],
                    ps, AF.Silu, bias=sb["pwbc"][:, ec:ec + 1]))
                inc("s", i)
                if V:
                    wait("s", c["s"])
                i = g_(V, lambda ec=ec, n4=n4: eng.tensor_tensor(
                    sb["x1t"][:, 2048 * ec + 512 * n4:2048 * ec + 512 * n4 + 512],
                    sb["xc2t"][:, 2048 * ec + 512 * n4:2048 * ec + 512 * n4 + 512],
                    sb["yt"][:, YTS * ec + 2 + 512 * n4:YTS * ec + 2 + 512 * n4 + 512],
                    OP.add))
                inc("v", i)
                if ec == DC - 1:
                    v_x1t[n4] = c["v"]
        PW_V = c["v"]
        PW_S = c["s"]
        t_pwend = c["t"]
        if upto == 42:
            tapstop()

        if G:
            wait("t", t_pwend)
        i = g_(G, lambda: wload(sb["wet0"], dr["ewt"][1], D))
        inc("wet", i, 16)

        if V:
            wait("t", t_pwend)
        for pc in range(4):
            i = g_(V, lambda pc=pc: eng.memset(sb["combP"][pc][:, :], 0.0))
            if pc == 3:
                inc("v", i)
        CB0_V = c["v"]

        rtr_inline = (upto == 4)
        T5R = c["t"]
        S5R = c["s"]
        v_sub = {}

        def comb_sl(sc):
            return sb["combP"][sc // 4][:, 1024 * (sc % 4):1024 * (sc % 4) + 1024]

        def router_mm(sc):
            ph = sc % 2
            ps = (sb["S0"] if ph == 0 else sb["S1"])[:, 0:8]
            if T:
                if sc == 0:
                    wait("v", PW_V)
                if sc >= 2:
                    wait("v", CB0_V + 3 * (sc - 2) + 1)
            last = None
            for dc in range(DC):
                last = g_(T, lambda dc=dc, sc=sc, ps=ps: eng.matmul(
                    ps,
                    sb["x1t"][:, 2048 * dc + 128 * sc:2048 * dc + 128 * sc + 128],
                    sb["rwt_sb"][:, 8 * dc:8 * dc + 8],
                    start=(dc == 0), stop=(dc == DC - 1)))
            inc("t", last)

        s4r = c["s"]
        for sc in range(SC):
            ph = sc % 2
            ps = (sb["S0"] if ph == 0 else sb["S1"])[:, 0:8]
            if rtr_inline:
                router_mm(sc)
            if V:
                wait("t", T5R + sc + 1)
            i = g_(V, lambda ps=ps: eng.tensor_tensor(
                sb["rlg"][:, :], ps, sb["rbb"][:, :], OP.add))
            inc("v", i)
            vdrain()
            if RSAFE & 1:
                i = g_(V, lambda: eng.tensor_copy(sb["rmx"][:, :], sb["rlg"][:, :]))
            else:
                i = g_(V, lambda: eng.max(sb["rmx"][:, :], sb["rlg"][:, :]))
            vdrain()
            i = g_(V, lambda: eng.tensor_tensor(
                sb["rc"][:, 8:9], sb["rmx"][:, 0:1], sb["rmx"][:, 1:2], OP.subtract))
            inc("v", i)
            v_sub[sc] = c["v"]
            if rtr_inline:
                if SS:
                    wait("v", v_sub[sc])
                i = g_(SS, lambda: eng.activation(
                    sb["rc"][:, 9:10], sb["rc"][:, 8:9],
                    AF.Identity if (RSAFE & 4) else AF.Sigmoid))
                inc("s", i)
            if V:
                wait("s", c["s"] if rtr_inline else S5R + sc + 1)
            i = g_(V, lambda: eng.tensor_scalar(
                sb["rc"][:, 10:11], sb["rc"][:, 9:10], -1.0, 1.0, OP.mult, OP.add))
            if RSAFE & 2:
                i = g_(V, lambda: eng.tensor_copy(sb["ris"][:, 0:8], sb["rlg"][:, :]))
                i = g_(V, lambda: eng.tensor_copy(sb["ris"][:, 8:16], sb["rlg"][:, :]))
            else:
                i = g_(V, lambda: eng.tensor_scalar(
                    sb["ris"][:, 0:8], sb["rlg"][:, :], sb["rmx"][:, 0:1],
                    sb["rc"][:, 9:10], OP.is_equal, OP.mult))
                i = g_(V, lambda: eng.tensor_scalar(
                    sb["ris"][:, 8:16], sb["rlg"][:, :], sb["rmx"][:, 1:2],
                    sb["rc"][:, 10:11], OP.is_equal, OP.mult))
            vdrain()
            i = g_(V, lambda sc=sc: eng.tensor_tensor(
                sb["rw"][:, 8 * sc:8 * sc + 8], sb["ris"][:, 0:8],
                sb["ris"][:, 8:16], OP.add))
            inc("v", i)
            vdrain()
            if not rtr_inline:
                if V:
                    wait("yes", sc + 1)
                i = g_(V, lambda sc=sc: eng.scalar_tensor_tensor(
                    comb_sl(sc),
                    sb["zbuf"][sc % 2][:, :], sb["rw"][:, 8 * sc:8 * sc + 1],
                    comb_sl(sc), OP.mult, OP.add))
                inc("yev", i)
        if upto == 4:
            tapstop()

        for e in range(E):
            for sc in range(SC):
                it = e * SC + sc
                ph = it % 2
                ps = sb["Lp"][:, 1024 * ph:1024 * ph + 1024]
                if T:
                    if it == 0:
                        wait("s", PW_S)
                        wait("ldg", LD_SM2)
                    if e == 0 and sc % 4 == 0:
                        wait("v", v_x1t[sc // 4])
                    if sc == 0:
                        wait("wet", 16 * (e + 1))
                    if it >= 2:
                        wait("yes", it - 1)
                last = None
                for n2 in range(2):
                    last = g_(T, lambda e=e, n2=n2, ps=ps: eng.matmul(
                        ps[:, 512 * n2:512 * n2 + 512],
                        sb["esel"][:, 128 * e:128 * e + 128],
                        sb["ebr2"][:, 512 * n2:512 * n2 + 512],
                        start=True, stop=False, skip_group_check=True))
                    for dc in range(DC):
                        last = g_(T, lambda e=e, dc=dc, sc=sc, n2=n2, ps=ps: eng.matmul(
                            ps[:, 512 * n2:512 * n2 + 512],
                            sb["x1t"][:, 2048 * dc + 128 * sc:2048 * dc + 128 * sc + 128],
                            sb["wet"][e % 2][:, 1024 * dc + 512 * n2:
                                             1024 * dc + 512 * n2 + 512],
                            start=False, stop=(dc == DC - 1), skip_group_check=True))
                inc("yet", last)
                if e == 0 and not rtr_inline:
                    router_mm(sc)
                    if SS:
                        wait("v", v_sub[sc])
                    i = g_(SS, lambda: eng.activation(
                        sb["rc"][:, 9:10], sb["rc"][:, 8:9],
                        AF.Identity if (RSAFE & 4) else AF.Sigmoid))
                    inc("s", i)
                if SS:
                    wait("yet", c["yet"])
                    if it >= 2:
                        wait("yev", it - 1)
                i = g_(SS, lambda ph=ph, ps=ps: eng.activation(
                    sb["zbuf"][ph][:, :], ps, AF.Silu))
                inc("yes", i)
                if e >= 1 or rtr_inline:
                    if V:
                        wait("yes", c["yes"])
                    i = g_(V, lambda e=e, sc=sc, ph=ph: eng.scalar_tensor_tensor(
                        comb_sl(sc),
                        sb["zbuf"][ph][:, :], sb["rw"][:, 8 * sc + e:8 * sc + e + 1],
                        comb_sl(sc), OP.mult, OP.add))
                    inc("yev", i)
            if e + 2 < E:
                if G:
                    wait("yet", 16 * (e + 1))
                i = g_(G, lambda e=e: wload(sb["wet"][e % 2], dr["ewt"][e + 2], D))
                inc("wet", i, 16)
        EXP_V = c["yev"]

        if upto == 5:
            tapstop()
        LD_OWT0 = c["ldg"]
        for wc in range(DC):
            if G and wc == 0:
                wait("yet", 16 * E)
            i = g_(G, lambda wc=wc: eng.dma_start(
                out=sb["owt_sb"][:, 1024 * wc:1024 * wc + 1024],
                in_=dr["owt"][128 * wc:128 * wc + 128, :]))
            inc("ldg", i, 16)
        LD_OWT = c["ldg"]

        t6 = c["t"]
        for sc in range(SC):
            if V:
                if sc == 0:
                    wait("yev", EXP_V)
                if sc >= 2:
                    wait("t", t6 + 2 * (sc - 2) + 2)
            i = g_(V, lambda sc=sc: eng.tensor_copy(
                sb["cbf"][sc % 2][:, :], comb_sl(sc)))
            inc("v", i)
            for half in range(2):
                ps = sb["CpB"][:, 1024 * half:1024 * half + 512]
                if T:
                    wait("v", c["v"])
                lastt = None
                for q in range(4):
                    dc = 4 * half + q
                    lastt = g_(T, lambda sc=sc, dc=dc, q=q, ps=ps: eng.transpose(
                        ps[:, 128 * q:128 * q + 128],
                        sb["cbf"][sc % 2][:, 128 * dc:128 * dc + 128],
                        sb["ident"][:, :]))
                inc("t", lastt)
                if V:
                    wait("t", c["t"])
                i = g_(V, lambda sc=sc, half=half, ps=ps: eng.tensor_copy(
                    sb["combT"].rearrange("p (dc s) -> p dc s", dc=DC)
                    [:, 4 * half:4 * half + 4, 128 * sc:128 * sc + 128],
                    ps.rearrange("p (q r) -> p q r", q=4)))
                inc("v", i)
        CBT_V = c["v"]

        s6 = c["s"]
        v6 = c["v"]
        for sc in range(SC):
            ph = sc % 2
            psM = sb["Lp"][:, 1024 * ph:1024 * ph + 1024]
            hp = sb["hpf"][ph]
            rcb = 16 + 4 * ph
            if G:
                if sc == 0:
                    wait("lsy", LSY3_END)
                if sc < 2:
                    wait("v", CBT_V)
                else:
                    wait("s", s6 + 3 * (sc - 2) + 3)
            i = g_(G, lambda hp=hp, sc=sc: eng.dma_start(
                out=hp[:, :], in_=yspill[128 * sc:128 * sc + 128, :]))
            inc("ldy", i, 16)
            if T:
                if sc == 0:
                    wait("v", CBT_V)
                if sc >= 2:
                    wait("v", v6 + 3 * (sc - 2) + 1)
            last = None
            for n2 in range(2):
                last = g_(T, lambda n2=n2, psM=psM: eng.matmul(
                    psM[:, 512 * n2:512 * n2 + 512], sb["ones1"][:, :],
                    sb["obr_sb"][0:1, 512 * n2:512 * n2 + 512],
                    start=True, stop=False, skip_group_check=True))
                for dc in range(DC):
                    if T and sc == 0 and n2 == 0:
                        wait("ldg", LD_OWT0 + 16 * (dc + 1))
                    last = g_(T, lambda dc=dc, sc=sc, n2=n2, psM=psM: eng.matmul(
                        psM[:, 512 * n2:512 * n2 + 512],
                        sb["combT"][:, 2048 * dc + 128 * sc:2048 * dc + 128 * sc + 128],
                        sb["owt_sb"][:, 1024 * dc + 512 * n2:1024 * dc + 512 * n2 + 512],
                        start=False, stop=(dc == DC - 1), skip_group_check=True))
            for dc in range(DC):
                last = g_(T, lambda dc=dc, sc=sc, psM=psM: eng.matmul(
                    psM[:, 128 * dc:128 * dc + 128],
                    sb["xc2t"][:, 2048 * dc + 128 * sc:2048 * dc + 128 * sc + 128],
                    sb["ident"][:, :],
                    start=False, stop=(dc == DC - 1), skip_group_check=True))
            inc("t", last)
            if V:
                wait("t", c["t"])
                wait("ldy", c["ldy"])
            i = g_(V, lambda hp=hp, psM=psM: eng.scalar_tensor_tensor(
                hp[:, :], psM, 1.0, hp[:, :], OP.mult, OP.add))
            inc("v", i)
            if SS:
                wait("v", c["v"])
            i = g_(SS, lambda hp=hp, rcb=rcb: eng.activation(
                sb["trash"][:, :], hp[:, :], AF.Square,
                accum_out=sb["rc"][:, rcb:rcb + 1]))
            inc("s", i)
            if V:
                wait("s", c["s"])
            i = g_(V, lambda rcb=rcb: eng.tensor_scalar(
                sb["rc"][:, rcb + 1:rcb + 2], sb["rc"][:, rcb:rcb + 1],
                1.0 / D, 1e-6, OP.mult, OP.add))
            inc("v", i)
            if SS:
                wait("v", c["v"])
            i = g_(SS, lambda rcb=rcb: eng.activation(
                sb["rc"][:, rcb + 2:rcb + 3], sb["rc"][:, rcb + 1:rcb + 2], AF.Sqrt))
            inc("s", i)
            if V:
                wait("s", c["s"])
            i = g_(V, lambda rcb=rcb: eng.reciprocal(
                sb["rc"][:, rcb + 3:rcb + 4], sb["rc"][:, rcb + 2:rcb + 3]))
            inc("v", i)
            if SS:
                wait("v", c["v"])
            i = g_(SS, lambda sc=sc, hp=hp, rcb=rcb: eng.activation(
                sb["t2bf"][:, 1024 * sc:1024 * sc + 1024], hp[:, :],
                AF.Identity, scale=sb["rc"][:, rcb + 3:rcb + 4]))
            inc("s", i)
        RMS_S = c["s"]

        v6t = c["v"]
        for sc in range(SC):
            for half in range(2):
                idx = 2 * sc + half
                ps = sb["CpB"][:, 1024 * half:1024 * half + 512]
                if T:
                    if idx == 0:
                        wait("s", RMS_S)
                    if idx >= 2:
                        wait("v", v6t + idx - 1)
                lastt = None
                for q in range(4):
                    dc = 4 * half + q
                    lastt = g_(T, lambda sc=sc, dc=dc, q=q, ps=ps: eng.transpose(
                        ps[:, 128 * q:128 * q + 128],
                        sb["t2bf"][:, 1024 * sc + 128 * dc:1024 * sc + 128 * dc + 128],
                        sb["ident"][:, :]))
                inc("t", lastt)
                if V:
                    wait("t", c["t"])
                i = g_(V, lambda sc=sc, half=half, ps=ps: eng.tensor_copy(
                    sb["t2t"].rearrange("p (dc s) -> p dc s", dc=DC)
                    [:, 4 * half:4 * half + 4, 128 * sc:128 * sc + 128],
                    ps.rearrange("p (q r) -> p q r", q=4)))
                inc("v", i)
        if V:
            wait("ldg", LD_FDT)
        last = None
        for dc in range(DC):
            last = g_(V, lambda dc=dc: eng.tensor_scalar(
                sb["fdt_sb"][:, 256 * dc:256 * dc + 256],
                sb["fdt_sb"][:, 256 * dc:256 * dc + 256],
                sb["nwc"][:, dc:dc + 1], None, OP.mult))
        inc("v", last)
        FD_V = c["v"]

        if upto == 6:
            tapstop()
        s7 = c["s"]
        for sc in range(SC):
            ph = sc % 2
            psF = (sb["S0"] if ph == 0 else sb["S1"])[:, 0:256]
            if T:
                if sc == 0:
                    wait("v", FD_V)
                if sc >= 2:
                    wait("s", s7 + sc - 1)
            last = g_(T, lambda psF=psF: eng.matmul(
                psF, sb["ones1"][:, :], sb["fdbr_sb"][0:1, :],
                start=True, stop=False, skip_group_check=True))
            for dc in range(DC):
                last = g_(T, lambda dc=dc, sc=sc, psF=psF: eng.matmul(
                    psF,
                    sb["t2t"][:, 2048 * dc + 128 * sc:2048 * dc + 128 * sc + 128],
                    sb["fdt_sb"][:, 256 * dc:256 * dc + 256],
                    start=False, stop=(dc == DC - 1), skip_group_check=True))
            inc("t", last)
            if SS:
                wait("t", c["t"])
            i = g_(SS, lambda sc=sc, psF=psF: eng.activation(
                sb["ff1"][:, 256 * sc:256 * sc + 256], psF, AF.Gelu))
            inc("s", i)
        FF1_S = c["s"]

        v7 = c["v"]
        for sc in range(SC):
            ph = sc % 2
            ps = sb["CpB"][:, 1024 * ph:1024 * ph + 256]
            if T:
                if sc == 0:
                    wait("s", FF1_S)
                if sc >= 2:
                    wait("v", v7 + sc - 1)
            lastt = None
            for q in range(2):
                lastt = g_(T, lambda sc=sc, q=q, ps=ps: eng.transpose(
                    ps[:, 128 * q:128 * q + 128],
                    sb["ff1"][:, 256 * sc + 128 * q:256 * sc + 128 * q + 128],
                    sb["ident"][:, :]))
            inc("t", lastt)
            if V:
                wait("t", c["t"])
            i = g_(V, lambda sc=sc, ps=ps: eng.tensor_copy(
                sb["ff1t"].rearrange("p (hc s) -> p hc s", hc=2)
                [:, :, 128 * sc:128 * sc + 128],
                ps.rearrange("p (q r) -> p q r", q=2)))
            inc("v", i)
        FF1T_V = c["v"]

        if V:
            wait("t", c["t"])
        lastd = None
        for dc in range(DC):
            lastd = g_(V, lambda dc=dc: eng.tensor_scalar(
                sb["nwdiag"][:, 128 * dc:128 * dc + 128], sb["ident"][:, :],
                sb["nwc"][:, dc:dc + 1], None, OP.mult))
        inc("v", lastd)
        NWD_V = c["v"]

        s7b = c["s"]
        v7b = c["v"]
        for sc in range(SC):
            ph = sc % 2
            psZ = sb["Lp"][:, 1024 * ph:1024 * ph + 1024]
            zf = sb["hpf"][ph]
            rcb = 24 + 4 * ph
            if G:
                if sc < 2:
                    wait("s", RMS_S)
                else:
                    wait("s", s7b + 2 * (sc - 2) + 2)
            i = g_(G, lambda zf=zf, sc=sc: eng.dma_start(
                out=zf[:, :], in_=yspill[128 * sc:128 * sc + 128, :]))
            inc("ldy", i, 16)
            if T:
                if sc == 0:
                    wait("v", NWD_V)
                if sc >= 2:
                    wait("v", v7b + 3 * (sc - 2) + 1)
            last = None
            for n2 in range(2):
                last = g_(T, lambda n2=n2, psZ=psZ: eng.matmul(
                    psZ[:, 512 * n2:512 * n2 + 512], sb["ones1"][:, :],
                    sb["fubr_sb"][0:1, 512 * n2:512 * n2 + 512],
                    start=True, stop=False, skip_group_check=True))
                for hcc in range(2):
                    last = g_(T, lambda hcc=hcc, sc=sc, n2=n2, psZ=psZ: eng.matmul(
                        psZ[:, 512 * n2:512 * n2 + 512],
                        sb["ff1t"][:, 2048 * hcc + 128 * sc:2048 * hcc + 128 * sc + 128],
                        sb["fut_sb"][:, 1024 * hcc + 512 * n2:1024 * hcc + 512 * n2 + 512],
                        start=False, stop=(hcc == 1), skip_group_check=True))
            for dc in range(DC):
                last = g_(T, lambda dc=dc, sc=sc, psZ=psZ: eng.matmul(
                    psZ[:, 128 * dc:128 * dc + 128],
                    sb["t2t"][:, 2048 * dc + 128 * sc:2048 * dc + 128 * sc + 128],
                    sb["nwdiag"][:, 128 * dc:128 * dc + 128],
                    start=False, stop=(dc == DC - 1), skip_group_check=True))
            inc("t", last)
            if V:
                wait("t", c["t"])
                wait("ldy", c["ldy"])
            i = g_(V, lambda zf=zf, psZ=psZ: eng.scalar_tensor_tensor(
                zf[:, :], psZ, 1.0, zf[:, :], OP.mult, OP.add))
            inc("v", i)
            vdrain()
            i = g_(V, lambda zf=zf: eng.bn_stats(sb["bns"][:, 0:6], zf[:, 0:512]))
            i = g_(V, lambda zf=zf: eng.bn_stats(sb["bns"][:, 6:12], zf[:, 512:1024]))
            vdrain()
            i = g_(V, lambda: eng.bn_aggr(sb["bnagg"][:, 0:2], sb["bns"][:, 0:12]))
            vdrain()
            i = g_(V, lambda rcb=rcb: eng.tensor_scalar(
                sb["rc"][:, rcb:rcb + 1], sb["bnagg"][:, 1:2], 1.0, 1e-5,
                OP.mult, OP.add))
            inc("v", i)
            if SS:
                wait("v", c["v"])
            i = g_(SS, lambda rcb=rcb: eng.activation(
                sb["rc"][:, rcb + 1:rcb + 2], sb["rc"][:, rcb:rcb + 1], AF.Sqrt))
            inc("s", i)
            if V:
                wait("s", c["s"])
            i = g_(V, lambda rcb=rcb: eng.reciprocal(
                sb["rc"][:, rcb + 2:rcb + 3], sb["rc"][:, rcb + 1:rcb + 2]))
            vdrain()
            i = g_(V, lambda rcb=rcb: eng.tensor_scalar(
                sb["rc"][:, rcb + 3:rcb + 4], sb["bnagg"][:, 0:1],
                sb["rc"][:, rcb + 2:rcb + 3], -1.0, OP.mult, OP.mult))
            inc("v", i)
            if SS:
                wait("v", c["v"])
                if sc >= 2:
                    wait("sto", 16 * (sc - 1))
            i = g_(SS, lambda ph=ph, zf=zf, rcb=rcb: eng.activation(
                sb["ostg"][ph][:, :], zf[:, :], AF.Identity,
                scale=sb["rc"][:, rcb + 2:rcb + 3], bias=sb["rc"][:, rcb + 3:rcb + 4]))
            inc("s", i)
            if SY:
                wait("s", c["s"])
            i = g_(SY, lambda sc=sc, ph=ph: eng.dma_start(
                out=out_h[128 * sc:128 * sc + 128, :], in_=sb["ostg"][ph][:, :]))
            inc("sto", i, 16)

        if upto == 7:
            tapstop()
        if SY:
            wait("sto", c["sto"])

    import contextlib
    with nc.Block() as block, contextlib.ExitStack() as st:
        sems = {}
        for n in SEM_NAMES:
            sems[n] = st.enter_context(nc.semaphore(n))
        ctx["sems"] = sems

        def sbuf(name, shape, dt):
            ctx[name] = st.enter_context(nc.sbuf_tensor(name, shape, dt))
            return ctx[name]

        def psum(name, shape):
            ctx[name] = st.enter_context(nc.psum_tensor(name, shape, F32))

        sbuf("ident", [128, 128], BF16)
        sbuf("ones1", [1, 128], BF16)
        sbuf("bqc", [128, 4], F32); sbuf("bkc", [128, 4], F32)
        sbuf("bvb", [128, HD], F32)
        sbuf("dwc", [128, DC * 3], F32); sbuf("dwbc", [128, DC], F32)
        sbuf("pwbc", [128, DC], F32); sbuf("rbb", [128, E], F32)
        sbuf("nwc", [128, DC], F32)
        sbuf("ebr2", [8, D], BF16)
        sbuf("esel", [8, E * 128], BF16)
        sbuf("obr_sb", [1, D], BF16)
        sbuf("bor_sb", [1, D], BF16)
        sbuf("fdbr_sb", [1, HID], BF16); sbuf("fubr_sb", [1, D], BF16)
        sbuf("rwt_sb", [128, DC * E], BF16)
        sbuf("rc", [128, 32], F32)
        sbuf("bns", [128, 24], F32); sbuf("bnagg", [128, 4], F32)
        sbuf("rlg", [128, 8], F32); sbuf("rmx", [128, 8], F32)
        sbuf("ris", [128, 16], F32)
        sbuf("rw", [128, 8 * SC], F32)
        sbuf("vt", [128, KC * DH], BF16)
        for par in range(2):
            sbuf(f"wdenA{par}", [128, 16], F32)
            sbuf(f"wdenB{par}", [128, 16], F32)
            sbuf(f"wsum{par}", [128, 16], F32)
            sbuf(f"wcol_f{par}", [128, 16], F32)
            sbuf(f"ucol_f{par}", [128, 16], F32)
            sbuf(f"wcol_b{par}", [128, 16], BF16)
            sbuf(f"ucol_b{par}", [128, 16], BF16)
        for nm in ("wdenA", "wdenB", "wsum", "wcol_f", "ucol_f",
                   "wcol_b", "ucol_b"):
            ctx[nm] = [ctx[f"{nm}0"], ctx[f"{nm}1"]]
        sbuf("yf32_0", [128, D], F32); sbuf("yf32_1", [128, D], F32)
        ctx["yf32"] = [ctx["yf32_0"], ctx["yf32_1"]]

        AR1 = sbuf("AR1", [128, 32768], BF16)
        AR2 = sbuf("AR2", [128, 32832], BF16)
        AR5 = sbuf("AR5", [128, 8192], BF16)
        AR6 = sbuf("AR6", [128, 8192], BF16)
        AR7 = sbuf("AR7", [128, 8192], BF16)

        a1 = AR1.ap(); a2 = AR2.ap(); a5 = AR5.ap(); a6 = AR6.ap(); a7 = AR7.ap()
        ctx["wq_sb"] = a1[:, 0:4096]
        ctx["wk_sb"] = a1[:, 4096:8192]
        ctx["wv_sb"] = a1[:, 8192:12288]
        ctx["Abuf"] = a1[:, 0:32768]
        ctx["convacc"] = [a1[:, 0:4096].bitcast(F32), a1[:, 0:4096].bitcast(F32)]
        ctx["t1f"] = [a1[:, 16384:18432].bitcast(F32),
                      a1[:, 18432:20480].bitcast(F32),
                      a1[:, 20480:22528].bitcast(F32),
                      a1[:, 22528:24576].bitcast(F32)]
        ctx["ybf"] = [a1[:, 28672:29696], a1[:, 29696:30720],
                      a1[:, 30720:31744], a1[:, 31744:32768]]
        ctx["cacc0"] = a1[:, 24576:25600].bitcast(F32)
        ctx["x1t"] = a1[:, 0:16384]
        ctx["xc2t"] = a1[:, 16384:32768]
        ctx["combT"] = a1[:, 0:16384]
        ctx["ff1"] = a1[:, 0:4096]
        ctx["ff1t"] = a1[:, 4096:8192]
        ctx["nwdiag"] = a1[:, 8192:9216]
        ctx["xbf"] = a2[:, 0:16384]
        ctx["xT"] = a2[:, 16384:32768]
        ctx["Tbuf"] = a2[:, 0:32768]
        ctx["yt"] = a2[:, 0:16448]
        ctx["wet1"] = a2[:, 16448:24640]
        ctx["xct_a"] = a2[:, 24640:32832]
        ctx["combP"] = None
        ctx["cbf"] = [a2[:, 16448:17472], a2[:, 17472:18496]]
        ctx["t2bf"] = a2[:, 16448:32832]
        ctx["t2t"] = a2[:, 0:16384]
        ctx["qt"] = a5[:, 0:8192]
        ctx["ot"] = a5[:, 0:8192]
        ctx["pwt_sb"] = a5[:, 0:8192]
        ctx["hpf"] = [a5[:, 0:2048].bitcast(F32), a5[:, 2048:4096].bitcast(F32)]
        ctx["ostg"] = [a5[:, 4096:6144].bitcast(F32), a5[:, 6144:8192].bitcast(F32)]
        ctx["kt"] = a6[:, 0:8192]
        ctx["xct_b"] = a6[:, 0:8192]
        ctx["wet0"] = a6[:, 0:8192]
        ctx["owt_sb"] = a6[:, 0:8192]
        ctx["vna"] = a7[:, 0:8192]
        ctx["obuf"] = a7[:, 0:8192]
        ctx["fdt_sb"] = a7[:, 0:2048]
        ctx["fut_sb"] = a7[:, 2048:4096]
        ctx["wo_sb"] = a7[:, 4096:8192]
        ctx["zbuf"] = [a7[:, 4096:5120], a7[:, 5120:6144]]
        ctx["tmpf"] = a7[:, 4096:6144].bitcast(F32)
        ctx["trash"] = a7[:, 6144:7168]
        ctx["combP"] = [a5[:, 0:8192].bitcast(F32),
                        a2[:, 0:8192].bitcast(F32),
                        a2[:, 8192:16384].bitcast(F32),
                        a2[:, 24640:32832].bitcast(F32)]
        ctx["wet"] = [ctx["wet1"], ctx["wet0"]]

        psum("Lp", [128, 2048])
        psum("Cp", [128, 1024])
        psum("S0", [128, 512])
        psum("S1", [128, 512])
        ctx["CpB"] = ctx["Cp"][:, :].bitcast(BF16)

        def _run(me, eng):
            try:
                emit(me, eng)
            except Exception as ex:
                if type(ex).__name__ != "_Stop":
                    raise

        @block.tensor
        def _(eng):
            _run("tensor", eng)

        @block.vector
        def _(eng):
            _run("vector", eng)

        @block.scalar
        def _(eng):
            _run("scalar", eng)

        @block.gpsimd
        def _(eng):
            _run("gpsimd", eng)

        @block.sync
        def _(eng):
            _run("sync", eng)

    return nc


def _prep_core_inputs(b, inp):
    f = lambda a: np.ascontiguousarray(a, dtype=np.float32)
    d = {
        "x": f(np.asarray(inp["L0_attn_x"])[b]),
        "wq": f(inp["L0_attn_Wq"]), "wk": f(inp["L0_attn_Wk"]),
        "wv": f(inp["L0_attn_Wv"]),
        "bqc": f(np.asarray(inp["L0_attn_bq"]).reshape(4, 128).T),
        "bkc": f(np.asarray(inp["L0_attn_bk"]).reshape(4, 128).T),
        "bvb": f(np.broadcast_to(np.asarray(inp["L0_attn_bv"]), (128, HD))),
        "wo": f(inp["L0_attn_Wo"]),
        "bob": f(np.broadcast_to(np.asarray(inp["L0_attn_bo"]), (128, D))),
        "bor": f(np.asarray(inp["L0_attn_bo"]).reshape(1, D)),
        "dwc": f(np.asarray(inp["L0_ffn_dw_kernel"])
                 .reshape(DC, 128, 3).transpose(1, 0, 2).reshape(128, DC * 3)),
        "dwbc": f(np.asarray(inp["L0_ffn_dw_bias"]).reshape(DC, 128).T),
        "pwt": f(np.asarray(inp["L0_ffn_pw_w"]).T),
        "pwbc": f(np.asarray(inp["L0_ffn_pw_b"]).reshape(DC, 128).T),
        "rwt": f(np.asarray(inp["L0_ffn_router_w"]).T),
        "rbb": f(np.broadcast_to(np.asarray(inp["L0_ffn_router_b"]), (128, E))),
        "ewt": f(np.asarray(inp["L0_ffn_expert_w"]).transpose(0, 2, 1)),
        "ebr": f(inp["L0_ffn_expert_b"]),
        "owt": f(np.asarray(inp["L0_ffn_out_w"]).T),
        "obr": f(np.asarray(inp["L0_ffn_out_b"]).reshape(1, D)),
        "nwc": f(np.asarray(inp["L0_ffn_norm_w"]).reshape(DC, 128).T),
        "nwb": f(np.broadcast_to(np.asarray(inp["L0_ffn_norm_w"]), (128, D))),
        "fdt": f(np.asarray(inp["L0_ffn_ff_down_w"]).T),
        "fdbr": f(np.asarray(inp["L0_ffn_ff_down_b"]).reshape(1, HID)),
        "fut": f(np.asarray(inp["L0_ffn_ff_up_w"]).T),
        "fubr": f(np.asarray(inp["L0_ffn_ff_up_b"]).reshape(1, D)),
        "ident": np.eye(128, dtype=np.float32),
        "ones1": np.ones((1, 128), np.float32),
        "esel": np.kron(np.eye(E, dtype=np.float32), np.ones((1, 128), np.float32)),
    }
    return {"i_" + k: v for k, v in d.items()}


_NC_CACHE = {}


def kernel(**inputs):
    if "prog" not in _NC_CACHE:
        _NC_CACHE["prog"] = build_program()
    nc = _NC_CACHE["prog"]
    in_maps = [_prep_core_inputs(b, inputs) for b in range(8)]
    res = run_bass_kernel_spmd(nc, in_maps, core_ids=list(range(8)))
    return np.stack(
        [np.asarray(res.results[b]["out"], np.float32) for b in range(8)], axis=0)


# version 09
# speedup vs baseline: 1.0023x; 1.0005x over previous
import numpy as np
import concourse.bass as bass
import concourse.mybir as mybir
from concourse.bass_utils import run_bass_kernel_spmd
from concourse.alu_op_type import AluOpType as OP

F32 = mybir.dt.float32
BF16 = mybir.dt.bfloat16
AF = mybir.ActivationFunctionType

S = 2048
D = 1024
H = 8
DH = 64
HD = 512
E = 8
HID = 256
SC = S // 128
DC = D // 128
KC = 16
NPASS = 39
YTS = 2056

SEM_NAMES = [
    "ldg", "ldy", "lsy", "sto", "xrv",
    "t", "v", "s",
    "snkt", "snkv", "lgt", "lgs", "avt", "avs", "vt",
    "wet", "wef", "yet", "yes", "yev",
]


RSAFE = 0
TSAFE = 1


def build_program(upto=None, taps=()):
    nc = bass.Bass("TRN2", target_bir_lowering=False, debug=False)

    dr = {}
    def din(name, shape, dt=F32):
        dr[name] = nc.dram_tensor("i_" + name, shape, dt, kind="ExternalInput").ap()
    din("x", [S, D])
    din("wq", [D, HD]); din("wk", [D, HD]); din("wv", [D, HD])
    din("bqc", [128, 4]); din("bkc", [128, 4]); din("bvb", [128, HD])
    din("wo", [HD, D]); din("bob", [128, D]); din("bor", [1, D])
    din("dwc", [128, DC * 3]); din("dwbc", [128, DC])
    din("pwt", [D, D]); din("pwbc", [128, DC])
    din("rwt", [D, E]); din("rbb", [128, E])
    din("ewt", [E, D, D])
    din("ebr", [E, D])
    din("owt", [D, D]); din("obr", [1, D])
    din("nwc", [128, DC]); din("nwb", [128, D])
    din("fdt", [D, HID]); din("fdbr", [1, HID])
    din("fut", [HID, D]); din("fubr", [1, D])
    din("ident", [128, 128])
    din("ones1", [1, 128])
    din("esel", [8, E * 128])
    yspill = nc.dram_tensor("yspill", [S, D], F32, kind="Internal").ap()
    out_h = nc.dram_tensor("out", [S, D], F32, kind="ExternalOutput").ap()
    dbg_h = {}
    for (dn, sn, shp, dt) in taps:
        dbg_h[dn] = nc.dram_tensor(dn, shp, dt, kind="ExternalOutput").ap()

    ctx = {}
    RSAFE = globals()["RSAFE"]
    TSAFE = globals()["TSAFE"]

    def emit(me, eng):
        c = {n: 0 for n in SEM_NAMES}
        sems = ctx["sems"]
        sb = ctx

        def inc(name, instr, k=1):
            c[name] += k
            if instr is not None and instr is not False:
                instr.then_inc(sems[name], k)

        def wait(name, val):
            if eng is not None:
                eng.wait_ge(sems[name], val)

        T = me == "tensor"; V = me == "vector"; SS = me == "scalar"
        G = me == "gpsimd"; SY = me == "sync"

        def g_(cond, instr_fn):
            return instr_fn() if cond else None

        def vdrain():
            if V:
                eng.drain()

        def wload(dst_ap, src_dr, o):
            return eng.dma_start(
                out=dst_ap.rearrange("p (c o) -> p c o", o=o),
                in_=src_dr.rearrange("(c p) o -> p c o", p=128))

        class _Stop(Exception):
            pass

        def tapstop():
            if SY:
                for nm2 in ("t", "v", "s", "snkv", "lgs", "avs", "yev",
                            "yes", "vt"):
                    if c[nm2] > 0:
                        wait(nm2, c[nm2])
                ns = 0
                for (dn, sn, shp, dt) in taps:
                    ap = sb[sn] if isinstance(sn, str) else sn(sb)
                    ap = ap[:, :]
                    i2 = eng.dma_start(out=dbg_h[dn][:, :] if len(shp) == 2
                                       else dbg_h[dn][:], in_=ap)
                    i2.then_inc(sems["sto"], 16)
                    ns += 16
                wait("sto", c["sto"] + ns)
            raise _Stop()

        i = g_(G, lambda: wload(sb["xbf"], dr["x"], D))
        inc("ldg", i, 16)
        i = g_(G, lambda: eng.dma_start(out=sb["ident"][:, :], in_=dr["ident"][:, :]))
        inc("ldg", i, 16)
        i = g_(G, lambda: eng.dma_start(out=sb["ones1"][:, :], in_=dr["ones1"][:, :]))
        inc("ldg", i, 16)
        for wname, dst in (("wq", "wq_sb"), ("wk", "wk_sb"), ("wv", "wv_sb")):
            i = g_(G, lambda wname=wname, dst=dst: wload(sb[dst], dr[wname], HD))
            inc("ldg", i, 16)
        for nm in ("bqc", "bkc", "bvb"):
            i = g_(G, lambda nm=nm: eng.dma_start(out=sb[nm][:, :], in_=dr[nm][:, :]))
            inc("ldg", i, 16)
        LD_ATTN = c["ldg"]

        i = g_(G, lambda: eng.dma_start(out=sb["bor_sb"][:, :], in_=dr["bor"][:, :]))
        inc("ldg", i, 16)
        for nm in ("dwc", "dwbc", "pwbc", "rbb", "nwc"):
            i = g_(G, lambda nm=nm: eng.dma_start(out=sb[nm][:, :], in_=dr[nm][:, :]))
            inc("ldg", i, 16)
        i = g_(G, lambda: wload(sb["rwt_sb"], dr["rwt"], E))
        inc("ldg", i, 16)
        i = g_(G, lambda: eng.dma_start(out=sb["ebr2"][:, :], in_=dr["ebr"][:, :]))
        inc("ldg", i, 16)
        i = g_(G, lambda: eng.dma_start(out=sb["esel"][:, :], in_=dr["esel"][:, :]))
        inc("ldg", i, 16)
        for nm, dst in (("obr", "obr_sb"), ("fdbr", "fdbr_sb"), ("fubr", "fubr_sb")):
            i = g_(G, lambda nm=nm, dst=dst: eng.dma_start(
                out=sb[dst][:, :], in_=dr[nm][:, :]))
            inc("ldg", i, 16)
        LD_SM2 = c["ldg"]

        if T:
            wait("ldg", 48)
        v0 = c["v"]
        for sc in range(SC):
            for half in range(2):
                idx = 2 * sc + half
                ps = sb["CpB"][:, 1024 * half:1024 * half + 512]
                if T and idx >= 2:
                    wait("v", v0 + idx - 1)
                lastt = None
                for q in range(4):
                    dc = 4 * half + q
                    lastt = g_(T, lambda q=q, dc=dc, sc=sc, ps=ps: eng.transpose(
                        ps[:, 128 * q:128 * q + 128],
                        sb["xbf"][:, 1024 * sc + 128 * dc:1024 * sc + 128 * dc + 128],
                        sb["ident"][:, :]))
                inc("t", lastt)
                if V:
                    wait("t", c["t"])
                i = g_(V, lambda sc=sc, half=half, ps=ps: eng.tensor_copy(
                    sb["xT"].rearrange("p (dc s) -> p dc s", dc=DC)
                    [:, 4 * half:4 * half + 4, 128 * sc:128 * sc + 128],
                    ps.rearrange("p (q r) -> p q r", q=4)))
                inc("v", i)
        XT_V = c["v"]

        if T:
            wait("ldg", LD_ATTN)
            wait("v", XT_V)
        s0 = c["s"]
        for wi, (wsb, qdst, bcol) in enumerate(
                (("wq_sb", "qt", "bqc"), ("wk_sb", "kt", "bkc"))):
            for oc in range(4):
                idx = 4 * wi + oc
                if T and idx >= 1:
                    wait("s", s0 + idx)
                last = None
                for n4 in range(4):
                    for dc in range(DC):
                        last = g_(T, lambda n4=n4, dc=dc, oc=oc, wsb=wsb: eng.matmul(
                            sb["Lp"][:, 512 * n4:512 * n4 + 512],
                            sb[wsb][:, 512 * dc + 128 * oc:512 * dc + 128 * oc + 128],
                            sb["xT"][:, 2048 * dc + 512 * n4:2048 * dc + 512 * n4 + 512],
                            start=(dc == 0), stop=(dc == DC - 1)))
                inc("t", last)
                if SS:
                    wait("t", c["t"])
                i = g_(SS, lambda oc=oc, qdst=qdst, bcol=bcol: eng.activation(
                    sb[qdst][:, 2048 * oc:2048 * oc + 2048], sb["Lp"][:, :],
                    AF.Identity, bias=sb[bcol][:, oc:oc + 1], scale=1.0))
                inc("s", i)
        vv0 = c["v"]
        for sc in range(SC):
            ph = sc % 2
            ps = sb["Cp"][:, 512 * ph:512 * ph + 512]
            if T and sc >= 2:
                wait("v", vv0 + sc - 1)
            last = None
            for dc in range(DC):
                last = g_(T, lambda dc=dc, sc=sc, ps=ps: eng.matmul(
                    ps,
                    sb["xT"][:, 2048 * dc + 128 * sc:2048 * dc + 128 * sc + 128],
                    sb["wv_sb"][:, 512 * dc:512 * dc + 512],
                    start=(dc == 0), stop=(dc == DC - 1)))
            inc("t", last)
            if V:
                wait("t", c["t"])
            i = g_(V, lambda sc=sc, ps=ps: eng.tensor_tensor(
                sb["vna"][:, 512 * sc:512 * sc + 512], ps, sb["bvb"][:, :], OP.add))
            inc("v", i)
        PROJ_V = c["v"]; PROJ_S = c["s"]
        if upto == 1:
            tapstop()

        head_lgs = {}
        def snk_groups(parity):
            if parity == 0:
                return [sb["Cp"][:, 0:4], sb["Cp"][:, 512:516],
                        sb["S0"][:, 0:4], sb["S1"][:, 0:4]]
            return [sb["Lp"][:, 0:4], sb["Lp"][:, 512:516],
                    sb["Lp"][:, 1024:1028], sb["Lp"][:, 1536:1540]]

        def emit_logits2(h, transposed, av_h=None, p0_h=None):
            oc = h // 2
            p0 = 64 * (h % 2)
            src_l, src_r = ("kt", "qt") if transposed else ("qt", "kt")
            dstbuf = "Tbuf" if transposed else "Abuf"
            LT = c["lgt"]; LS = c["lgs"]
            if av_h is not None:
                AVB = c["avs"]
                hpav = av_h % 2
            if p0_h is not None:
                hp0 = p0_h % 2
                groups0 = snk_groups(0)
                vbase0 = c["snkv"]
                if T:
                    wait("avs", c["avs"])
            last0 = None
            for qc in range(KC):
                if av_h is not None:
                    phav = qc % 2
                    psav = sb["Cp"][:, 512 * phav:512 * phav + 64]
                    if T:
                        if qc == 0:
                            wait("vt", c["vt"])
                        if qc >= 2:
                            wait("avs", AVB + qc - 1)
                    lastav = None
                    for kc in range(KC):
                        lastav = g_(T, lambda kc=kc, qc=qc, psav=psav: eng.matmul(
                            psav,
                            sb["Tbuf"][:, 2048 * kc + 128 * qc:2048 * kc + 128 * qc + 128],
                            sb["vt"][:, 64 * kc:64 * kc + 64],
                            start=(kc == 0), stop=(kc == KC - 1)))
                    inc("avt", lastav)
                    if V:
                        wait("avt", c["avt"])
                    i = g_(V, lambda av_h=av_h, qc=qc, hpav=hpav, psav=psav:
                           eng.tensor_scalar(
                               sb["obuf"][:, 512 * qc + 64 * av_h:512 * qc + 64 * av_h + 64],
                               psav, sb["wcol_f"][hpav][:, qc:qc + 1], None, OP.mult))
                    inc("avs", i)
                if T and qc >= 1:
                    wait("lgs", LS + 2 * (qc - 1) + 1)
                lastmm = None
                for n4 in (0, 1):
                    lastmm = g_(T, lambda n4=n4, qc=qc, oc=oc, p0=p0,
                                src_l=src_l, src_r=src_r: eng.matmul(
                        sb["Lp"][:, 512 * n4:512 * n4 + 512],
                        sb[src_l][p0:p0 + 64,
                                  2048 * oc + 128 * qc:2048 * oc + 128 * qc + 128],
                        sb[src_r][p0:p0 + 64,
                                  2048 * oc + 512 * n4:2048 * oc + 512 * n4 + 512],
                        start=True, stop=True))
                inc("lgt", lastmm)
                if T and qc >= 1:
                    wait("lgs", LS + 2 * (qc - 1) + 2)
                for n4 in (2, 3):
                    lastmm = g_(T, lambda n4=n4, qc=qc, oc=oc, p0=p0,
                                src_l=src_l, src_r=src_r: eng.matmul(
                        sb["Lp"][:, 512 * n4:512 * n4 + 512],
                        sb[src_l][p0:p0 + 64,
                                  2048 * oc + 128 * qc:2048 * oc + 128 * qc + 128],
                        sb[src_r][p0:p0 + 64,
                                  2048 * oc + 512 * n4:2048 * oc + 512 * n4 + 512],
                        start=True, stop=True))
                inc("lgt", lastmm)
                if SS:
                    wait("lgt", LT + 2 * qc + 1)
                    if transposed and qc == 0 and c["avt"] > 0:
                        wait("avt", c["avt"])
                i = g_(SS, lambda h=h, qc=qc, transposed=transposed, dstbuf=dstbuf:
                       eng.activation(
                           sb[dstbuf][:, 2048 * qc:2048 * qc + 1024],
                           sb["Lp"][:, 0:1024], AF.Exp, scale=0.125,
                           accum_out=(None if transposed
                                      else sb["wdenA"][h % 2][:, qc:qc + 1])))
                inc("lgs", i)
                if SS:
                    wait("lgt", LT + 2 * qc + 2)
                i = g_(SS, lambda h=h, qc=qc, transposed=transposed, dstbuf=dstbuf:
                       eng.activation(
                           sb[dstbuf][:, 2048 * qc + 1024:2048 * qc + 2048],
                           sb["Lp"][:, 1024:2048], AF.Exp, scale=0.125,
                           accum_out=(None if transposed
                                      else sb["wdenB"][h % 2][:, qc:qc + 1])))
                inc("lgs", i)
                if p0_h is not None:
                    j = qc
                    gidx = j // 4
                    reg = groups0[gidx]
                    for cc in range(16):
                        if T and j == 0 and cc % 4 == 0:
                            wait("snkv", vbase0 - 4 + (cc // 4) + 1)
                        last0 = g_(T, lambda j=j, cc=cc, reg=reg, hp0=hp0: eng.matmul(
                            reg[:, j % 4:j % 4 + 1],
                            sb["Abuf"][:, 2048 * cc + 128 * j:2048 * cc + 128 * j + 128],
                            sb["wcol_b"][hp0][:, cc:cc + 1],
                            start=(j % 4 == 0 and cc == 0),
                            stop=(j % 4 == 3 and cc == 15),
                            skip_group_check=True))
                    if j % 4 == 3:
                        inc("snkt", last0)
                        if V:
                            wait("snkt", c["snkt"])
                        i = g_(V, lambda hp0=hp0, gidx=gidx, reg=reg: eng.reciprocal(
                            sb["ucol_f"][hp0][:, 4 * gidx:4 * gidx + 4], reg[:, 0:4]))
                        if gidx >= 1:
                            i2 = g_(V, lambda hp0=hp0, g2=gidx - 1: eng.tensor_copy(
                                sb["ucol_b"][hp0][:, 4 * g2:4 * g2 + 4],
                                sb["ucol_f"][hp0][:, 4 * g2:4 * g2 + 4]))
                            inc("snkv", i2)
            if p0_h is not None:
                i3 = g_(V, lambda hp0=hp0: eng.tensor_copy(
                    sb["ucol_b"][hp0][:, 12:16], sb["ucol_f"][hp0][:, 12:16]))
                inc("snkv", i3)

        def emit_w1(h):
            hp = h % 2
            base = c["lgs"]
            def _a(g):
                return g_(V, lambda: eng.tensor_tensor(
                    sb["wsum"][hp][:, 4 * g:4 * g + 4],
                    sb["wdenA"][hp][:, 4 * g:4 * g + 4],
                    sb["wdenB"][hp][:, 4 * g:4 * g + 4], OP.add))
            def _r(g):
                return g_(V, lambda: eng.reciprocal(
                    sb["wcol_f"][hp][:, 4 * g:4 * g + 4],
                    sb["wsum"][hp][:, 4 * g:4 * g + 4]))
            def _c(g):
                return g_(V, lambda: eng.tensor_copy(
                    sb["wcol_b"][hp][:, 4 * g:4 * g + 4],
                    sb["wcol_f"][hp][:, 4 * g:4 * g + 4]))
            if V:
                wait("lgs", base - 32 + 8)
            _a(0)
            if V:
                wait("lgs", base - 32 + 16)
            _a(1)
            _r(0)
            if V:
                wait("lgs", base - 32 + 24)
            _a(2)
            _r(1)
            inc("snkv", _c(0))
            if V:
                wait("lgs", base - 32 + 32)
            _a(3)
            _r(2)
            inc("snkv", _c(1))
            _r(3)
            inc("snkv", _c(2))
            inc("snkv", _c(3))

        def emit_pass(h, p):
            hp = h % 2
            col = (p % 2 == 0)
            mat = "Abuf" if col else "Tbuf"
            src_b = "wcol_b" if col else "ucol_b"
            dst_f = "ucol_f" if col else "wcol_f"
            dst_b = "ucol_b" if col else "wcol_b"
            groups = snk_groups(p % 2)
            vbase = c["snkv"]
            if T and p == 1:
                wait("lgs", head_lgs[h])
            last = None
            for j in range(16):
                gidx = j // 4
                reg = groups[gidx]
                for cc in range(16):
                    if T and j == 0 and cc % 4 == 0:
                        wait("snkv", vbase - 4 + (cc // 4) + 1)
                    last = g_(T, lambda j=j, cc=cc, reg=reg, mat=mat,
                              src_b=src_b, hp=hp, gidx=gidx: eng.matmul(
                        reg[:, j % 4:j % 4 + 1],
                        sb[mat][:, 2048 * cc + 128 * j:2048 * cc + 128 * j + 128],
                        sb[src_b][hp][:, cc:cc + 1],
                        start=(j % 4 == 0 and cc == 0),
                        stop=(j % 4 == 3 and cc == 15),
                        skip_group_check=True))
                if j % 4 == 3:
                    inc("snkt", last)
                    if V:
                        wait("snkt", c["snkt"])
                    i = g_(V, lambda hp=hp, gidx=gidx, reg=reg, dst_f=dst_f:
                           eng.reciprocal(
                               sb[dst_f][hp][:, 4 * gidx:4 * gidx + 4],
                               reg[:, 0:4]))
                    if gidx >= 1:
                        i2 = g_(V, lambda hp=hp, g2=gidx - 1, dst_f=dst_f,
                                dst_b=dst_b: eng.tensor_copy(
                                    sb[dst_b][hp][:, 4 * g2:4 * g2 + 4],
                                    sb[dst_f][hp][:, 4 * g2:4 * g2 + 4]))
                        inc("snkv", i2)
            i3 = g_(V, lambda hp=hp, dst_f=dst_f, dst_b=dst_b: eng.tensor_copy(
                sb[dst_b][hp][:, 12:16], sb[dst_f][hp][:, 12:16]))
            inc("snkv", i3)

        def emit_vtilde(h):
            hp = h % 2
            last = None
            for kc in range(KC):
                last = g_(V, lambda kc=kc, h=h, hp=hp: eng.tensor_scalar(
                    sb["vt"][:, 64 * kc:64 * kc + 64],
                    sb["vna"][:, 512 * kc + 64 * h:512 * kc + 64 * h + 64],
                    sb["ucol_f"][hp][:, kc:kc + 1], None, OP.mult))
            inc("vt", last)

        def emit_av(h):
            hp = h % 2
            av0 = c["avs"]
            for qc in range(KC):
                ph = qc % 2
                ps = sb["Cp"][:, 512 * ph:512 * ph + 64]
                if T:
                    if qc == 0:
                        wait("vt", c["vt"])
                    if qc >= 2:
                        wait("avs", av0 + qc - 1)
                last = None
                for kc in range(KC):
                    last = g_(T, lambda kc=kc, qc=qc, ps=ps: eng.matmul(
                        ps,
                        sb["Tbuf"][:, 2048 * kc + 128 * qc:2048 * kc + 128 * qc + 128],
                        sb["vt"][:, 64 * kc:64 * kc + 64],
                        start=(kc == 0), stop=(kc == KC - 1)))
                inc("avt", last)
                if V:
                    wait("avt", c["avt"])
                i = g_(V, lambda h=h, qc=qc, hp=hp, ps=ps: eng.tensor_scalar(
                    sb["obuf"][:, 512 * qc + 64 * h:512 * qc + 64 * h + 64],
                    ps, sb["wcol_f"][hp][:, qc:qc + 1], None, OP.mult))
                inc("avs", i)

        if T:
            wait("s", PROJ_S)
            wait("v", PROJ_V)
        emit_logits2(0, False)
        emit_w1(0)
        emit_logits2(0, True, p0_h=0)
        head_lgs[0] = c["lgs"]
        for h in range(H):
            for p in range(1, NPASS):
                emit_pass(h, p)
            if upto == 21 and h == 0:
                tapstop()
            if upto == 210 + h:
                tapstop()
            emit_vtilde(h)
            if h + 1 >= H:
                emit_av(h)
            if upto == 2 and h == 0:
                emit_av(h)
                tapstop()
            if h + 1 < H:
                if TSAFE & 1:
                    emit_av(h)
                if SS:
                    wait("snkt", c["snkt"])
                emit_logits2(h + 1, False,
                             av_h=(None if (TSAFE & 1) else h))
                emit_w1(h + 1)
                emit_logits2(h + 1, True,
                             p0_h=(None if (TSAFE & 2) else h + 1))
                head_lgs[h + 1] = c["lgs"]
                if TSAFE & 2:
                    emit_pass(h + 1, 0)
                if upto == 220 + h:
                    tapstop()

        if T:
            wait("avs", c["avs"])
            wait("ldg", LD_ATTN)
        v3 = c["v"]
        idx3 = 0
        for gq in (2, 3, 1, 0):
            for hc in range(4):
                ph = idx3 % 2
                ps = sb["CpB"][:, 1024 * ph:1024 * ph + 512]
                if T and idx3 >= 2:
                    wait("v", v3 + idx3 - 1)
                lastt = None
                for q in range(4):
                    sc = 4 * gq + q
                    lastt = g_(T, lambda sc=sc, hc=hc, q=q, ps=ps: eng.transpose(
                        ps[:, 128 * q:128 * q + 128],
                        sb["obuf"][:, 512 * sc + 128 * hc:512 * sc + 128 * hc + 128],
                        sb["ident"][:, :]))
                inc("t", lastt)
                if V:
                    wait("t", c["t"])
                i = g_(V, lambda hc=hc, gq=gq, ps=ps: eng.tensor_copy(
                    sb["ot"][:, 2048 * hc + 512 * gq:2048 * hc + 512 * gq + 512], ps))
                inc("v", i)
                idx3 += 1
        OT_V = c["v"]

        for wc in range(4):
            if G and wc == 0:
                wait("v", v3 + 4)
            if G and wc == 2:
                wait("v", v3 + 8)
            i = g_(G, lambda wc=wc: eng.dma_start(
                out=sb["wo_sb"][:, 1024 * wc:1024 * wc + 1024],
                in_=dr["wo"][128 * wc:128 * wc + 128, :]))
            inc("ldg", i, 16)
        LD_WO = c["ldg"]
        if G:
            wait("v", v3 + 12)
        i = g_(G, lambda: wload(sb["fut_sb"], dr["fut"], D))
        inc("ldg", i, 16)
        if G:
            wait("v", OT_V)
        i = g_(G, lambda: wload(sb["fdt_sb"], dr["fdt"], HID))
        inc("ldg", i, 16)
        LD_FDT = c["ldg"]
        if G:
            wait("avt", 16 * H)
        i = g_(G, lambda: wload(sb["wet1"], dr["ewt"][0], D))
        inc("wet", i, 16)
        if upto == 29:
            tapstop()

        lsy3 = c["lsy"]
        s_ybf = {}; s_evac = {}; s_sqrt = {}
        v_stt = {}; v_eps = {}; v_mb = {}; v_yf = {}; t_wo = {}
        v_conv = {}; v_x1t = {}

        def xct_sl(dc, lo, ln):
            buf = sb["xct_a"] if dc < 4 else sb["xct_b"]
            base = 2048 * (dc % 4)
            return buf[:, base + lo:base + lo + ln]

        if V:
            wait("avt", 16 * H)
        i = g_(V, lambda: eng.memset(
            sb["yt"].rearrange("p (dc s) -> p dc s", s=YTS)[:, :, 0:2], 0.0))
        inc("v", i)
        vdrain()

        def ph3_stageA(sc):
            ph = sc % 2
            psW = sb["Lp"][:, 1024 * ph:1024 * ph + 1024]
            t1 = sb["t1f"][sc % 4]
            if SY and sc == 0:
                wait("snkt", c["snkt"])
            if SY and sc >= 4:
                wait("s", s_ybf[sc - 4])
                wait("v", v_yf[sc - 4])
            i = g_(SY, lambda sc=sc, t1=t1: eng.dma_start(
                out=t1[:, :], in_=dr["x"][128 * sc:128 * sc + 128, :]))
            inc("lsy", i, 16)
            if T:
                if sc == 0:
                    wait("v", OT_V)
                    wait("ldg", LD_WO)
                if sc >= 2:
                    wait("v", v_stt[sc - 2])
            last = None
            for n2 in range(2):
                last = g_(T, lambda n2=n2, psW=psW: eng.matmul(
                    psW[:, 512 * n2:512 * n2 + 512], sb["ones1"][:, :],
                    sb["bor_sb"][0:1, 512 * n2:512 * n2 + 512],
                    start=True, stop=False, skip_group_check=True))
                for hcc in range(4):
                    last = g_(T, lambda n2=n2, hcc=hcc, sc=sc, psW=psW: eng.matmul(
                        psW[:, 512 * n2:512 * n2 + 512],
                        sb["ot"][:, 2048 * hcc + 128 * sc:2048 * hcc + 128 * sc + 128],
                        sb["wo_sb"][:, 1024 * hcc + 512 * n2:1024 * hcc + 512 * n2 + 512],
                        start=False, stop=(hcc == 3), skip_group_check=True))
            inc("t", last)
            t_wo[sc] = c["t"]
            rcb = 4 * ph
            if V:
                wait("t", t_wo[sc])
                wait("lsy", lsy3 + (16 if sc == 0 else 32 * sc))
            i = g_(V, lambda t1=t1, psW=psW: eng.scalar_tensor_tensor(
                t1[:, :], psW, 1.0, t1[:, :], OP.mult, OP.add))
            inc("v", i)
            v_stt[sc] = c["v"]
            vdrain()
            i = g_(V, lambda t1=t1, ph=ph: eng.bn_stats(
                sb["bns"][:, 12 * ph:12 * ph + 6], t1[:, 0:512]))
            i = g_(V, lambda t1=t1, ph=ph: eng.bn_stats(
                sb["bns"][:, 12 * ph + 6:12 * ph + 12], t1[:, 512:1024]))
            vdrain()
            i = g_(V, lambda ph=ph: eng.bn_aggr(
                sb["bnagg"][:, 2 * ph:2 * ph + 2],
                sb["bns"][:, 12 * ph:12 * ph + 12]))
            vdrain()
            i = g_(V, lambda rcb=rcb, ph=ph: eng.tensor_scalar(
                sb["rc"][:, rcb:rcb + 1], sb["bnagg"][:, 2 * ph + 1:2 * ph + 2],
                1.0, 1e-5, OP.mult, OP.add))
            inc("v", i)
            v_eps[sc] = c["v"]

        def ph3_stageB(sc):
            rcb = 4 * (sc % 2)
            if SS:
                wait("v", v_eps[sc])
            i = g_(SS, lambda rcb=rcb: eng.activation(
                sb["rc"][:, rcb + 1:rcb + 2], sb["rc"][:, rcb:rcb + 1], AF.Sqrt))
            inc("s", i)
            s_sqrt[sc] = c["s"]

        def ph3_stageC(sc):
            ph = sc % 2
            rcb = 4 * ph
            if V:
                wait("s", s_sqrt[sc])
            i = g_(V, lambda rcb=rcb: eng.reciprocal(
                sb["rc"][:, rcb + 2:rcb + 3], sb["rc"][:, rcb + 1:rcb + 2]))
            vdrain()
            i = g_(V, lambda rcb=rcb, ph=ph: eng.tensor_scalar(
                sb["rc"][:, rcb + 3:rcb + 4], sb["bnagg"][:, 2 * ph:2 * ph + 1],
                sb["rc"][:, rcb + 2:rcb + 3], -1.0, OP.mult, OP.mult))
            inc("v", i)
            v_mb[sc] = c["v"]
            vdrain()
            if V and sc >= 2:
                wait("lsy", lsy3 + 32 * (sc - 2) + 48)
            i = g_(V, lambda sc=sc, ph=ph, rcb=rcb: eng.tensor_scalar(
                sb["yf32"][ph][:, :], sb["t1f"][sc % 4][:, :],
                sb["rc"][:, rcb + 2:rcb + 3], sb["rc"][:, rcb + 3:rcb + 4],
                OP.mult, OP.add))
            inc("v", i)
            v_yf[sc] = c["v"]

        def ph3_stageD(sc):
            ph = sc % 2
            rcb = 4 * ph
            t1 = sb["t1f"][sc % 4]
            if SS:
                wait("v", v_mb[sc])
            i = g_(SS, lambda sc=sc, t1=t1, rcb=rcb: eng.activation(
                sb["ybf"][sc % 4][:, :], t1[:, :], AF.Identity,
                scale=sb["rc"][:, rcb + 2:rcb + 3], bias=sb["rc"][:, rcb + 3:rcb + 4]))
            inc("s", i)
            s_ybf[sc] = c["s"]
            if SY:
                wait("s", s_ybf[sc])
                wait("v", v_yf[sc])
            i = g_(SY, lambda sc=sc, ph=ph: eng.dma_start(
                out=yspill[128 * sc:128 * sc + 128, :], in_=sb["yf32"][ph][:, :]))
            inc("lsy", i, 16)

        def ph3_back(sc):
            yb = sb["ybf"][sc % 4]
            for half in range(2):
                ps = sb["CpB"][:, 1024 * half:1024 * half + 512]
                if T:
                    wait("s", max(s_ybf[sc], s_evac.get((sc - 1, half), 0)))
                lastt = None
                for q in range(4):
                    dc = 4 * half + q
                    lastt = g_(T, lambda yb=yb, dc=dc, q=q, ps=ps: eng.transpose(
                        ps[:, 128 * q:128 * q + 128],
                        yb[:, 128 * dc:128 * dc + 128],
                        sb["ident"][:, :]))
                inc("t", lastt)
                if SS:
                    wait("t", c["t"])
                i = g_(SS, lambda sc=sc, half=half, ps=ps: eng.activation(
                    sb["yt"].rearrange("p (dc s) -> p dc s", s=YTS)
                    [:, 4 * half:4 * half + 4, 2 + 128 * sc:2 + 128 * sc + 128],
                    ps.rearrange("p (q r) -> p q r", q=4), AF.Copy))
                inc("s", i)
                s_evac[(sc, half)] = c["s"]

        def conv0_part(dc):
            ca = sb["cacc0"]
            y0 = YTS * dc
            if V and dc == 0:
                wait("ldg", LD_SM2)
                wait("s", s_evac[(3, 1)])
            vdrain()
            i = g_(V, lambda ca=ca, y0=y0, dc=dc: eng.tensor_scalar(
                ca[:, :], sb["yt"][:, y0:y0 + 512],
                sb["dwc"][:, 3 * dc:3 * dc + 1], sb["dwbc"][:, dc:dc + 1],
                OP.mult, OP.add))
            vdrain()
            i = g_(V, lambda ca=ca, y0=y0, dc=dc: eng.scalar_tensor_tensor(
                ca[:, :], sb["yt"][:, y0 + 1:y0 + 1 + 512],
                sb["dwc"][:, 3 * dc + 1:3 * dc + 2], ca[:, :], OP.mult, OP.add))
            vdrain()
            i = g_(V, lambda ca=ca, y0=y0, dc=dc: eng.scalar_tensor_tensor(
                xct_sl(dc, 0, 512),
                sb["yt"][:, y0 + 2:y0 + 2 + 512],
                sb["dwc"][:, 3 * dc + 2:3 * dc + 3], ca[:, :], OP.mult, OP.add))
            if dc == DC - 1:
                inc("v", i)
                v_conv[0] = c["v"]

        for sc in range(SC):
            ph3_stageA(sc)
            ph3_stageB(sc)
            if sc >= 1:
                ph3_stageC(sc - 1)
                ph3_stageD(sc - 1)
            if sc >= 8:
                conv0_part(sc - 8)
            if sc >= 3:
                ph3_back(sc - 3)
            if upto == 31 and sc == 3:
                tapstop()
        ph3_stageC(SC - 1)
        ph3_stageD(SC - 1)
        ph3_back(SC - 3)
        ph3_back(SC - 2)
        ph3_back(SC - 1)
        LSY3_END = c["lsy"]
        PH3_V = c["v"]
        if upto == 3:
            tapstop()

        LD_PWT0 = c["ldg"]
        pwt_pos = {}
        for pi, wc in enumerate((0, 2, 4, 6, 1, 3, 5, 7)):
            if G and pi == 0:
                wait("t", t_wo[7])
            if G and pi == 4:
                wait("t", t_wo[SC - 1])
            i = g_(G, lambda wc=wc: eng.dma_start(
                out=sb["pwt_sb"][:, 1024 * wc:1024 * wc + 1024],
                in_=dr["pwt"][128 * wc:128 * wc + 128, :]))
            inc("ldg", i, 16)
            pwt_pos[wc] = c["ldg"]
        LD_PWT = c["ldg"]

        cacc = sb["trash"].bitcast(F32)

        def conv_group(n4):
            lo = 512 * n4
            if V:
                wait("s", s_evac[(4 * n4 + 3, 1)])
            i = None
            for dc in range(DC):
                y0 = YTS * dc + lo
                vdrain()
                i = g_(V, lambda y0=y0, dc=dc: eng.tensor_scalar(
                    cacc[:, :], sb["yt"][:, y0:y0 + 512],
                    sb["dwc"][:, 3 * dc:3 * dc + 1], sb["dwbc"][:, dc:dc + 1],
                    OP.mult, OP.add))
                vdrain()
                i = g_(V, lambda y0=y0, dc=dc: eng.scalar_tensor_tensor(
                    cacc[:, :], sb["yt"][:, y0 + 1:y0 + 1 + 512],
                    sb["dwc"][:, 3 * dc + 1:3 * dc + 2], cacc[:, :],
                    OP.mult, OP.add))
                vdrain()
                i = g_(V, lambda y0=y0, dc=dc, lo=lo: eng.scalar_tensor_tensor(
                    xct_sl(dc, lo, 512),
                    sb["yt"][:, y0 + 2:y0 + 2 + 512],
                    sb["dwc"][:, 3 * dc + 2:3 * dc + 3], cacc[:, :],
                    OP.mult, OP.add))
            inc("v", i)
            v_conv[n4] = c["v"]

        if upto == 41:
            conv_group(1); conv_group(2); conv_group(3)
            tapstop()

        s4 = c["s"]
        for n4 in range(4):
            if n4 + 1 < 4:
                conv_group(n4 + 1)
            for ec in range(DC):
                idx = 8 * n4 + ec
                ph = idx % 2
                ps = sb["Lp"][:, 512 * ph:512 * ph + 512]
                if T:
                    if ec == 0:
                        wait("v", v_conv[n4])
                    if idx >= 2:
                        wait("s", s4 + idx - 1)
                last = None
                for dc in range(DC):
                    if T and n4 == 0:
                        wait("ldg", pwt_pos[dc])
                    last = g_(T, lambda dc=dc, ec=ec, n4=n4, ps=ps: eng.matmul(
                        ps,
                        sb["pwt_sb"][:, 1024 * dc + 128 * ec:1024 * dc + 128 * ec + 128],
                        xct_sl(dc, 512 * n4, 512),
                        start=(dc == 0), stop=(dc == DC - 1)))
                inc("t", last)
                if SS:
                    wait("t", c["t"])
                i = g_(SS, lambda ec=ec, n4=n4, ps=ps: eng.activation(
                    sb["xc2t"][:, 2048 * ec + 512 * n4:2048 * ec + 512 * n4 + 512],
                    ps, AF.Silu, bias=sb["pwbc"][:, ec:ec + 1]))
                inc("s", i)
                if V:
                    wait("s", c["s"])
                i = g_(V, lambda ec=ec, n4=n4: eng.tensor_tensor(
                    sb["x1t"][:, 2048 * ec + 512 * n4:2048 * ec + 512 * n4 + 512],
                    sb["xc2t"][:, 2048 * ec + 512 * n4:2048 * ec + 512 * n4 + 512],
                    sb["yt"][:, YTS * ec + 2 + 512 * n4:YTS * ec + 2 + 512 * n4 + 512],
                    OP.add))
                inc("v", i)
                if ec == DC - 1:
                    v_x1t[n4] = c["v"]
        PW_V = c["v"]
        PW_S = c["s"]
        t_pwend = c["t"]
        if upto == 42:
            tapstop()

        if G:
            wait("t", t_pwend)
        i = g_(G, lambda: wload(sb["wet0"], dr["ewt"][1], D))
        inc("wet", i, 16)

        if V:
            wait("t", t_pwend)
        for pc in range(4):
            i = g_(V, lambda pc=pc: eng.memset(sb["combP"][pc][:, :], 0.0))
            if pc == 3:
                inc("v", i)
        CB0_V = c["v"]

        rtr_inline = (upto == 4)
        T5R = c["t"]
        S5R = c["s"]
        v_sub = {}

        def comb_sl(sc):
            return sb["combP"][sc // 4][:, 1024 * (sc % 4):1024 * (sc % 4) + 1024]

        def router_mm(sc):
            ph = sc % 2
            ps = (sb["S0"] if ph == 0 else sb["S1"])[:, 0:8]
            if T:
                if sc == 0:
                    wait("v", PW_V)
                if sc >= 2:
                    wait("v", CB0_V + 3 * (sc - 2) + 1)
            last = None
            for dc in range(DC):
                last = g_(T, lambda dc=dc, sc=sc, ps=ps: eng.matmul(
                    ps,
                    sb["x1t"][:, 2048 * dc + 128 * sc:2048 * dc + 128 * sc + 128],
                    sb["rwt_sb"][:, 8 * dc:8 * dc + 8],
                    start=(dc == 0), stop=(dc == DC - 1)))
            inc("t", last)

        s4r = c["s"]
        for sc in range(SC):
            ph = sc % 2
            ps = (sb["S0"] if ph == 0 else sb["S1"])[:, 0:8]
            if rtr_inline:
                router_mm(sc)
            if V:
                wait("t", T5R + sc + 1)
            i = g_(V, lambda ps=ps: eng.tensor_tensor(
                sb["rlg"][:, :], ps, sb["rbb"][:, :], OP.add))
            inc("v", i)
            vdrain()
            if RSAFE & 1:
                i = g_(V, lambda: eng.tensor_copy(sb["rmx"][:, :], sb["rlg"][:, :]))
            else:
                i = g_(V, lambda: eng.max(sb["rmx"][:, :], sb["rlg"][:, :]))
            vdrain()
            i = g_(V, lambda: eng.tensor_tensor(
                sb["rc"][:, 8:9], sb["rmx"][:, 0:1], sb["rmx"][:, 1:2], OP.subtract))
            inc("v", i)
            v_sub[sc] = c["v"]
            if rtr_inline:
                if SS:
                    wait("v", v_sub[sc])
                i = g_(SS, lambda: eng.activation(
                    sb["rc"][:, 9:10], sb["rc"][:, 8:9],
                    AF.Identity if (RSAFE & 4) else AF.Sigmoid))
                inc("s", i)
            if V:
                wait("s", c["s"] if rtr_inline else S5R + sc + 1)
            i = g_(V, lambda: eng.tensor_scalar(
                sb["rc"][:, 10:11], sb["rc"][:, 9:10], -1.0, 1.0, OP.mult, OP.add))
            if RSAFE & 2:
                i = g_(V, lambda: eng.tensor_copy(sb["ris"][:, 0:8], sb["rlg"][:, :]))
                i = g_(V, lambda: eng.tensor_copy(sb["ris"][:, 8:16], sb["rlg"][:, :]))
            else:
                i = g_(V, lambda: eng.tensor_scalar(
                    sb["ris"][:, 0:8], sb["rlg"][:, :], sb["rmx"][:, 0:1],
                    sb["rc"][:, 9:10], OP.is_equal, OP.mult))
                i = g_(V, lambda: eng.tensor_scalar(
                    sb["ris"][:, 8:16], sb["rlg"][:, :], sb["rmx"][:, 1:2],
                    sb["rc"][:, 10:11], OP.is_equal, OP.mult))
            vdrain()
            i = g_(V, lambda sc=sc: eng.tensor_tensor(
                sb["rw"][:, 8 * sc:8 * sc + 8], sb["ris"][:, 0:8],
                sb["ris"][:, 8:16], OP.add))
            inc("v", i)
            vdrain()
            if not rtr_inline:
                if V:
                    wait("yes", sc + 1)
                i = g_(V, lambda sc=sc: eng.scalar_tensor_tensor(
                    comb_sl(sc),
                    sb["zbuf"][sc % 2][:, :], sb["rw"][:, 8 * sc:8 * sc + 1],
                    comb_sl(sc), OP.mult, OP.add))
                inc("yev", i)
        if upto == 4:
            tapstop()

        for e in range(E):
            for sc in range(SC):
                it = e * SC + sc
                ph = it % 2
                ps = sb["Lp"][:, 1024 * ph:1024 * ph + 1024]
                if T:
                    if it == 0:
                        wait("s", PW_S)
                        wait("ldg", LD_SM2)
                    if e == 0 and sc % 4 == 0:
                        wait("v", v_x1t[sc // 4])
                    if sc == 0:
                        wait("wet", 16 * (e + 1))
                    if it >= 2:
                        wait("yes", it - 1)
                last = None
                for n2 in range(2):
                    last = g_(T, lambda e=e, n2=n2, ps=ps: eng.matmul(
                        ps[:, 512 * n2:512 * n2 + 512],
                        sb["esel"][:, 128 * e:128 * e + 128],
                        sb["ebr2"][:, 512 * n2:512 * n2 + 512],
                        start=True, stop=False, skip_group_check=True))
                    for dc in range(DC):
                        last = g_(T, lambda e=e, dc=dc, sc=sc, n2=n2, ps=ps: eng.matmul(
                            ps[:, 512 * n2:512 * n2 + 512],
                            sb["x1t"][:, 2048 * dc + 128 * sc:2048 * dc + 128 * sc + 128],
                            sb["wet"][e % 2][:, 1024 * dc + 512 * n2:
                                             1024 * dc + 512 * n2 + 512],
                            start=False, stop=(dc == DC - 1), skip_group_check=True))
                inc("yet", last)
                if e == 0 and not rtr_inline:
                    router_mm(sc)
                    if SS:
                        wait("v", v_sub[sc])
                    i = g_(SS, lambda: eng.activation(
                        sb["rc"][:, 9:10], sb["rc"][:, 8:9],
                        AF.Identity if (RSAFE & 4) else AF.Sigmoid))
                    inc("s", i)
                if SS:
                    wait("yet", c["yet"])
                    if it >= 2:
                        wait("yev", it - 1)
                i = g_(SS, lambda ph=ph, ps=ps: eng.activation(
                    sb["zbuf"][ph][:, :], ps, AF.Silu))
                inc("yes", i)
                if e >= 1 or rtr_inline:
                    if V:
                        wait("yes", c["yes"])
                    i = g_(V, lambda e=e, sc=sc, ph=ph: eng.scalar_tensor_tensor(
                        comb_sl(sc),
                        sb["zbuf"][ph][:, :], sb["rw"][:, 8 * sc + e:8 * sc + e + 1],
                        comb_sl(sc), OP.mult, OP.add))
                    inc("yev", i)
            if e + 2 < E:
                if G:
                    wait("yet", 16 * (e + 1))
                i = g_(G, lambda e=e: wload(sb["wet"][e % 2], dr["ewt"][e + 2], D))
                inc("wet", i, 16)
        EXP_V = c["yev"]

        if upto == 5:
            tapstop()
        LD_OWT0 = c["ldg"]
        for wc in range(DC):
            if G and wc == 0:
                wait("yet", 16 * E)
            i = g_(G, lambda wc=wc: eng.dma_start(
                out=sb["owt_sb"][:, 1024 * wc:1024 * wc + 1024],
                in_=dr["owt"][128 * wc:128 * wc + 128, :]))
            inc("ldg", i, 16)
        LD_OWT = c["ldg"]

        t6 = c["t"]
        for sc in range(SC):
            if V:
                if sc == 0:
                    wait("yev", EXP_V)
                if sc >= 2:
                    wait("t", t6 + 2 * (sc - 2) + 2)
            i = g_(V, lambda sc=sc: eng.tensor_copy(
                sb["cbf"][sc % 2][:, :], comb_sl(sc)))
            inc("v", i)
            for half in range(2):
                ps = sb["CpB"][:, 1024 * half:1024 * half + 512]
                if T:
                    wait("v", c["v"])
                lastt = None
                for q in range(4):
                    dc = 4 * half + q
                    lastt = g_(T, lambda sc=sc, dc=dc, q=q, ps=ps: eng.transpose(
                        ps[:, 128 * q:128 * q + 128],
                        sb["cbf"][sc % 2][:, 128 * dc:128 * dc + 128],
                        sb["ident"][:, :]))
                inc("t", lastt)
                if V:
                    wait("t", c["t"])
                i = g_(V, lambda sc=sc, half=half, ps=ps: eng.tensor_copy(
                    sb["combT"].rearrange("p (dc s) -> p dc s", dc=DC)
                    [:, 4 * half:4 * half + 4, 128 * sc:128 * sc + 128],
                    ps.rearrange("p (q r) -> p q r", q=4)))
                inc("v", i)
        CBT_V = c["v"]

        s6 = c["s"]
        v6 = c["v"]
        for sc in range(SC):
            ph = sc % 2
            psM = sb["Lp"][:, 1024 * ph:1024 * ph + 1024]
            hp = sb["hpf"][ph]
            rcb = 16 + 4 * ph
            if G:
                if sc == 0:
                    wait("lsy", LSY3_END)
                if sc < 2:
                    wait("v", CBT_V)
                else:
                    wait("s", s6 + 3 * (sc - 2) + 3)
            i = g_(G, lambda hp=hp, sc=sc: eng.dma_start(
                out=hp[:, :], in_=yspill[128 * sc:128 * sc + 128, :]))
            inc("ldy", i, 16)
            if T:
                if sc == 0:
                    wait("v", CBT_V)
                if sc >= 2:
                    wait("v", v6 + 3 * (sc - 2) + 1)
            last = None
            for n2 in range(2):
                last = g_(T, lambda n2=n2, psM=psM: eng.matmul(
                    psM[:, 512 * n2:512 * n2 + 512], sb["ones1"][:, :],
                    sb["obr_sb"][0:1, 512 * n2:512 * n2 + 512],
                    start=True, stop=False, skip_group_check=True))
                for dc in range(DC):
                    if T and sc == 0 and n2 == 0:
                        wait("ldg", LD_OWT0 + 16 * (dc + 1))
                    last = g_(T, lambda dc=dc, sc=sc, n2=n2, psM=psM: eng.matmul(
                        psM[:, 512 * n2:512 * n2 + 512],
                        sb["combT"][:, 2048 * dc + 128 * sc:2048 * dc + 128 * sc + 128],
                        sb["owt_sb"][:, 1024 * dc + 512 * n2:1024 * dc + 512 * n2 + 512],
                        start=False, stop=(dc == DC - 1), skip_group_check=True))
            for dc in range(DC):
                last = g_(T, lambda dc=dc, sc=sc, psM=psM: eng.matmul(
                    psM[:, 128 * dc:128 * dc + 128],
                    sb["xc2t"][:, 2048 * dc + 128 * sc:2048 * dc + 128 * sc + 128],
                    sb["ident"][:, :],
                    start=False, stop=(dc == DC - 1), skip_group_check=True))
            inc("t", last)
            if V:
                wait("t", c["t"])
                wait("ldy", c["ldy"])
            i = g_(V, lambda hp=hp, psM=psM: eng.scalar_tensor_tensor(
                hp[:, :], psM, 1.0, hp[:, :], OP.mult, OP.add))
            inc("v", i)
            if SS:
                wait("v", c["v"])
            i = g_(SS, lambda hp=hp, rcb=rcb: eng.activation(
                sb["trash"][:, :], hp[:, :], AF.Square,
                accum_out=sb["rc"][:, rcb:rcb + 1]))
            inc("s", i)
            if V:
                wait("s", c["s"])
            i = g_(V, lambda rcb=rcb: eng.tensor_scalar(
                sb["rc"][:, rcb + 1:rcb + 2], sb["rc"][:, rcb:rcb + 1],
                1.0 / D, 1e-6, OP.mult, OP.add))
            inc("v", i)
            if SS:
                wait("v", c["v"])
            i = g_(SS, lambda rcb=rcb: eng.activation(
                sb["rc"][:, rcb + 2:rcb + 3], sb["rc"][:, rcb + 1:rcb + 2], AF.Sqrt))
            inc("s", i)
            if V:
                wait("s", c["s"])
            i = g_(V, lambda rcb=rcb: eng.reciprocal(
                sb["rc"][:, rcb + 3:rcb + 4], sb["rc"][:, rcb + 2:rcb + 3]))
            inc("v", i)
            if SS:
                wait("v", c["v"])
            i = g_(SS, lambda sc=sc, hp=hp, rcb=rcb: eng.activation(
                sb["t2bf"][:, 1024 * sc:1024 * sc + 1024], hp[:, :],
                AF.Identity, scale=sb["rc"][:, rcb + 3:rcb + 4]))
            inc("s", i)
        RMS_S = c["s"]

        v6t = c["v"]
        for sc in range(SC):
            for half in range(2):
                idx = 2 * sc + half
                ps = sb["CpB"][:, 1024 * half:1024 * half + 512]
                if T:
                    if idx == 0:
                        wait("s", RMS_S)
                    if idx >= 2:
                        wait("v", v6t + idx - 1)
                lastt = None
                for q in range(4):
                    dc = 4 * half + q
                    lastt = g_(T, lambda sc=sc, dc=dc, q=q, ps=ps: eng.transpose(
                        ps[:, 128 * q:128 * q + 128],
                        sb["t2bf"][:, 1024 * sc + 128 * dc:1024 * sc + 128 * dc + 128],
                        sb["ident"][:, :]))
                inc("t", lastt)
                if V:
                    wait("t", c["t"])
                i = g_(V, lambda sc=sc, half=half, ps=ps: eng.tensor_copy(
                    sb["t2t"].rearrange("p (dc s) -> p dc s", dc=DC)
                    [:, 4 * half:4 * half + 4, 128 * sc:128 * sc + 128],
                    ps.rearrange("p (q r) -> p q r", q=4)))
                inc("v", i)
        if V:
            wait("ldg", LD_FDT)
        last = None
        for dc in range(DC):
            last = g_(V, lambda dc=dc: eng.tensor_scalar(
                sb["fdt_sb"][:, 256 * dc:256 * dc + 256],
                sb["fdt_sb"][:, 256 * dc:256 * dc + 256],
                sb["nwc"][:, dc:dc + 1], None, OP.mult))
        inc("v", last)
        FD_V = c["v"]

        if upto == 6:
            tapstop()
        s7 = c["s"]
        for sc in range(SC):
            ph = sc % 2
            psF = (sb["S0"] if ph == 0 else sb["S1"])[:, 0:256]
            if T:
                if sc == 0:
                    wait("v", FD_V)
                if sc >= 2:
                    wait("s", s7 + sc - 1)
            last = g_(T, lambda psF=psF: eng.matmul(
                psF, sb["ones1"][:, :], sb["fdbr_sb"][0:1, :],
                start=True, stop=False, skip_group_check=True))
            for dc in range(DC):
                last = g_(T, lambda dc=dc, sc=sc, psF=psF: eng.matmul(
                    psF,
                    sb["t2t"][:, 2048 * dc + 128 * sc:2048 * dc + 128 * sc + 128],
                    sb["fdt_sb"][:, 256 * dc:256 * dc + 256],
                    start=False, stop=(dc == DC - 1), skip_group_check=True))
            inc("t", last)
            if SS:
                wait("t", c["t"])
            i = g_(SS, lambda sc=sc, psF=psF: eng.activation(
                sb["ff1"][:, 256 * sc:256 * sc + 256], psF, AF.Gelu))
            inc("s", i)
        FF1_S = c["s"]

        v7 = c["v"]
        for sc in range(SC):
            ph = sc % 2
            ps = sb["CpB"][:, 1024 * ph:1024 * ph + 256]
            if T:
                if sc == 0:
                    wait("s", FF1_S)
                if sc >= 2:
                    wait("v", v7 + sc - 1)
            lastt = None
            for q in range(2):
                lastt = g_(T, lambda sc=sc, q=q, ps=ps: eng.transpose(
                    ps[:, 128 * q:128 * q + 128],
                    sb["ff1"][:, 256 * sc + 128 * q:256 * sc + 128 * q + 128],
                    sb["ident"][:, :]))
            inc("t", lastt)
            if V:
                wait("t", c["t"])
            i = g_(V, lambda sc=sc, ps=ps: eng.tensor_copy(
                sb["ff1t"].rearrange("p (hc s) -> p hc s", hc=2)
                [:, :, 128 * sc:128 * sc + 128],
                ps.rearrange("p (q r) -> p q r", q=2)))
            inc("v", i)
        FF1T_V = c["v"]

        if V:
            wait("t", c["t"])
        lastd = None
        for dc in range(DC):
            lastd = g_(V, lambda dc=dc: eng.tensor_scalar(
                sb["nwdiag"][:, 128 * dc:128 * dc + 128], sb["ident"][:, :],
                sb["nwc"][:, dc:dc + 1], None, OP.mult))
        inc("v", lastd)
        NWD_V = c["v"]

        s7b = c["s"]
        v7b = c["v"]
        for sc in range(SC):
            ph = sc % 2
            psZ = sb["Lp"][:, 1024 * ph:1024 * ph + 1024]
            zf = sb["hpf"][ph]
            rcb = 24 + 4 * ph
            if G:
                if sc < 2:
                    wait("s", RMS_S)
                else:
                    wait("s", s7b + 2 * (sc - 2) + 2)
            i = g_(G, lambda zf=zf, sc=sc: eng.dma_start(
                out=zf[:, :], in_=yspill[128 * sc:128 * sc + 128, :]))
            inc("ldy", i, 16)
            if T:
                if sc == 0:
                    wait("v", NWD_V)
                if sc >= 2:
                    wait("v", v7b + 3 * (sc - 2) + 1)
            last = None
            for n2 in range(2):
                last = g_(T, lambda n2=n2, psZ=psZ: eng.matmul(
                    psZ[:, 512 * n2:512 * n2 + 512], sb["ones1"][:, :],
                    sb["fubr_sb"][0:1, 512 * n2:512 * n2 + 512],
                    start=True, stop=False, skip_group_check=True))
                for hcc in range(2):
                    last = g_(T, lambda hcc=hcc, sc=sc, n2=n2, psZ=psZ: eng.matmul(
                        psZ[:, 512 * n2:512 * n2 + 512],
                        sb["ff1t"][:, 2048 * hcc + 128 * sc:2048 * hcc + 128 * sc + 128],
                        sb["fut_sb"][:, 1024 * hcc + 512 * n2:1024 * hcc + 512 * n2 + 512],
                        start=False, stop=(hcc == 1), skip_group_check=True))
            for dc in range(DC):
                last = g_(T, lambda dc=dc, sc=sc, psZ=psZ: eng.matmul(
                    psZ[:, 128 * dc:128 * dc + 128],
                    sb["t2t"][:, 2048 * dc + 128 * sc:2048 * dc + 128 * sc + 128],
                    sb["nwdiag"][:, 128 * dc:128 * dc + 128],
                    start=False, stop=(dc == DC - 1), skip_group_check=True))
            inc("t", last)
            if V:
                wait("t", c["t"])
                wait("ldy", c["ldy"])
            i = g_(V, lambda zf=zf, psZ=psZ: eng.scalar_tensor_tensor(
                zf[:, :], psZ, 1.0, zf[:, :], OP.mult, OP.add))
            inc("v", i)
            vdrain()
            i = g_(V, lambda zf=zf: eng.bn_stats(sb["bns"][:, 0:6], zf[:, 0:512]))
            i = g_(V, lambda zf=zf: eng.bn_stats(sb["bns"][:, 6:12], zf[:, 512:1024]))
            vdrain()
            i = g_(V, lambda: eng.bn_aggr(sb["bnagg"][:, 0:2], sb["bns"][:, 0:12]))
            vdrain()
            i = g_(V, lambda rcb=rcb: eng.tensor_scalar(
                sb["rc"][:, rcb:rcb + 1], sb["bnagg"][:, 1:2], 1.0, 1e-5,
                OP.mult, OP.add))
            inc("v", i)
            if SS:
                wait("v", c["v"])
            i = g_(SS, lambda rcb=rcb: eng.activation(
                sb["rc"][:, rcb + 1:rcb + 2], sb["rc"][:, rcb:rcb + 1], AF.Sqrt))
            inc("s", i)
            if V:
                wait("s", c["s"])
            i = g_(V, lambda rcb=rcb: eng.reciprocal(
                sb["rc"][:, rcb + 2:rcb + 3], sb["rc"][:, rcb + 1:rcb + 2]))
            vdrain()
            i = g_(V, lambda rcb=rcb: eng.tensor_scalar(
                sb["rc"][:, rcb + 3:rcb + 4], sb["bnagg"][:, 0:1],
                sb["rc"][:, rcb + 2:rcb + 3], -1.0, OP.mult, OP.mult))
            inc("v", i)
            if SS:
                wait("v", c["v"])
                if sc >= 2:
                    wait("sto", 16 * (sc - 1))
            i = g_(SS, lambda ph=ph, zf=zf, rcb=rcb: eng.activation(
                sb["ostg"][ph][:, :], zf[:, :], AF.Identity,
                scale=sb["rc"][:, rcb + 2:rcb + 3], bias=sb["rc"][:, rcb + 3:rcb + 4]))
            inc("s", i)
            if SY:
                wait("s", c["s"])
            i = g_(SY, lambda sc=sc, ph=ph: eng.dma_start(
                out=out_h[128 * sc:128 * sc + 128, :], in_=sb["ostg"][ph][:, :]))
            inc("sto", i, 16)

        if upto == 7:
            tapstop()
        if SY:
            wait("sto", c["sto"])

    import contextlib
    with nc.Block() as block, contextlib.ExitStack() as st:
        sems = {}
        for n in SEM_NAMES:
            sems[n] = st.enter_context(nc.semaphore(n))
        ctx["sems"] = sems

        def sbuf(name, shape, dt):
            ctx[name] = st.enter_context(nc.sbuf_tensor(name, shape, dt))
            return ctx[name]

        def psum(name, shape):
            ctx[name] = st.enter_context(nc.psum_tensor(name, shape, F32))

        sbuf("ident", [128, 128], BF16)
        sbuf("ones1", [1, 128], BF16)
        sbuf("bqc", [128, 4], F32); sbuf("bkc", [128, 4], F32)
        sbuf("bvb", [128, HD], F32)
        sbuf("dwc", [128, DC * 3], F32); sbuf("dwbc", [128, DC], F32)
        sbuf("pwbc", [128, DC], F32); sbuf("rbb", [128, E], F32)
        sbuf("nwc", [128, DC], F32)
        sbuf("ebr2", [8, D], BF16)
        sbuf("esel", [8, E * 128], BF16)
        sbuf("obr_sb", [1, D], BF16)
        sbuf("bor_sb", [1, D], BF16)
        sbuf("fdbr_sb", [1, HID], BF16); sbuf("fubr_sb", [1, D], BF16)
        sbuf("rwt_sb", [128, DC * E], BF16)
        sbuf("rc", [128, 32], F32)
        sbuf("bns", [128, 24], F32); sbuf("bnagg", [128, 4], F32)
        sbuf("rlg", [128, 8], F32); sbuf("rmx", [128, 8], F32)
        sbuf("ris", [128, 16], F32)
        sbuf("rw", [128, 8 * SC], F32)
        sbuf("vt", [128, KC * DH], BF16)
        for par in range(2):
            sbuf(f"wdenA{par}", [128, 16], F32)
            sbuf(f"wdenB{par}", [128, 16], F32)
            sbuf(f"wsum{par}", [128, 16], F32)
            sbuf(f"wcol_f{par}", [128, 16], F32)
            sbuf(f"ucol_f{par}", [128, 16], F32)
            sbuf(f"wcol_b{par}", [128, 16], BF16)
            sbuf(f"ucol_b{par}", [128, 16], BF16)
        for nm in ("wdenA", "wdenB", "wsum", "wcol_f", "ucol_f",
                   "wcol_b", "ucol_b"):
            ctx[nm] = [ctx[f"{nm}0"], ctx[f"{nm}1"]]
        sbuf("yf32_0", [128, D], F32); sbuf("yf32_1", [128, D], F32)
        ctx["yf32"] = [ctx["yf32_0"], ctx["yf32_1"]]

        AR1 = sbuf("AR1", [128, 32768], BF16)
        AR2 = sbuf("AR2", [128, 32832], BF16)
        AR5 = sbuf("AR5", [128, 8192], BF16)
        AR6 = sbuf("AR6", [128, 8192], BF16)
        AR7 = sbuf("AR7", [128, 8192], BF16)

        a1 = AR1.ap(); a2 = AR2.ap(); a5 = AR5.ap(); a6 = AR6.ap(); a7 = AR7.ap()
        ctx["wq_sb"] = a1[:, 0:4096]
        ctx["wk_sb"] = a1[:, 4096:8192]
        ctx["wv_sb"] = a1[:, 8192:12288]
        ctx["Abuf"] = a1[:, 0:32768]
        ctx["convacc"] = [a1[:, 0:4096].bitcast(F32), a1[:, 0:4096].bitcast(F32)]
        ctx["t1f"] = [a1[:, 16384:18432].bitcast(F32),
                      a1[:, 18432:20480].bitcast(F32),
                      a1[:, 20480:22528].bitcast(F32),
                      a1[:, 22528:24576].bitcast(F32)]
        ctx["ybf"] = [a1[:, 28672:29696], a1[:, 29696:30720],
                      a1[:, 30720:31744], a1[:, 31744:32768]]
        ctx["cacc0"] = a1[:, 24576:25600].bitcast(F32)
        ctx["x1t"] = a1[:, 0:16384]
        ctx["xc2t"] = a1[:, 16384:32768]
        ctx["combT"] = a1[:, 0:16384]
        ctx["ff1"] = a1[:, 0:4096]
        ctx["ff1t"] = a1[:, 4096:8192]
        ctx["nwdiag"] = a1[:, 8192:9216]
        ctx["xbf"] = a2[:, 0:16384]
        ctx["xT"] = a2[:, 16384:32768]
        ctx["Tbuf"] = a2[:, 0:32768]
        ctx["yt"] = a2[:, 0:16448]
        ctx["wet1"] = a2[:, 16448:24640]
        ctx["xct_a"] = a2[:, 24640:32832]
        ctx["combP"] = None
        ctx["cbf"] = [a2[:, 16448:17472], a2[:, 17472:18496]]
        ctx["t2bf"] = a2[:, 16448:32832]
        ctx["t2t"] = a2[:, 0:16384]
        ctx["qt"] = a5[:, 0:8192]
        ctx["ot"] = a5[:, 0:8192]
        ctx["pwt_sb"] = a5[:, 0:8192]
        ctx["hpf"] = [a5[:, 0:2048].bitcast(F32), a5[:, 2048:4096].bitcast(F32)]
        ctx["ostg"] = [a5[:, 4096:6144].bitcast(F32), a5[:, 6144:8192].bitcast(F32)]
        ctx["kt"] = a6[:, 0:8192]
        ctx["xct_b"] = a6[:, 0:8192]
        ctx["wet0"] = a6[:, 0:8192]
        ctx["owt_sb"] = a6[:, 0:8192]
        ctx["vna"] = a7[:, 0:8192]
        ctx["obuf"] = a7[:, 0:8192]
        ctx["fdt_sb"] = a7[:, 0:2048]
        ctx["fut_sb"] = a7[:, 2048:4096]
        ctx["wo_sb"] = a7[:, 4096:8192]
        ctx["zbuf"] = [a7[:, 4096:5120], a7[:, 5120:6144]]
        ctx["tmpf"] = a7[:, 4096:6144].bitcast(F32)
        ctx["trash"] = a7[:, 6144:7168]
        ctx["combP"] = [a5[:, 0:8192].bitcast(F32),
                        a2[:, 0:8192].bitcast(F32),
                        a2[:, 8192:16384].bitcast(F32),
                        a2[:, 24640:32832].bitcast(F32)]
        ctx["wet"] = [ctx["wet1"], ctx["wet0"]]

        psum("Lp", [128, 2048])
        psum("Cp", [128, 1024])
        psum("S0", [128, 512])
        psum("S1", [128, 512])
        ctx["CpB"] = ctx["Cp"][:, :].bitcast(BF16)

        def _run(me, eng):
            try:
                emit(me, eng)
            except Exception as ex:
                if type(ex).__name__ != "_Stop":
                    raise

        @block.tensor
        def _(eng):
            _run("tensor", eng)

        @block.vector
        def _(eng):
            _run("vector", eng)

        @block.scalar
        def _(eng):
            _run("scalar", eng)

        @block.gpsimd
        def _(eng):
            _run("gpsimd", eng)

        @block.sync
        def _(eng):
            _run("sync", eng)

    return nc


def _prep_core_inputs(b, inp):
    f = lambda a: np.ascontiguousarray(a, dtype=np.float32)
    d = {
        "x": f(np.asarray(inp["L0_attn_x"])[b]),
        "wq": f(inp["L0_attn_Wq"]), "wk": f(inp["L0_attn_Wk"]),
        "wv": f(inp["L0_attn_Wv"]),
        "bqc": f(np.asarray(inp["L0_attn_bq"]).reshape(4, 128).T),
        "bkc": f(np.asarray(inp["L0_attn_bk"]).reshape(4, 128).T),
        "bvb": f(np.broadcast_to(np.asarray(inp["L0_attn_bv"]), (128, HD))),
        "wo": f(inp["L0_attn_Wo"]),
        "bob": f(np.broadcast_to(np.asarray(inp["L0_attn_bo"]), (128, D))),
        "bor": f(np.asarray(inp["L0_attn_bo"]).reshape(1, D)),
        "dwc": f(np.asarray(inp["L0_ffn_dw_kernel"])
                 .reshape(DC, 128, 3).transpose(1, 0, 2).reshape(128, DC * 3)),
        "dwbc": f(np.asarray(inp["L0_ffn_dw_bias"]).reshape(DC, 128).T),
        "pwt": f(np.asarray(inp["L0_ffn_pw_w"]).T),
        "pwbc": f(np.asarray(inp["L0_ffn_pw_b"]).reshape(DC, 128).T),
        "rwt": f(np.asarray(inp["L0_ffn_router_w"]).T),
        "rbb": f(np.broadcast_to(np.asarray(inp["L0_ffn_router_b"]), (128, E))),
        "ewt": f(np.asarray(inp["L0_ffn_expert_w"]).transpose(0, 2, 1)),
        "ebr": f(inp["L0_ffn_expert_b"]),
        "owt": f(np.asarray(inp["L0_ffn_out_w"]).T),
        "obr": f(np.asarray(inp["L0_ffn_out_b"]).reshape(1, D)),
        "nwc": f(np.asarray(inp["L0_ffn_norm_w"]).reshape(DC, 128).T),
        "nwb": f(np.broadcast_to(np.asarray(inp["L0_ffn_norm_w"]), (128, D))),
        "fdt": f(np.asarray(inp["L0_ffn_ff_down_w"]).T),
        "fdbr": f(np.asarray(inp["L0_ffn_ff_down_b"]).reshape(1, HID)),
        "fut": f(np.asarray(inp["L0_ffn_ff_up_w"]).T),
        "fubr": f(np.asarray(inp["L0_ffn_ff_up_b"]).reshape(1, D)),
        "ident": np.eye(128, dtype=np.float32),
        "ones1": np.ones((1, 128), np.float32),
        "esel": np.kron(np.eye(E, dtype=np.float32), np.ones((1, 128), np.float32)),
    }
    return {"i_" + k: v for k, v in d.items()}


_NC_CACHE = {}


def kernel(**inputs):
    if "prog" not in _NC_CACHE:
        _NC_CACHE["prog"] = build_program()
    nc = _NC_CACHE["prog"]
    in_maps = [_prep_core_inputs(b, inputs) for b in range(8)]
    res = run_bass_kernel_spmd(nc, in_maps, core_ids=list(range(8)))
    return np.stack(
        [np.asarray(res.results[b]["out"], np.float32) for b in range(8)], axis=0)


# version 10
# speedup vs baseline: 1.0038x; 1.0015x over previous
import numpy as np
import concourse.bass as bass
import concourse.mybir as mybir
from concourse.bass_utils import run_bass_kernel_spmd
from concourse.alu_op_type import AluOpType as OP

F32 = mybir.dt.float32
BF16 = mybir.dt.bfloat16
AF = mybir.ActivationFunctionType

S = 2048
D = 1024
H = 8
DH = 64
HD = 512
E = 8
HID = 256
SC = S // 128
DC = D // 128
KC = 16
NPASS = 39
YTS = 2056

SEM_NAMES = [
    "ldg", "ldy", "lsy", "sto", "xrv",
    "t", "v", "s",
    "snkt", "snkv", "lgt", "lgs", "avt", "avs", "vt",
    "wet", "wef", "yet", "yes", "yev",
]


RSAFE = 0
TSAFE = 1


def build_program(upto=None, taps=()):
    nc = bass.Bass("TRN2", target_bir_lowering=False, debug=False)

    dr = {}
    def din(name, shape, dt=F32):
        dr[name] = nc.dram_tensor("i_" + name, shape, dt, kind="ExternalInput").ap()
    din("x", [S, D])
    din("wq", [D, HD]); din("wk", [D, HD]); din("wv", [D, HD])
    din("bqc", [128, 4]); din("bkc", [128, 4]); din("bvb", [128, HD])
    din("wo", [HD, D]); din("bob", [128, D]); din("bor", [1, D])
    din("dwc", [128, DC * 3]); din("dwbc", [128, DC])
    din("pwt", [D, D]); din("pwbc", [128, DC])
    din("rwt", [D, E]); din("rbb", [128, E])
    din("ewt", [E, D, D])
    din("ebr", [E, D])
    din("owt", [D, D]); din("obr", [1, D])
    din("nwc", [128, DC]); din("nwb", [128, D])
    din("fdt", [D, HID]); din("fdbr", [1, HID])
    din("fut", [HID, D]); din("fubr", [1, D])
    din("ident", [128, 128])
    din("ones1", [1, 128])
    din("esel", [8, E * 128])
    yspill = nc.dram_tensor("yspill", [S, D], F32, kind="Internal").ap()
    out_h = nc.dram_tensor("out", [S, D], F32, kind="ExternalOutput").ap()
    dbg_h = {}
    for (dn, sn, shp, dt) in taps:
        dbg_h[dn] = nc.dram_tensor(dn, shp, dt, kind="ExternalOutput").ap()

    ctx = {}
    RSAFE = globals()["RSAFE"]
    TSAFE = globals()["TSAFE"]

    def emit(me, eng):
        c = {n: 0 for n in SEM_NAMES}
        sems = ctx["sems"]
        sb = ctx

        def inc(name, instr, k=1):
            c[name] += k
            if instr is not None and instr is not False:
                instr.then_inc(sems[name], k)

        def wait(name, val):
            if eng is not None:
                eng.wait_ge(sems[name], val)

        T = me == "tensor"; V = me == "vector"; SS = me == "scalar"
        G = me == "gpsimd"; SY = me == "sync"

        def g_(cond, instr_fn):
            return instr_fn() if cond else None

        def vdrain():
            if V:
                eng.drain()

        def wload(dst_ap, src_dr, o):
            return eng.dma_start(
                out=dst_ap.rearrange("p (c o) -> p c o", o=o),
                in_=src_dr.rearrange("(c p) o -> p c o", p=128))

        class _Stop(Exception):
            pass

        def tapstop():
            if SY:
                for nm2 in ("t", "v", "s", "snkv", "lgs", "avs", "yev",
                            "yes", "vt"):
                    if c[nm2] > 0:
                        wait(nm2, c[nm2])
                ns = 0
                for (dn, sn, shp, dt) in taps:
                    ap = sb[sn] if isinstance(sn, str) else sn(sb)
                    ap = ap[:, :]
                    i2 = eng.dma_start(out=dbg_h[dn][:, :] if len(shp) == 2
                                       else dbg_h[dn][:], in_=ap)
                    i2.then_inc(sems["sto"], 16)
                    ns += 16
                wait("sto", c["sto"] + ns)
            raise _Stop()

        i = g_(G, lambda: wload(sb["xbf"], dr["x"], D))
        inc("ldg", i, 16)
        i = g_(G, lambda: eng.dma_start(out=sb["ident"][:, :], in_=dr["ident"][:, :]))
        inc("ldg", i, 16)
        i = g_(G, lambda: eng.dma_start(out=sb["ones1"][:, :], in_=dr["ones1"][:, :]))
        inc("ldg", i, 16)
        for wname, dst in (("wq", "wq_sb"), ("wk", "wk_sb"), ("wv", "wv_sb")):
            i = g_(G, lambda wname=wname, dst=dst: wload(sb[dst], dr[wname], HD))
            inc("ldg", i, 16)
        for nm in ("bqc", "bkc", "bvb"):
            i = g_(G, lambda nm=nm: eng.dma_start(out=sb[nm][:, :], in_=dr[nm][:, :]))
            inc("ldg", i, 16)
        LD_ATTN = c["ldg"]

        i = g_(G, lambda: eng.dma_start(out=sb["bor_sb"][:, :], in_=dr["bor"][:, :]))
        inc("ldg", i, 16)
        for nm in ("dwc", "dwbc", "pwbc", "rbb", "nwc"):
            i = g_(G, lambda nm=nm: eng.dma_start(out=sb[nm][:, :], in_=dr[nm][:, :]))
            inc("ldg", i, 16)
        i = g_(G, lambda: wload(sb["rwt_sb"], dr["rwt"], E))
        inc("ldg", i, 16)
        i = g_(G, lambda: eng.dma_start(out=sb["ebr2"][:, :], in_=dr["ebr"][:, :]))
        inc("ldg", i, 16)
        i = g_(G, lambda: eng.dma_start(out=sb["esel"][:, :], in_=dr["esel"][:, :]))
        inc("ldg", i, 16)
        for nm, dst in (("obr", "obr_sb"), ("fdbr", "fdbr_sb"), ("fubr", "fubr_sb")):
            i = g_(G, lambda nm=nm, dst=dst: eng.dma_start(
                out=sb[dst][:, :], in_=dr[nm][:, :]))
            inc("ldg", i, 16)
        LD_SM2 = c["ldg"]

        if T:
            wait("ldg", 48)
        v0 = c["v"]
        for sc in range(SC):
            for half in range(2):
                idx = 2 * sc + half
                ps = sb["CpB"][:, 1024 * half:1024 * half + 512]
                if T and idx >= 2:
                    wait("v", v0 + idx - 1)
                lastt = None
                for q in range(4):
                    dc = 4 * half + q
                    lastt = g_(T, lambda q=q, dc=dc, sc=sc, ps=ps: eng.transpose(
                        ps[:, 128 * q:128 * q + 128],
                        sb["xbf"][:, 1024 * sc + 128 * dc:1024 * sc + 128 * dc + 128],
                        sb["ident"][:, :]))
                inc("t", lastt)
                if V:
                    wait("t", c["t"])
                i = g_(V, lambda sc=sc, half=half, ps=ps: eng.tensor_copy(
                    sb["xT"].rearrange("p (dc s) -> p dc s", dc=DC)
                    [:, 4 * half:4 * half + 4, 128 * sc:128 * sc + 128],
                    ps.rearrange("p (q r) -> p q r", q=4)))
                inc("v", i)
        XT_V = c["v"]

        if T:
            wait("ldg", LD_ATTN)
            wait("v", XT_V)
        s0 = c["s"]
        for wi, (wsb, qdst, bcol) in enumerate(
                (("wq_sb", "qt", "bqc"), ("wk_sb", "kt", "bkc"))):
            for oc in range(4):
                idx = 4 * wi + oc
                if T and idx >= 1:
                    wait("s", s0 + idx)
                last = None
                for n4 in range(4):
                    for dc in range(DC):
                        last = g_(T, lambda n4=n4, dc=dc, oc=oc, wsb=wsb: eng.matmul(
                            sb["Lp"][:, 512 * n4:512 * n4 + 512],
                            sb[wsb][:, 512 * dc + 128 * oc:512 * dc + 128 * oc + 128],
                            sb["xT"][:, 2048 * dc + 512 * n4:2048 * dc + 512 * n4 + 512],
                            start=(dc == 0), stop=(dc == DC - 1)))
                inc("t", last)
                if SS:
                    wait("t", c["t"])
                i = g_(SS, lambda oc=oc, qdst=qdst, bcol=bcol: eng.activation(
                    sb[qdst][:, 2048 * oc:2048 * oc + 2048], sb["Lp"][:, :],
                    AF.Identity, bias=sb[bcol][:, oc:oc + 1], scale=1.0))
                inc("s", i)
        vv0 = c["v"]
        for sc in range(SC):
            ph = sc % 2
            ps = sb["Cp"][:, 512 * ph:512 * ph + 512]
            if T and sc >= 2:
                wait("v", vv0 + sc - 1)
            last = None
            for dc in range(DC):
                last = g_(T, lambda dc=dc, sc=sc, ps=ps: eng.matmul(
                    ps,
                    sb["xT"][:, 2048 * dc + 128 * sc:2048 * dc + 128 * sc + 128],
                    sb["wv_sb"][:, 512 * dc:512 * dc + 512],
                    start=(dc == 0), stop=(dc == DC - 1)))
            inc("t", last)
            if V:
                wait("t", c["t"])
            i = g_(V, lambda sc=sc, ps=ps: eng.tensor_tensor(
                sb["vna"][:, 512 * sc:512 * sc + 512], ps, sb["bvb"][:, :], OP.add))
            inc("v", i)
        PROJ_V = c["v"]; PROJ_S = c["s"]
        if upto == 1:
            tapstop()

        head_lgs = {}
        def snk_groups(parity):
            if parity == 0:
                return [sb["Cp"][:, 0:4], sb["Cp"][:, 512:516],
                        sb["S0"][:, 0:4], sb["S1"][:, 0:4]]
            return [sb["Lp"][:, 0:4], sb["Lp"][:, 512:516],
                    sb["Lp"][:, 1024:1028], sb["Lp"][:, 1536:1540]]

        def emit_logits2(h, transposed, av_h=None, p0_h=None):
            oc = h // 2
            p0 = 64 * (h % 2)
            src_l, src_r = ("kt", "qt") if transposed else ("qt", "kt")
            dstbuf = "Tbuf" if transposed else "Abuf"
            LT = c["lgt"]; LS = c["lgs"]
            if av_h is not None:
                AVB = c["avs"]
                hpav = av_h % 2
            if p0_h is not None:
                hp0 = p0_h % 2
                groups0 = snk_groups(0)
                vbase0 = c["snkv"]
                if T:
                    wait("avs", c["avs"])
            last0 = None
            for qc in range(KC):
                if av_h is not None:
                    phav = qc % 2
                    psav = sb["Cp"][:, 512 * phav:512 * phav + 64]
                    if T:
                        if qc == 0:
                            wait("vt", c["vt"])
                        if qc >= 2:
                            wait("avs", AVB + qc - 1)
                    lastav = None
                    for kc in range(KC):
                        lastav = g_(T, lambda kc=kc, qc=qc, psav=psav: eng.matmul(
                            psav,
                            sb["Tbuf"][:, 2048 * kc + 128 * qc:2048 * kc + 128 * qc + 128],
                            sb["vt"][:, 64 * kc:64 * kc + 64],
                            start=(kc == 0), stop=(kc == KC - 1)))
                    inc("avt", lastav)
                    if V:
                        wait("avt", c["avt"])
                    i = g_(V, lambda av_h=av_h, qc=qc, hpav=hpav, psav=psav:
                           eng.tensor_scalar(
                               sb["obuf"][:, 512 * qc + 64 * av_h:512 * qc + 64 * av_h + 64],
                               psav, sb["wcol_f"][hpav][:, qc:qc + 1], None, OP.mult))
                    inc("avs", i)
                if T and qc >= 1:
                    wait("lgs", LS + 2 * (qc - 1) + 1)
                lastmm = None
                for n4 in (0, 1):
                    lastmm = g_(T, lambda n4=n4, qc=qc, oc=oc, p0=p0,
                                src_l=src_l, src_r=src_r: eng.matmul(
                        sb["Lp"][:, 512 * n4:512 * n4 + 512],
                        sb[src_l][p0:p0 + 64,
                                  2048 * oc + 128 * qc:2048 * oc + 128 * qc + 128],
                        sb[src_r][p0:p0 + 64,
                                  2048 * oc + 512 * n4:2048 * oc + 512 * n4 + 512],
                        start=True, stop=True))
                inc("lgt", lastmm)
                if T and qc >= 1:
                    wait("lgs", LS + 2 * (qc - 1) + 2)
                for n4 in (2, 3):
                    lastmm = g_(T, lambda n4=n4, qc=qc, oc=oc, p0=p0,
                                src_l=src_l, src_r=src_r: eng.matmul(
                        sb["Lp"][:, 512 * n4:512 * n4 + 512],
                        sb[src_l][p0:p0 + 64,
                                  2048 * oc + 128 * qc:2048 * oc + 128 * qc + 128],
                        sb[src_r][p0:p0 + 64,
                                  2048 * oc + 512 * n4:2048 * oc + 512 * n4 + 512],
                        start=True, stop=True))
                inc("lgt", lastmm)
                if SS:
                    wait("lgt", LT + 2 * qc + 1)
                    if transposed and qc == 0 and c["avt"] > 0:
                        wait("avt", c["avt"])
                i = g_(SS, lambda h=h, qc=qc, transposed=transposed, dstbuf=dstbuf:
                       eng.activation(
                           sb[dstbuf][:, 2048 * qc:2048 * qc + 1024],
                           sb["Lp"][:, 0:1024], AF.Exp, scale=0.125,
                           accum_out=(None if transposed
                                      else sb["wdenA"][h % 2][:, qc:qc + 1])))
                inc("lgs", i)
                if SS:
                    wait("lgt", LT + 2 * qc + 2)
                i = g_(SS, lambda h=h, qc=qc, transposed=transposed, dstbuf=dstbuf:
                       eng.activation(
                           sb[dstbuf][:, 2048 * qc + 1024:2048 * qc + 2048],
                           sb["Lp"][:, 1024:2048], AF.Exp, scale=0.125,
                           accum_out=(None if transposed
                                      else sb["wdenB"][h % 2][:, qc:qc + 1])))
                inc("lgs", i)
                if p0_h is not None:
                    j = qc
                    gidx = j // 4
                    reg = groups0[gidx]
                    for cc in range(16):
                        if T and j == 0 and cc % 4 == 0:
                            wait("snkv", vbase0 - 4 + (cc // 4) + 1)
                        last0 = g_(T, lambda j=j, cc=cc, reg=reg, hp0=hp0: eng.matmul(
                            reg[:, j % 4:j % 4 + 1],
                            sb["Abuf"][:, 2048 * cc + 128 * j:2048 * cc + 128 * j + 128],
                            sb["wcol_b"][hp0][:, cc:cc + 1],
                            start=(j % 4 == 0 and cc == 0),
                            stop=(j % 4 == 3 and cc == 15),
                            skip_group_check=True))
                    if j % 4 == 3:
                        inc("snkt", last0)
                        if V:
                            wait("snkt", c["snkt"])
                        i = g_(V, lambda hp0=hp0, gidx=gidx, reg=reg: eng.reciprocal(
                            sb["ucol_f"][hp0][:, 4 * gidx:4 * gidx + 4], reg[:, 0:4]))
                        if gidx >= 1:
                            i2 = g_(V, lambda hp0=hp0, g2=gidx - 1: eng.tensor_copy(
                                sb["ucol_b"][hp0][:, 4 * g2:4 * g2 + 4],
                                sb["ucol_f"][hp0][:, 4 * g2:4 * g2 + 4]))
                            inc("snkv", i2)
            if p0_h is not None:
                i3 = g_(V, lambda hp0=hp0: eng.tensor_copy(
                    sb["ucol_b"][hp0][:, 12:16], sb["ucol_f"][hp0][:, 12:16]))
                inc("snkv", i3)

        def emit_w1(h):
            hp = h % 2
            base = c["lgs"]
            def _a(g):
                return g_(V, lambda: eng.tensor_tensor(
                    sb["wsum"][hp][:, 4 * g:4 * g + 4],
                    sb["wdenA"][hp][:, 4 * g:4 * g + 4],
                    sb["wdenB"][hp][:, 4 * g:4 * g + 4], OP.add))
            def _r(g):
                return g_(V, lambda: eng.reciprocal(
                    sb["wcol_f"][hp][:, 4 * g:4 * g + 4],
                    sb["wsum"][hp][:, 4 * g:4 * g + 4]))
            def _c(g):
                return g_(V, lambda: eng.tensor_copy(
                    sb["wcol_b"][hp][:, 4 * g:4 * g + 4],
                    sb["wcol_f"][hp][:, 4 * g:4 * g + 4]))
            if V:
                wait("lgs", base - 32 + 8)
            _a(0)
            if V:
                wait("lgs", base - 32 + 16)
            _a(1)
            _r(0)
            if V:
                wait("lgs", base - 32 + 24)
            _a(2)
            _r(1)
            inc("snkv", _c(0))
            if V:
                wait("lgs", base - 32 + 32)
            _a(3)
            _r(2)
            inc("snkv", _c(1))
            _r(3)
            inc("snkv", _c(2))
            inc("snkv", _c(3))

        def emit_pass(h, p):
            hp = h % 2
            col = (p % 2 == 0)
            mat = "Abuf" if col else "Tbuf"
            src_b = "wcol_b" if col else "ucol_b"
            dst_f = "ucol_f" if col else "wcol_f"
            dst_b = "ucol_b" if col else "wcol_b"
            groups = snk_groups(p % 2)
            vbase = c["snkv"]
            if T and p == 1:
                wait("lgs", head_lgs[h])
            last = None
            for j in range(16):
                gidx = j // 4
                reg = groups[gidx]
                for cc in range(16):
                    if T and j == 0 and cc % 4 == 0:
                        wait("snkv", vbase - 4 + (cc // 4) + 1)
                    last = g_(T, lambda j=j, cc=cc, reg=reg, mat=mat,
                              src_b=src_b, hp=hp, gidx=gidx: eng.matmul(
                        reg[:, j % 4:j % 4 + 1],
                        sb[mat][:, 2048 * cc + 128 * j:2048 * cc + 128 * j + 128],
                        sb[src_b][hp][:, cc:cc + 1],
                        start=(j % 4 == 0 and cc == 0),
                        stop=(j % 4 == 3 and cc == 15),
                        skip_group_check=True))
                if j % 4 == 3:
                    inc("snkt", last)
                    if V:
                        wait("snkt", c["snkt"])
                    i = g_(V, lambda hp=hp, gidx=gidx, reg=reg, dst_f=dst_f:
                           eng.reciprocal(
                               sb[dst_f][hp][:, 4 * gidx:4 * gidx + 4],
                               reg[:, 0:4]))
                    if gidx >= 1:
                        i2 = g_(V, lambda hp=hp, g2=gidx - 1, dst_f=dst_f,
                                dst_b=dst_b: eng.tensor_copy(
                                    sb[dst_b][hp][:, 4 * g2:4 * g2 + 4],
                                    sb[dst_f][hp][:, 4 * g2:4 * g2 + 4]))
                        inc("snkv", i2)
            i3 = g_(V, lambda hp=hp, dst_f=dst_f, dst_b=dst_b: eng.tensor_copy(
                sb[dst_b][hp][:, 12:16], sb[dst_f][hp][:, 12:16]))
            inc("snkv", i3)

        def emit_vtilde(h):
            hp = h % 2
            last = None
            for kc in range(KC):
                last = g_(V, lambda kc=kc, h=h, hp=hp: eng.tensor_scalar(
                    sb["vt"][:, 64 * kc:64 * kc + 64],
                    sb["vna"][:, 512 * kc + 64 * h:512 * kc + 64 * h + 64],
                    sb["ucol_f"][hp][:, kc:kc + 1], None, OP.mult))
            inc("vt", last)

        def emit_av(h):
            hp = h % 2
            av0 = c["avs"]
            for qc in range(KC):
                ph = qc % 2
                ps = sb["Cp"][:, 512 * ph:512 * ph + 64]
                if T:
                    if qc == 0:
                        wait("vt", c["vt"])
                    if qc >= 2:
                        wait("avs", av0 + qc - 1)
                last = None
                for kc in range(KC):
                    last = g_(T, lambda kc=kc, qc=qc, ps=ps: eng.matmul(
                        ps,
                        sb["Tbuf"][:, 2048 * kc + 128 * qc:2048 * kc + 128 * qc + 128],
                        sb["vt"][:, 64 * kc:64 * kc + 64],
                        start=(kc == 0), stop=(kc == KC - 1)))
                inc("avt", last)
                if V:
                    wait("avt", c["avt"])
                i = g_(V, lambda h=h, qc=qc, hp=hp, ps=ps: eng.tensor_scalar(
                    sb["obuf"][:, 512 * qc + 64 * h:512 * qc + 64 * h + 64],
                    ps, sb["wcol_f"][hp][:, qc:qc + 1], None, OP.mult))
                inc("avs", i)

        if T:
            wait("s", PROJ_S)
            wait("v", PROJ_V)
        emit_logits2(0, False)
        emit_w1(0)
        emit_logits2(0, True, p0_h=0)
        head_lgs[0] = c["lgs"]
        for h in range(H):
            for p in range(1, NPASS):
                emit_pass(h, p)
            if upto == 21 and h == 0:
                tapstop()
            if upto == 210 + h:
                tapstop()
            emit_vtilde(h)
            if h + 1 >= H:
                emit_av(h)
            if upto == 2 and h == 0:
                emit_av(h)
                tapstop()
            if h + 1 < H:
                if TSAFE & 1:
                    emit_av(h)
                if SS:
                    wait("snkt", c["snkt"])
                emit_logits2(h + 1, False,
                             av_h=(None if (TSAFE & 1) else h))
                emit_w1(h + 1)
                emit_logits2(h + 1, True,
                             p0_h=(None if (TSAFE & 2) else h + 1))
                head_lgs[h + 1] = c["lgs"]
                if TSAFE & 2:
                    emit_pass(h + 1, 0)
                if upto == 220 + h:
                    tapstop()

        if T:
            wait("avs", c["avs"])
            wait("ldg", LD_ATTN)
        v3 = c["v"]
        idx3 = 0
        for gq in (2, 3, 1, 0):
            for hc in range(4):
                ph = idx3 % 2
                ps = sb["CpB"][:, 1024 * ph:1024 * ph + 512]
                if T and idx3 >= 2:
                    wait("v", v3 + idx3 - 1)
                lastt = None
                for q in range(4):
                    sc = 4 * gq + q
                    lastt = g_(T, lambda sc=sc, hc=hc, q=q, ps=ps: eng.transpose(
                        ps[:, 128 * q:128 * q + 128],
                        sb["obuf"][:, 512 * sc + 128 * hc:512 * sc + 128 * hc + 128],
                        sb["ident"][:, :]))
                inc("t", lastt)
                if V:
                    wait("t", c["t"])
                i = g_(V, lambda hc=hc, gq=gq, ps=ps: eng.tensor_copy(
                    sb["ot"][:, 2048 * hc + 512 * gq:2048 * hc + 512 * gq + 512], ps))
                inc("v", i)
                idx3 += 1
        OT_V = c["v"]

        for wc in range(4):
            if G and wc == 0:
                wait("v", v3 + 4)
            if G and wc == 2:
                wait("v", v3 + 8)
            i = g_(G, lambda wc=wc: eng.dma_start(
                out=sb["wo_sb"][:, 1024 * wc:1024 * wc + 1024],
                in_=dr["wo"][128 * wc:128 * wc + 128, :]))
            inc("ldg", i, 16)
        LD_WO = c["ldg"]
        if G:
            wait("v", v3 + 12)
        i = g_(G, lambda: wload(sb["fut_sb"], dr["fut"], D))
        inc("ldg", i, 16)
        if G:
            wait("v", OT_V)
        i = g_(G, lambda: wload(sb["fdt_sb"], dr["fdt"], HID))
        inc("ldg", i, 16)
        LD_FDT = c["ldg"]
        if G:
            wait("avt", 16 * H)
        i = g_(G, lambda: wload(sb["wet1"], dr["ewt"][0], D))
        inc("wet", i, 16)
        if upto == 29:
            tapstop()

        lsy3 = c["lsy"]
        s_ybf = {}; s_evac = {}; s_sqrt = {}
        v_stt = {}; v_eps = {}; v_mb = {}; t_wo = {}
        v_conv = {}; v_x1t = {}

        def xct_sl(dc, lo, ln):
            buf = sb["xct_a"] if dc < 4 else sb["xct_b"]
            base = 2048 * (dc % 4)
            return buf[:, base + lo:base + lo + ln]

        if V:
            wait("avt", 16 * H)
        i = g_(V, lambda: eng.memset(
            sb["yt"].rearrange("p (dc s) -> p dc s", s=YTS)[:, :, 0:2], 0.0))
        inc("v", i)
        vdrain()

        def ph3_stageA(sc):
            ph = sc % 2
            psW = sb["Lp"][:, 1024 * ph:1024 * ph + 1024]
            t1 = sb["t1f"][sc % 4]
            if SY and sc == 0:
                wait("snkt", c["snkt"])
            if SY and sc >= 4:
                wait("s", s_ybf[sc - 4])
            i = g_(SY, lambda sc=sc, t1=t1: eng.dma_start(
                out=t1[:, :], in_=dr["x"][128 * sc:128 * sc + 128, :]))
            inc("lsy", i, 16)
            if T:
                if sc == 0:
                    wait("v", OT_V)
                    wait("ldg", LD_WO)
                if sc >= 2:
                    wait("v", v_stt[sc - 2])
            last = None
            for n2 in range(2):
                last = g_(T, lambda n2=n2, psW=psW: eng.matmul(
                    psW[:, 512 * n2:512 * n2 + 512], sb["ones1"][:, :],
                    sb["bor_sb"][0:1, 512 * n2:512 * n2 + 512],
                    start=True, stop=False, skip_group_check=True))
                for hcc in range(4):
                    last = g_(T, lambda n2=n2, hcc=hcc, sc=sc, psW=psW: eng.matmul(
                        psW[:, 512 * n2:512 * n2 + 512],
                        sb["ot"][:, 2048 * hcc + 128 * sc:2048 * hcc + 128 * sc + 128],
                        sb["wo_sb"][:, 1024 * hcc + 512 * n2:1024 * hcc + 512 * n2 + 512],
                        start=False, stop=(hcc == 3), skip_group_check=True))
            inc("t", last)
            t_wo[sc] = c["t"]
            rcb = 4 * ph
            if V:
                wait("t", t_wo[sc])
                wait("lsy", lsy3 + (16 if sc == 0 else 32 * sc))
            i = g_(V, lambda t1=t1, psW=psW: eng.scalar_tensor_tensor(
                t1[:, :], psW, 1.0, t1[:, :], OP.mult, OP.add))
            inc("v", i)
            v_stt[sc] = c["v"]
            vdrain()
            i = g_(V, lambda t1=t1, ph=ph: eng.bn_stats(
                sb["bns"][:, 12 * ph:12 * ph + 6], t1[:, 0:512]))
            i = g_(V, lambda t1=t1, ph=ph: eng.bn_stats(
                sb["bns"][:, 12 * ph + 6:12 * ph + 12], t1[:, 512:1024]))
            vdrain()
            i = g_(V, lambda ph=ph: eng.bn_aggr(
                sb["bnagg"][:, 2 * ph:2 * ph + 2],
                sb["bns"][:, 12 * ph:12 * ph + 12]))
            vdrain()
            i = g_(V, lambda rcb=rcb, ph=ph: eng.tensor_scalar(
                sb["rc"][:, rcb:rcb + 1], sb["bnagg"][:, 2 * ph + 1:2 * ph + 2],
                1.0, 1e-5, OP.mult, OP.add))
            inc("v", i)
            v_eps[sc] = c["v"]

        def ph3_stageB(sc):
            rcb = 4 * (sc % 2)
            if SS:
                wait("v", v_eps[sc])
            i = g_(SS, lambda rcb=rcb: eng.activation(
                sb["rc"][:, rcb + 1:rcb + 2], sb["rc"][:, rcb:rcb + 1], AF.Sqrt))
            inc("s", i)
            s_sqrt[sc] = c["s"]

        def ph3_stageC(sc):
            ph = sc % 2
            rcb = 4 * ph
            if V:
                wait("s", s_sqrt[sc])
            i = g_(V, lambda rcb=rcb: eng.reciprocal(
                sb["rc"][:, rcb + 2:rcb + 3], sb["rc"][:, rcb + 1:rcb + 2]))
            vdrain()
            i = g_(V, lambda rcb=rcb, ph=ph: eng.tensor_scalar(
                sb["rc"][:, rcb + 3:rcb + 4], sb["bnagg"][:, 2 * ph:2 * ph + 1],
                sb["rc"][:, rcb + 2:rcb + 3], -1.0, OP.mult, OP.mult))
            inc("v", i)
            v_mb[sc] = c["v"]

        def ph3_stageD(sc):
            ph = sc % 2
            rcb = 4 * ph
            t1 = sb["t1f"][sc % 4]
            if SS:
                wait("v", v_mb[sc])
                if sc >= 2:
                    wait("lsy", lsy3 + 32 * (sc - 2) + 48)
            i = g_(SS, lambda ph=ph, t1=t1, rcb=rcb: eng.activation(
                sb["yf32"][ph][:, :], t1[:, :], AF.Identity,
                scale=sb["rc"][:, rcb + 2:rcb + 3], bias=sb["rc"][:, rcb + 3:rcb + 4]))
            inc("s", i)
            i = g_(SS, lambda sc=sc, t1=t1, rcb=rcb: eng.activation(
                sb["ybf"][sc % 4][:, :], t1[:, :], AF.Identity,
                scale=sb["rc"][:, rcb + 2:rcb + 3], bias=sb["rc"][:, rcb + 3:rcb + 4]))
            inc("s", i)
            s_ybf[sc] = c["s"]
            if SY:
                wait("s", s_ybf[sc])
            i = g_(SY, lambda sc=sc, ph=ph: eng.dma_start(
                out=yspill[128 * sc:128 * sc + 128, :], in_=sb["yf32"][ph][:, :]))
            inc("lsy", i, 16)

        def ph3_back(sc):
            yb = sb["ybf"][sc % 4]
            for half in range(2):
                ps = sb["CpB"][:, 1024 * half:1024 * half + 512]
                if T:
                    wait("s", max(s_ybf[sc], s_evac.get((sc - 1, half), 0)))
                lastt = None
                for q in range(4):
                    dc = 4 * half + q
                    lastt = g_(T, lambda yb=yb, dc=dc, q=q, ps=ps: eng.transpose(
                        ps[:, 128 * q:128 * q + 128],
                        yb[:, 128 * dc:128 * dc + 128],
                        sb["ident"][:, :]))
                inc("t", lastt)
                if SS:
                    wait("t", c["t"])
                i = g_(SS, lambda sc=sc, half=half, ps=ps: eng.activation(
                    sb["yt"].rearrange("p (dc s) -> p dc s", s=YTS)
                    [:, 4 * half:4 * half + 4, 2 + 128 * sc:2 + 128 * sc + 128],
                    ps.rearrange("p (q r) -> p q r", q=4), AF.Copy))
                inc("s", i)
                s_evac[(sc, half)] = c["s"]

        def conv0_part(dc):
            ca = sb["cacc0"]
            y0 = YTS * dc
            if V and dc == 0:
                wait("ldg", LD_SM2)
                wait("s", s_evac[(3, 1)])
            vdrain()
            i = g_(V, lambda ca=ca, y0=y0, dc=dc: eng.tensor_scalar(
                ca[:, :], sb["yt"][:, y0:y0 + 512],
                sb["dwc"][:, 3 * dc:3 * dc + 1], sb["dwbc"][:, dc:dc + 1],
                OP.mult, OP.add))
            vdrain()
            i = g_(V, lambda ca=ca, y0=y0, dc=dc: eng.scalar_tensor_tensor(
                ca[:, :], sb["yt"][:, y0 + 1:y0 + 1 + 512],
                sb["dwc"][:, 3 * dc + 1:3 * dc + 2], ca[:, :], OP.mult, OP.add))
            vdrain()
            i = g_(V, lambda ca=ca, y0=y0, dc=dc: eng.scalar_tensor_tensor(
                xct_sl(dc, 0, 512),
                sb["yt"][:, y0 + 2:y0 + 2 + 512],
                sb["dwc"][:, 3 * dc + 2:3 * dc + 3], ca[:, :], OP.mult, OP.add))
            if dc == DC - 1:
                inc("v", i)
                v_conv[0] = c["v"]

        for sc in range(SC):
            ph3_stageA(sc)
            ph3_stageB(sc)
            if sc >= 1:
                ph3_stageC(sc - 1)
                ph3_stageD(sc - 1)
            if sc >= 8:
                conv0_part(sc - 8)
            if sc >= 3:
                ph3_back(sc - 3)
            if upto == 31 and sc == 3:
                tapstop()
        ph3_stageC(SC - 1)
        ph3_stageD(SC - 1)
        ph3_back(SC - 3)
        ph3_back(SC - 2)
        ph3_back(SC - 1)
        LSY3_END = c["lsy"]
        PH3_V = c["v"]
        if upto == 3:
            tapstop()

        LD_PWT0 = c["ldg"]
        pwt_pos = {}
        for pi, wc in enumerate((0, 2, 4, 6, 1, 3, 5, 7)):
            if G and pi == 0:
                wait("t", t_wo[7])
            if G and pi == 4:
                wait("t", t_wo[SC - 1])
            i = g_(G, lambda wc=wc: eng.dma_start(
                out=sb["pwt_sb"][:, 1024 * wc:1024 * wc + 1024],
                in_=dr["pwt"][128 * wc:128 * wc + 128, :]))
            inc("ldg", i, 16)
            pwt_pos[wc] = c["ldg"]
        LD_PWT = c["ldg"]

        cacc = sb["trash"].bitcast(F32)

        def conv_group(n4):
            lo = 512 * n4
            if V:
                wait("s", s_evac[(4 * n4 + 3, 1)])
            i = None
            for dc in range(DC):
                y0 = YTS * dc + lo
                vdrain()
                i = g_(V, lambda y0=y0, dc=dc: eng.tensor_scalar(
                    cacc[:, :], sb["yt"][:, y0:y0 + 512],
                    sb["dwc"][:, 3 * dc:3 * dc + 1], sb["dwbc"][:, dc:dc + 1],
                    OP.mult, OP.add))
                vdrain()
                i = g_(V, lambda y0=y0, dc=dc: eng.scalar_tensor_tensor(
                    cacc[:, :], sb["yt"][:, y0 + 1:y0 + 1 + 512],
                    sb["dwc"][:, 3 * dc + 1:3 * dc + 2], cacc[:, :],
                    OP.mult, OP.add))
                vdrain()
                i = g_(V, lambda y0=y0, dc=dc, lo=lo: eng.scalar_tensor_tensor(
                    xct_sl(dc, lo, 512),
                    sb["yt"][:, y0 + 2:y0 + 2 + 512],
                    sb["dwc"][:, 3 * dc + 2:3 * dc + 3], cacc[:, :],
                    OP.mult, OP.add))
            inc("v", i)
            v_conv[n4] = c["v"]

        if upto == 41:
            conv_group(1); conv_group(2); conv_group(3)
            tapstop()

        s4 = c["s"]
        for n4 in range(4):
            if n4 + 1 < 4:
                conv_group(n4 + 1)
            for ec in range(DC):
                idx = 8 * n4 + ec
                ph = idx % 2
                ps = sb["Lp"][:, 512 * ph:512 * ph + 512]
                if T:
                    if ec == 0:
                        wait("v", v_conv[n4])
                    if idx >= 2:
                        wait("s", s4 + idx - 1)
                last = None
                for dc in range(DC):
                    if T and n4 == 0:
                        wait("ldg", pwt_pos[dc])
                    last = g_(T, lambda dc=dc, ec=ec, n4=n4, ps=ps: eng.matmul(
                        ps,
                        sb["pwt_sb"][:, 1024 * dc + 128 * ec:1024 * dc + 128 * ec + 128],
                        xct_sl(dc, 512 * n4, 512),
                        start=(dc == 0), stop=(dc == DC - 1)))
                inc("t", last)
                if SS:
                    wait("t", c["t"])
                i = g_(SS, lambda ec=ec, n4=n4, ps=ps: eng.activation(
                    sb["xc2t"][:, 2048 * ec + 512 * n4:2048 * ec + 512 * n4 + 512],
                    ps, AF.Silu, bias=sb["pwbc"][:, ec:ec + 1]))
                inc("s", i)
                if V:
                    wait("s", c["s"])
                i = g_(V, lambda ec=ec, n4=n4: eng.tensor_tensor(
                    sb["x1t"][:, 2048 * ec + 512 * n4:2048 * ec + 512 * n4 + 512],
                    sb["xc2t"][:, 2048 * ec + 512 * n4:2048 * ec + 512 * n4 + 512],
                    sb["yt"][:, YTS * ec + 2 + 512 * n4:YTS * ec + 2 + 512 * n4 + 512],
                    OP.add))
                inc("v", i)
                if ec == DC - 1:
                    v_x1t[n4] = c["v"]
        PW_V = c["v"]
        PW_S = c["s"]
        t_pwend = c["t"]
        if upto == 42:
            tapstop()

        if G:
            wait("t", t_pwend)
        i = g_(G, lambda: wload(sb["wet0"], dr["ewt"][1], D))
        inc("wet", i, 16)

        if V:
            wait("t", t_pwend)
        for pc in range(4):
            i = g_(V, lambda pc=pc: eng.memset(sb["combP"][pc][:, :], 0.0))
            if pc == 3:
                inc("v", i)
        CB0_V = c["v"]

        rtr_inline = (upto == 4)
        T5R = c["t"]
        S5R = c["s"]
        v_sub = {}

        def comb_sl(sc):
            return sb["combP"][sc // 4][:, 1024 * (sc % 4):1024 * (sc % 4) + 1024]

        def router_mm(sc):
            ph = sc % 2
            ps = (sb["S0"] if ph == 0 else sb["S1"])[:, 0:8]
            if T:
                if sc == 0:
                    wait("v", PW_V)
                if sc >= 2:
                    wait("v", CB0_V + 3 * (sc - 2) + 1)
            last = None
            for dc in range(DC):
                last = g_(T, lambda dc=dc, sc=sc, ps=ps: eng.matmul(
                    ps,
                    sb["x1t"][:, 2048 * dc + 128 * sc:2048 * dc + 128 * sc + 128],
                    sb["rwt_sb"][:, 8 * dc:8 * dc + 8],
                    start=(dc == 0), stop=(dc == DC - 1)))
            inc("t", last)

        s4r = c["s"]
        for sc in range(SC):
            ph = sc % 2
            ps = (sb["S0"] if ph == 0 else sb["S1"])[:, 0:8]
            if rtr_inline:
                router_mm(sc)
            if V:
                wait("t", T5R + sc + 1)
            i = g_(V, lambda ps=ps: eng.tensor_tensor(
                sb["rlg"][:, :], ps, sb["rbb"][:, :], OP.add))
            inc("v", i)
            vdrain()
            if RSAFE & 1:
                i = g_(V, lambda: eng.tensor_copy(sb["rmx"][:, :], sb["rlg"][:, :]))
            else:
                i = g_(V, lambda: eng.max(sb["rmx"][:, :], sb["rlg"][:, :]))
            vdrain()
            i = g_(V, lambda: eng.tensor_tensor(
                sb["rc"][:, 8:9], sb["rmx"][:, 0:1], sb["rmx"][:, 1:2], OP.subtract))
            inc("v", i)
            v_sub[sc] = c["v"]
            if rtr_inline:
                if SS:
                    wait("v", v_sub[sc])
                i = g_(SS, lambda: eng.activation(
                    sb["rc"][:, 9:10], sb["rc"][:, 8:9],
                    AF.Identity if (RSAFE & 4) else AF.Sigmoid))
                inc("s", i)
            if V:
                wait("s", c["s"] if rtr_inline else S5R + sc + 1)
            i = g_(V, lambda: eng.tensor_scalar(
                sb["rc"][:, 10:11], sb["rc"][:, 9:10], -1.0, 1.0, OP.mult, OP.add))
            if RSAFE & 2:
                i = g_(V, lambda: eng.tensor_copy(sb["ris"][:, 0:8], sb["rlg"][:, :]))
                i = g_(V, lambda: eng.tensor_copy(sb["ris"][:, 8:16], sb["rlg"][:, :]))
            else:
                i = g_(V, lambda: eng.tensor_scalar(
                    sb["ris"][:, 0:8], sb["rlg"][:, :], sb["rmx"][:, 0:1],
                    sb["rc"][:, 9:10], OP.is_equal, OP.mult))
                i = g_(V, lambda: eng.tensor_scalar(
                    sb["ris"][:, 8:16], sb["rlg"][:, :], sb["rmx"][:, 1:2],
                    sb["rc"][:, 10:11], OP.is_equal, OP.mult))
            vdrain()
            i = g_(V, lambda sc=sc: eng.tensor_tensor(
                sb["rw"][:, 8 * sc:8 * sc + 8], sb["ris"][:, 0:8],
                sb["ris"][:, 8:16], OP.add))
            inc("v", i)
            vdrain()
            if not rtr_inline:
                if V:
                    wait("yes", sc + 1)
                i = g_(V, lambda sc=sc: eng.scalar_tensor_tensor(
                    comb_sl(sc),
                    sb["zbuf"][sc % 2][:, :], sb["rw"][:, 8 * sc:8 * sc + 1],
                    comb_sl(sc), OP.mult, OP.add))
                inc("yev", i)
        if upto == 4:
            tapstop()

        for e in range(E):
            for sc in range(SC):
                it = e * SC + sc
                ph = it % 2
                ps = sb["Lp"][:, 1024 * ph:1024 * ph + 1024]
                if T:
                    if it == 0:
                        wait("s", PW_S)
                        wait("ldg", LD_SM2)
                    if e == 0 and sc % 4 == 0:
                        wait("v", v_x1t[sc // 4])
                    if sc == 0:
                        wait("wet", 16 * (e + 1))
                    if it >= 2:
                        wait("yes", it - 1)
                last = None
                for n2 in range(2):
                    last = g_(T, lambda e=e, n2=n2, ps=ps: eng.matmul(
                        ps[:, 512 * n2:512 * n2 + 512],
                        sb["esel"][:, 128 * e:128 * e + 128],
                        sb["ebr2"][:, 512 * n2:512 * n2 + 512],
                        start=True, stop=False, skip_group_check=True))
                    for dc in range(DC):
                        last = g_(T, lambda e=e, dc=dc, sc=sc, n2=n2, ps=ps: eng.matmul(
                            ps[:, 512 * n2:512 * n2 + 512],
                            sb["x1t"][:, 2048 * dc + 128 * sc:2048 * dc + 128 * sc + 128],
                            sb["wet"][e % 2][:, 1024 * dc + 512 * n2:
                                             1024 * dc + 512 * n2 + 512],
                            start=False, stop=(dc == DC - 1), skip_group_check=True))
                inc("yet", last)
                if e == 0 and not rtr_inline:
                    router_mm(sc)
                    if SS:
                        wait("v", v_sub[sc])
                    i = g_(SS, lambda: eng.activation(
                        sb["rc"][:, 9:10], sb["rc"][:, 8:9],
                        AF.Identity if (RSAFE & 4) else AF.Sigmoid))
                    inc("s", i)
                if SS:
                    wait("yet", c["yet"])
                    if it >= 2:
                        wait("yev", it - 1)
                i = g_(SS, lambda ph=ph, ps=ps: eng.activation(
                    sb["zbuf"][ph][:, :], ps, AF.Silu))
                inc("yes", i)
                if e >= 1 or rtr_inline:
                    if V:
                        wait("yes", c["yes"])
                    i = g_(V, lambda e=e, sc=sc, ph=ph: eng.scalar_tensor_tensor(
                        comb_sl(sc),
                        sb["zbuf"][ph][:, :], sb["rw"][:, 8 * sc + e:8 * sc + e + 1],
                        comb_sl(sc), OP.mult, OP.add))
                    inc("yev", i)
            if e + 2 < E:
                if G:
                    wait("yet", 16 * (e + 1))
                i = g_(G, lambda e=e: wload(sb["wet"][e % 2], dr["ewt"][e + 2], D))
                inc("wet", i, 16)
        EXP_V = c["yev"]

        if upto == 5:
            tapstop()
        LD_OWT0 = c["ldg"]
        for wc in range(DC):
            if G and wc == 0:
                wait("yet", 16 * E)
            i = g_(G, lambda wc=wc: eng.dma_start(
                out=sb["owt_sb"][:, 1024 * wc:1024 * wc + 1024],
                in_=dr["owt"][128 * wc:128 * wc + 128, :]))
            inc("ldg", i, 16)
        LD_OWT = c["ldg"]

        t6 = c["t"]
        for sc in range(SC):
            if V:
                if sc == 0:
                    wait("yev", EXP_V)
                if sc >= 2:
                    wait("t", t6 + 2 * (sc - 2) + 2)
            i = g_(V, lambda sc=sc: eng.tensor_copy(
                sb["cbf"][sc % 2][:, :], comb_sl(sc)))
            inc("v", i)
            for half in range(2):
                ps = sb["CpB"][:, 1024 * half:1024 * half + 512]
                if T:
                    wait("v", c["v"])
                lastt = None
                for q in range(4):
                    dc = 4 * half + q
                    lastt = g_(T, lambda sc=sc, dc=dc, q=q, ps=ps: eng.transpose(
                        ps[:, 128 * q:128 * q + 128],
                        sb["cbf"][sc % 2][:, 128 * dc:128 * dc + 128],
                        sb["ident"][:, :]))
                inc("t", lastt)
                if V:
                    wait("t", c["t"])
                i = g_(V, lambda sc=sc, half=half, ps=ps: eng.tensor_copy(
                    sb["combT"].rearrange("p (dc s) -> p dc s", dc=DC)
                    [:, 4 * half:4 * half + 4, 128 * sc:128 * sc + 128],
                    ps.rearrange("p (q r) -> p q r", q=4)))
                inc("v", i)
        CBT_V = c["v"]

        if V:
            wait("ldg", LD_FDT)
        last = None
        for dc in range(DC):
            last = g_(V, lambda dc=dc: eng.tensor_scalar(
                sb["fdt_sb"][:, 256 * dc:256 * dc + 256],
                sb["fdt_sb"][:, 256 * dc:256 * dc + 256],
                sb["nwc"][:, dc:dc + 1], None, OP.mult))
        inc("v", last)

        s6 = c["s"]
        v6 = c["v"]
        for sc in range(SC):
            ph = sc % 2
            psM = sb["Lp"][:, 1024 * ph:1024 * ph + 1024]
            hp = sb["hpf"][ph]
            rcb = 16 + 4 * ph
            if G:
                if sc == 0:
                    wait("lsy", LSY3_END)
                if sc < 2:
                    wait("v", CBT_V)
                else:
                    wait("s", s6 + 3 * (sc - 2) + 3)
            i = g_(G, lambda hp=hp, sc=sc: eng.dma_start(
                out=hp[:, :], in_=yspill[128 * sc:128 * sc + 128, :]))
            inc("ldy", i, 16)
            if T:
                if sc == 0:
                    wait("v", CBT_V)
                if sc >= 2:
                    wait("v", v6 + 3 * (sc - 2) + 1)
            last = None
            for n2 in range(2):
                last = g_(T, lambda n2=n2, psM=psM: eng.matmul(
                    psM[:, 512 * n2:512 * n2 + 512], sb["ones1"][:, :],
                    sb["obr_sb"][0:1, 512 * n2:512 * n2 + 512],
                    start=True, stop=False, skip_group_check=True))
                for dc in range(DC):
                    if T and sc == 0 and n2 == 0:
                        wait("ldg", LD_OWT0 + 16 * (dc + 1))
                    last = g_(T, lambda dc=dc, sc=sc, n2=n2, psM=psM: eng.matmul(
                        psM[:, 512 * n2:512 * n2 + 512],
                        sb["combT"][:, 2048 * dc + 128 * sc:2048 * dc + 128 * sc + 128],
                        sb["owt_sb"][:, 1024 * dc + 512 * n2:1024 * dc + 512 * n2 + 512],
                        start=False, stop=(dc == DC - 1), skip_group_check=True))
            for dc in range(DC):
                last = g_(T, lambda dc=dc, sc=sc, psM=psM: eng.matmul(
                    psM[:, 128 * dc:128 * dc + 128],
                    sb["xc2t"][:, 2048 * dc + 128 * sc:2048 * dc + 128 * sc + 128],
                    sb["ident"][:, :],
                    start=False, stop=(dc == DC - 1), skip_group_check=True))
            inc("t", last)
            if V:
                wait("t", c["t"])
                wait("ldy", c["ldy"])
            i = g_(V, lambda hp=hp, psM=psM: eng.scalar_tensor_tensor(
                hp[:, :], psM, 1.0, hp[:, :], OP.mult, OP.add))
            inc("v", i)
            if SS:
                wait("v", c["v"])
            i = g_(SS, lambda hp=hp, rcb=rcb: eng.activation(
                sb["trash"][:, :], hp[:, :], AF.Square,
                accum_out=sb["rc"][:, rcb:rcb + 1]))
            inc("s", i)
            if V:
                wait("s", c["s"])
            i = g_(V, lambda rcb=rcb: eng.tensor_scalar(
                sb["rc"][:, rcb + 1:rcb + 2], sb["rc"][:, rcb:rcb + 1],
                1.0 / D, 1e-6, OP.mult, OP.add))
            inc("v", i)
            if SS:
                wait("v", c["v"])
            i = g_(SS, lambda rcb=rcb: eng.activation(
                sb["rc"][:, rcb + 2:rcb + 3], sb["rc"][:, rcb + 1:rcb + 2], AF.Sqrt))
            inc("s", i)
            if V:
                wait("s", c["s"])
            i = g_(V, lambda rcb=rcb: eng.reciprocal(
                sb["rc"][:, rcb + 3:rcb + 4], sb["rc"][:, rcb + 2:rcb + 3]))
            inc("v", i)
            if SS:
                wait("v", c["v"])
            i = g_(SS, lambda sc=sc, hp=hp, rcb=rcb: eng.activation(
                sb["t2bf"][:, 1024 * sc:1024 * sc + 1024], hp[:, :],
                AF.Identity, scale=sb["rc"][:, rcb + 3:rcb + 4]))
            inc("s", i)
        RMS_S = c["s"]

        v6t = c["v"]
        for sc in range(SC):
            for half in range(2):
                idx = 2 * sc + half
                ps = sb["CpB"][:, 1024 * half:1024 * half + 512]
                if T:
                    if idx == 0:
                        wait("s", RMS_S)
                    if idx >= 2:
                        wait("v", v6t + idx - 1)
                lastt = None
                for q in range(4):
                    dc = 4 * half + q
                    lastt = g_(T, lambda sc=sc, dc=dc, q=q, ps=ps: eng.transpose(
                        ps[:, 128 * q:128 * q + 128],
                        sb["t2bf"][:, 1024 * sc + 128 * dc:1024 * sc + 128 * dc + 128],
                        sb["ident"][:, :]))
                inc("t", lastt)
                if V:
                    wait("t", c["t"])
                i = g_(V, lambda sc=sc, half=half, ps=ps: eng.tensor_copy(
                    sb["t2t"].rearrange("p (dc s) -> p dc s", dc=DC)
                    [:, 4 * half:4 * half + 4, 128 * sc:128 * sc + 128],
                    ps.rearrange("p (q r) -> p q r", q=4)))
                inc("v", i)
        T2T_V = c["v"]

        if upto == 6:
            tapstop()
        s7 = c["s"]
        for sc in range(SC):
            ph = sc % 2
            psF = (sb["S0"] if ph == 0 else sb["S1"])[:, 0:256]
            if T:
                if sc == 0:
                    wait("v", T2T_V)
                if sc >= 2:
                    wait("s", s7 + sc - 1)
            last = g_(T, lambda psF=psF: eng.matmul(
                psF, sb["ones1"][:, :], sb["fdbr_sb"][0:1, :],
                start=True, stop=False, skip_group_check=True))
            for dc in range(DC):
                last = g_(T, lambda dc=dc, sc=sc, psF=psF: eng.matmul(
                    psF,
                    sb["t2t"][:, 2048 * dc + 128 * sc:2048 * dc + 128 * sc + 128],
                    sb["fdt_sb"][:, 256 * dc:256 * dc + 256],
                    start=False, stop=(dc == DC - 1), skip_group_check=True))
            inc("t", last)
            if SS:
                wait("t", c["t"])
            i = g_(SS, lambda sc=sc, psF=psF: eng.activation(
                sb["ff1"][:, 256 * sc:256 * sc + 256], psF, AF.Gelu))
            inc("s", i)
        FF1_S = c["s"]

        v7 = c["v"]
        for sc in range(SC):
            ph = sc % 2
            ps = sb["CpB"][:, 1024 * ph:1024 * ph + 256]
            if T:
                if sc == 0:
                    wait("s", FF1_S)
                if sc >= 2:
                    wait("v", v7 + sc - 1)
            lastt = None
            for q in range(2):
                lastt = g_(T, lambda sc=sc, q=q, ps=ps: eng.transpose(
                    ps[:, 128 * q:128 * q + 128],
                    sb["ff1"][:, 256 * sc + 128 * q:256 * sc + 128 * q + 128],
                    sb["ident"][:, :]))
            inc("t", lastt)
            if V:
                wait("t", c["t"])
            i = g_(V, lambda sc=sc, ps=ps: eng.tensor_copy(
                sb["ff1t"].rearrange("p (hc s) -> p hc s", hc=2)
                [:, :, 128 * sc:128 * sc + 128],
                ps.rearrange("p (q r) -> p q r", q=2)))
            inc("v", i)
        FF1T_V = c["v"]

        if V:
            wait("t", c["t"])
        lastd = None
        for dc in range(DC):
            lastd = g_(V, lambda dc=dc: eng.tensor_scalar(
                sb["nwdiag"][:, 128 * dc:128 * dc + 128], sb["ident"][:, :],
                sb["nwc"][:, dc:dc + 1], None, OP.mult))
        inc("v", lastd)
        NWD_V = c["v"]

        s7b = c["s"]
        v7b = c["v"]
        for sc in range(SC):
            ph = sc % 2
            psZ = sb["Lp"][:, 1024 * ph:1024 * ph + 1024]
            zf = sb["hpf"][ph]
            rcb = 24 + 4 * ph
            if G:
                if sc < 2:
                    wait("s", RMS_S)
                else:
                    wait("s", s7b + 2 * (sc - 2) + 2)
            i = g_(G, lambda zf=zf, sc=sc: eng.dma_start(
                out=zf[:, :], in_=yspill[128 * sc:128 * sc + 128, :]))
            inc("ldy", i, 16)
            if T:
                if sc == 0:
                    wait("v", NWD_V)
                if sc >= 2:
                    wait("v", v7b + 3 * (sc - 2) + 1)
            last = None
            for n2 in range(2):
                last = g_(T, lambda n2=n2, psZ=psZ: eng.matmul(
                    psZ[:, 512 * n2:512 * n2 + 512], sb["ones1"][:, :],
                    sb["fubr_sb"][0:1, 512 * n2:512 * n2 + 512],
                    start=True, stop=False, skip_group_check=True))
                for hcc in range(2):
                    last = g_(T, lambda hcc=hcc, sc=sc, n2=n2, psZ=psZ: eng.matmul(
                        psZ[:, 512 * n2:512 * n2 + 512],
                        sb["ff1t"][:, 2048 * hcc + 128 * sc:2048 * hcc + 128 * sc + 128],
                        sb["fut_sb"][:, 1024 * hcc + 512 * n2:1024 * hcc + 512 * n2 + 512],
                        start=False, stop=(hcc == 1), skip_group_check=True))
            for dc in range(DC):
                last = g_(T, lambda dc=dc, sc=sc, psZ=psZ: eng.matmul(
                    psZ[:, 128 * dc:128 * dc + 128],
                    sb["t2t"][:, 2048 * dc + 128 * sc:2048 * dc + 128 * sc + 128],
                    sb["nwdiag"][:, 128 * dc:128 * dc + 128],
                    start=False, stop=(dc == DC - 1), skip_group_check=True))
            inc("t", last)
            if V:
                wait("t", c["t"])
                wait("ldy", c["ldy"])
            i = g_(V, lambda zf=zf, psZ=psZ: eng.scalar_tensor_tensor(
                zf[:, :], psZ, 1.0, zf[:, :], OP.mult, OP.add))
            inc("v", i)
            vdrain()
            i = g_(V, lambda zf=zf: eng.bn_stats(sb["bns"][:, 0:6], zf[:, 0:512]))
            i = g_(V, lambda zf=zf: eng.bn_stats(sb["bns"][:, 6:12], zf[:, 512:1024]))
            vdrain()
            i = g_(V, lambda: eng.bn_aggr(sb["bnagg"][:, 0:2], sb["bns"][:, 0:12]))
            vdrain()
            i = g_(V, lambda rcb=rcb: eng.tensor_scalar(
                sb["rc"][:, rcb:rcb + 1], sb["bnagg"][:, 1:2], 1.0, 1e-5,
                OP.mult, OP.add))
            inc("v", i)
            if SS:
                wait("v", c["v"])
            i = g_(SS, lambda rcb=rcb: eng.activation(
                sb["rc"][:, rcb + 1:rcb + 2], sb["rc"][:, rcb:rcb + 1], AF.Sqrt))
            inc("s", i)
            if V:
                wait("s", c["s"])
            i = g_(V, lambda rcb=rcb: eng.reciprocal(
                sb["rc"][:, rcb + 2:rcb + 3], sb["rc"][:, rcb + 1:rcb + 2]))
            vdrain()
            i = g_(V, lambda rcb=rcb: eng.tensor_scalar(
                sb["rc"][:, rcb + 3:rcb + 4], sb["bnagg"][:, 0:1],
                sb["rc"][:, rcb + 2:rcb + 3], -1.0, OP.mult, OP.mult))
            inc("v", i)
            if SS:
                wait("v", c["v"])
                if sc >= 2:
                    wait("sto", 16 * (sc - 1))
            i = g_(SS, lambda ph=ph, zf=zf, rcb=rcb: eng.activation(
                sb["ostg"][ph][:, :], zf[:, :], AF.Identity,
                scale=sb["rc"][:, rcb + 2:rcb + 3], bias=sb["rc"][:, rcb + 3:rcb + 4]))
            inc("s", i)
            if SY:
                wait("s", c["s"])
            i = g_(SY, lambda sc=sc, ph=ph: eng.dma_start(
                out=out_h[128 * sc:128 * sc + 128, :], in_=sb["ostg"][ph][:, :]))
            inc("sto", i, 16)

        if upto == 7:
            tapstop()
        if SY:
            wait("sto", c["sto"])

    import contextlib
    with nc.Block() as block, contextlib.ExitStack() as st:
        sems = {}
        for n in SEM_NAMES:
            sems[n] = st.enter_context(nc.semaphore(n))
        ctx["sems"] = sems

        def sbuf(name, shape, dt):
            ctx[name] = st.enter_context(nc.sbuf_tensor(name, shape, dt))
            return ctx[name]

        def psum(name, shape):
            ctx[name] = st.enter_context(nc.psum_tensor(name, shape, F32))

        sbuf("ident", [128, 128], BF16)
        sbuf("ones1", [1, 128], BF16)
        sbuf("bqc", [128, 4], F32); sbuf("bkc", [128, 4], F32)
        sbuf("bvb", [128, HD], F32)
        sbuf("dwc", [128, DC * 3], F32); sbuf("dwbc", [128, DC], F32)
        sbuf("pwbc", [128, DC], F32); sbuf("rbb", [128, E], F32)
        sbuf("nwc", [128, DC], F32)
        sbuf("ebr2", [8, D], BF16)
        sbuf("esel", [8, E * 128], BF16)
        sbuf("obr_sb", [1, D], BF16)
        sbuf("bor_sb", [1, D], BF16)
        sbuf("fdbr_sb", [1, HID], BF16); sbuf("fubr_sb", [1, D], BF16)
        sbuf("rwt_sb", [128, DC * E], BF16)
        sbuf("rc", [128, 32], F32)
        sbuf("bns", [128, 24], F32); sbuf("bnagg", [128, 4], F32)
        sbuf("rlg", [128, 8], F32); sbuf("rmx", [128, 8], F32)
        sbuf("ris", [128, 16], F32)
        sbuf("rw", [128, 8 * SC], F32)
        sbuf("vt", [128, KC * DH], BF16)
        for par in range(2):
            sbuf(f"wdenA{par}", [128, 16], F32)
            sbuf(f"wdenB{par}", [128, 16], F32)
            sbuf(f"wsum{par}", [128, 16], F32)
            sbuf(f"wcol_f{par}", [128, 16], F32)
            sbuf(f"ucol_f{par}", [128, 16], F32)
            sbuf(f"wcol_b{par}", [128, 16], BF16)
            sbuf(f"ucol_b{par}", [128, 16], BF16)
        for nm in ("wdenA", "wdenB", "wsum", "wcol_f", "ucol_f",
                   "wcol_b", "ucol_b"):
            ctx[nm] = [ctx[f"{nm}0"], ctx[f"{nm}1"]]
        sbuf("yf32_0", [128, D], F32); sbuf("yf32_1", [128, D], F32)
        ctx["yf32"] = [ctx["yf32_0"], ctx["yf32_1"]]

        AR1 = sbuf("AR1", [128, 32768], BF16)
        AR2 = sbuf("AR2", [128, 32832], BF16)
        AR5 = sbuf("AR5", [128, 8192], BF16)
        AR6 = sbuf("AR6", [128, 8192], BF16)
        AR7 = sbuf("AR7", [128, 8192], BF16)

        a1 = AR1.ap(); a2 = AR2.ap(); a5 = AR5.ap(); a6 = AR6.ap(); a7 = AR7.ap()
        ctx["wq_sb"] = a1[:, 0:4096]
        ctx["wk_sb"] = a1[:, 4096:8192]
        ctx["wv_sb"] = a1[:, 8192:12288]
        ctx["Abuf"] = a1[:, 0:32768]
        ctx["convacc"] = [a1[:, 0:4096].bitcast(F32), a1[:, 0:4096].bitcast(F32)]
        ctx["t1f"] = [a1[:, 16384:18432].bitcast(F32),
                      a1[:, 18432:20480].bitcast(F32),
                      a1[:, 20480:22528].bitcast(F32),
                      a1[:, 22528:24576].bitcast(F32)]
        ctx["ybf"] = [a1[:, 28672:29696], a1[:, 29696:30720],
                      a1[:, 30720:31744], a1[:, 31744:32768]]
        ctx["cacc0"] = a1[:, 24576:25600].bitcast(F32)
        ctx["x1t"] = a1[:, 0:16384]
        ctx["xc2t"] = a1[:, 16384:32768]
        ctx["combT"] = a1[:, 0:16384]
        ctx["ff1"] = a1[:, 0:4096]
        ctx["ff1t"] = a1[:, 4096:8192]
        ctx["nwdiag"] = a1[:, 8192:9216]
        ctx["xbf"] = a2[:, 0:16384]
        ctx["xT"] = a2[:, 16384:32768]
        ctx["Tbuf"] = a2[:, 0:32768]
        ctx["yt"] = a2[:, 0:16448]
        ctx["wet1"] = a2[:, 16448:24640]
        ctx["xct_a"] = a2[:, 24640:32832]
        ctx["combP"] = None
        ctx["cbf"] = [a2[:, 16448:17472], a2[:, 17472:18496]]
        ctx["t2bf"] = a2[:, 16448:32832]
        ctx["t2t"] = a2[:, 0:16384]
        ctx["qt"] = a5[:, 0:8192]
        ctx["ot"] = a5[:, 0:8192]
        ctx["pwt_sb"] = a5[:, 0:8192]
        ctx["hpf"] = [a5[:, 0:2048].bitcast(F32), a5[:, 2048:4096].bitcast(F32)]
        ctx["ostg"] = [a5[:, 4096:6144].bitcast(F32), a5[:, 6144:8192].bitcast(F32)]
        ctx["kt"] = a6[:, 0:8192]
        ctx["xct_b"] = a6[:, 0:8192]
        ctx["wet0"] = a6[:, 0:8192]
        ctx["owt_sb"] = a6[:, 0:8192]
        ctx["vna"] = a7[:, 0:8192]
        ctx["obuf"] = a7[:, 0:8192]
        ctx["fdt_sb"] = a7[:, 0:2048]
        ctx["fut_sb"] = a7[:, 2048:4096]
        ctx["wo_sb"] = a7[:, 4096:8192]
        ctx["zbuf"] = [a7[:, 4096:5120], a7[:, 5120:6144]]
        ctx["tmpf"] = a7[:, 4096:6144].bitcast(F32)
        ctx["trash"] = a7[:, 6144:7168]
        ctx["combP"] = [a5[:, 0:8192].bitcast(F32),
                        a2[:, 0:8192].bitcast(F32),
                        a2[:, 8192:16384].bitcast(F32),
                        a2[:, 24640:32832].bitcast(F32)]
        ctx["wet"] = [ctx["wet1"], ctx["wet0"]]

        psum("Lp", [128, 2048])
        psum("Cp", [128, 1024])
        psum("S0", [128, 512])
        psum("S1", [128, 512])
        ctx["CpB"] = ctx["Cp"][:, :].bitcast(BF16)

        def _run(me, eng):
            try:
                emit(me, eng)
            except Exception as ex:
                if type(ex).__name__ != "_Stop":
                    raise

        @block.tensor
        def _(eng):
            _run("tensor", eng)

        @block.vector
        def _(eng):
            _run("vector", eng)

        @block.scalar
        def _(eng):
            _run("scalar", eng)

        @block.gpsimd
        def _(eng):
            _run("gpsimd", eng)

        @block.sync
        def _(eng):
            _run("sync", eng)

    return nc


def _prep_core_inputs(b, inp):
    f = lambda a: np.ascontiguousarray(a, dtype=np.float32)
    d = {
        "x": f(np.asarray(inp["L0_attn_x"])[b]),
        "wq": f(inp["L0_attn_Wq"]), "wk": f(inp["L0_attn_Wk"]),
        "wv": f(inp["L0_attn_Wv"]),
        "bqc": f(np.asarray(inp["L0_attn_bq"]).reshape(4, 128).T),
        "bkc": f(np.asarray(inp["L0_attn_bk"]).reshape(4, 128).T),
        "bvb": f(np.broadcast_to(np.asarray(inp["L0_attn_bv"]), (128, HD))),
        "wo": f(inp["L0_attn_Wo"]),
        "bob": f(np.broadcast_to(np.asarray(inp["L0_attn_bo"]), (128, D))),
        "bor": f(np.asarray(inp["L0_attn_bo"]).reshape(1, D)),
        "dwc": f(np.asarray(inp["L0_ffn_dw_kernel"])
                 .reshape(DC, 128, 3).transpose(1, 0, 2).reshape(128, DC * 3)),
        "dwbc": f(np.asarray(inp["L0_ffn_dw_bias"]).reshape(DC, 128).T),
        "pwt": f(np.asarray(inp["L0_ffn_pw_w"]).T),
        "pwbc": f(np.asarray(inp["L0_ffn_pw_b"]).reshape(DC, 128).T),
        "rwt": f(np.asarray(inp["L0_ffn_router_w"]).T),
        "rbb": f(np.broadcast_to(np.asarray(inp["L0_ffn_router_b"]), (128, E))),
        "ewt": f(np.asarray(inp["L0_ffn_expert_w"]).transpose(0, 2, 1)),
        "ebr": f(inp["L0_ffn_expert_b"]),
        "owt": f(np.asarray(inp["L0_ffn_out_w"]).T),
        "obr": f(np.asarray(inp["L0_ffn_out_b"]).reshape(1, D)),
        "nwc": f(np.asarray(inp["L0_ffn_norm_w"]).reshape(DC, 128).T),
        "nwb": f(np.broadcast_to(np.asarray(inp["L0_ffn_norm_w"]), (128, D))),
        "fdt": f(np.asarray(inp["L0_ffn_ff_down_w"]).T),
        "fdbr": f(np.asarray(inp["L0_ffn_ff_down_b"]).reshape(1, HID)),
        "fut": f(np.asarray(inp["L0_ffn_ff_up_w"]).T),
        "fubr": f(np.asarray(inp["L0_ffn_ff_up_b"]).reshape(1, D)),
        "ident": np.eye(128, dtype=np.float32),
        "ones1": np.ones((1, 128), np.float32),
        "esel": np.kron(np.eye(E, dtype=np.float32), np.ones((1, 128), np.float32)),
    }
    return {"i_" + k: v for k, v in d.items()}


_NC_CACHE = {}


def kernel(**inputs):
    if "prog" not in _NC_CACHE:
        _NC_CACHE["prog"] = build_program()
    nc = _NC_CACHE["prog"]
    in_maps = [_prep_core_inputs(b, inputs) for b in range(8)]
    res = run_bass_kernel_spmd(nc, in_maps, core_ids=list(range(8)))
    return np.stack(
        [np.asarray(res.results[b]["out"], np.float32) for b in range(8)], axis=0)


# version 11 (final)
# speedup vs baseline: 1.0088x; 1.0050x over previous
import numpy as np
import concourse.bass as bass
import concourse.mybir as mybir
from concourse.bass_utils import run_bass_kernel_spmd
from concourse.alu_op_type import AluOpType as OP

F32 = mybir.dt.float32
BF16 = mybir.dt.bfloat16
AF = mybir.ActivationFunctionType

S = 2048
D = 1024
H = 8
DH = 64
HD = 512
E = 8
HID = 256
SC = S // 128
DC = D // 128
KC = 16
NPASS = 39
YTS = 2056

SEM_NAMES = [
    "ldg", "ldy", "lsy", "sto", "xrv",
    "t", "v", "s",
    "snkt", "snkv", "lgt", "lgs", "avt", "avs", "vt",
    "wet", "wef", "yet", "yes", "yev",
]


RSAFE = 0
TSAFE = 1


def build_program(upto=None, taps=()):
    nc = bass.Bass("TRN2", target_bir_lowering=False, debug=False)

    dr = {}
    def din(name, shape, dt=F32):
        dr[name] = nc.dram_tensor("i_" + name, shape, dt, kind="ExternalInput").ap()
    din("x", [S, D])
    din("wq", [D, HD]); din("wk", [D, HD]); din("wv", [D, HD])
    din("bqc", [128, 4]); din("bkc", [128, 4]); din("bvb", [128, HD])
    din("wo", [HD, D]); din("bob", [128, D]); din("bor", [1, D])
    din("dwc", [128, DC * 3]); din("dwbc", [128, DC])
    din("pwt", [D, D]); din("pwbc", [128, DC])
    din("rwt", [D, E]); din("rbb", [128, E])
    din("ewt", [E, D, D])
    din("ebr", [E, D])
    din("owt", [D, D]); din("obr", [1, D])
    din("nwc", [128, DC]); din("nwb", [128, D])
    din("fdt", [D, HID]); din("fdbr", [1, HID])
    din("fut", [HID, D]); din("fubr", [1, D])
    din("ident", [128, 128])
    din("ones1", [1, 128])
    din("esel", [8, E * 128])
    yspill = nc.dram_tensor("yspill", [S, D], F32, kind="Internal").ap()
    out_h = nc.dram_tensor("out", [S, D], F32, kind="ExternalOutput").ap()
    dbg_h = {}
    for (dn, sn, shp, dt) in taps:
        dbg_h[dn] = nc.dram_tensor(dn, shp, dt, kind="ExternalOutput").ap()

    ctx = {}
    RSAFE = globals()["RSAFE"]
    TSAFE = globals()["TSAFE"]

    def emit(me, eng):
        c = {n: 0 for n in SEM_NAMES}
        sems = ctx["sems"]
        sb = ctx

        def inc(name, instr, k=1):
            c[name] += k
            if instr is not None and instr is not False:
                instr.then_inc(sems[name], k)

        def wait(name, val):
            if eng is not None:
                eng.wait_ge(sems[name], val)

        T = me == "tensor"; V = me == "vector"; SS = me == "scalar"
        G = me == "gpsimd"; SY = me == "sync"

        def g_(cond, instr_fn):
            return instr_fn() if cond else None

        def vdrain():
            if V:
                eng.drain()

        def wload(dst_ap, src_dr, o):
            return eng.dma_start(
                out=dst_ap.rearrange("p (c o) -> p c o", o=o),
                in_=src_dr.rearrange("(c p) o -> p c o", p=128))

        class _Stop(Exception):
            pass

        def tapstop():
            if SY:
                for nm2 in ("t", "v", "s", "snkv", "lgs", "avs", "yev",
                            "yes", "vt"):
                    if c[nm2] > 0:
                        wait(nm2, c[nm2])
                ns = 0
                for (dn, sn, shp, dt) in taps:
                    ap = sb[sn] if isinstance(sn, str) else sn(sb)
                    ap = ap[:, :]
                    i2 = eng.dma_start(out=dbg_h[dn][:, :] if len(shp) == 2
                                       else dbg_h[dn][:], in_=ap)
                    i2.then_inc(sems["sto"], 16)
                    ns += 16
                wait("sto", c["sto"] + ns)
            raise _Stop()

        i = g_(G, lambda: wload(sb["xbf"], dr["x"], D))
        inc("ldg", i, 16)
        i = g_(G, lambda: eng.dma_start(out=sb["ident"][:, :], in_=dr["ident"][:, :]))
        inc("ldg", i, 16)
        i = g_(G, lambda: eng.dma_start(out=sb["ones1"][:, :], in_=dr["ones1"][:, :]))
        inc("ldg", i, 16)
        for wname, dst in (("wq", "wq_sb"), ("wk", "wk_sb"), ("wv", "wv_sb")):
            i = g_(G, lambda wname=wname, dst=dst: wload(sb[dst], dr[wname], HD))
            inc("ldg", i, 16)
        for nm in ("bqc", "bkc", "bvb"):
            i = g_(G, lambda nm=nm: eng.dma_start(out=sb[nm][:, :], in_=dr[nm][:, :]))
            inc("ldg", i, 16)
        LD_ATTN = c["ldg"]

        i = g_(G, lambda: eng.dma_start(out=sb["bor_sb"][:, :], in_=dr["bor"][:, :]))
        inc("ldg", i, 16)
        for nm in ("dwc", "dwbc", "pwbc", "rbb", "nwc"):
            i = g_(G, lambda nm=nm: eng.dma_start(out=sb[nm][:, :], in_=dr[nm][:, :]))
            inc("ldg", i, 16)
        i = g_(G, lambda: wload(sb["rwt_sb"], dr["rwt"], E))
        inc("ldg", i, 16)
        i = g_(G, lambda: eng.dma_start(out=sb["ebr2"][:, :], in_=dr["ebr"][:, :]))
        inc("ldg", i, 16)
        i = g_(G, lambda: eng.dma_start(out=sb["esel"][:, :], in_=dr["esel"][:, :]))
        inc("ldg", i, 16)
        for nm, dst in (("obr", "obr_sb"), ("fdbr", "fdbr_sb"), ("fubr", "fubr_sb")):
            i = g_(G, lambda nm=nm, dst=dst: eng.dma_start(
                out=sb[dst][:, :], in_=dr[nm][:, :]))
            inc("ldg", i, 16)
        LD_SM2 = c["ldg"]

        if T:
            wait("ldg", 48)
        v0 = c["v"]
        for sc in range(SC):
            for half in range(2):
                idx = 2 * sc + half
                ps = sb["CpB"][:, 1024 * half:1024 * half + 512]
                if T and idx >= 2:
                    wait("v", v0 + idx - 1)
                lastt = None
                for q in range(4):
                    dc = 4 * half + q
                    lastt = g_(T, lambda q=q, dc=dc, sc=sc, ps=ps: eng.transpose(
                        ps[:, 128 * q:128 * q + 128],
                        sb["xbf"][:, 1024 * sc + 128 * dc:1024 * sc + 128 * dc + 128],
                        sb["ident"][:, :]))
                inc("t", lastt)
                if V:
                    wait("t", c["t"])
                i = g_(V, lambda sc=sc, half=half, ps=ps: eng.tensor_copy(
                    sb["xT"].rearrange("p (dc s) -> p dc s", dc=DC)
                    [:, 4 * half:4 * half + 4, 128 * sc:128 * sc + 128],
                    ps.rearrange("p (q r) -> p q r", q=4)))
                inc("v", i)
        XT_V = c["v"]

        if T:
            wait("ldg", LD_ATTN)
            wait("v", XT_V)
        s0 = c["s"]
        for wi, (wsb, qdst, bcol) in enumerate(
                (("wq_sb", "qt", "bqc"), ("wk_sb", "kt", "bkc"))):
            for oc in range(4):
                idx = 4 * wi + oc
                if T and idx >= 1:
                    wait("s", s0 + idx)
                last = None
                for n4 in range(4):
                    for dc in range(DC):
                        last = g_(T, lambda n4=n4, dc=dc, oc=oc, wsb=wsb: eng.matmul(
                            sb["Lp"][:, 512 * n4:512 * n4 + 512],
                            sb[wsb][:, 512 * dc + 128 * oc:512 * dc + 128 * oc + 128],
                            sb["xT"][:, 2048 * dc + 512 * n4:2048 * dc + 512 * n4 + 512],
                            start=(dc == 0), stop=(dc == DC - 1)))
                inc("t", last)
                if SS:
                    wait("t", c["t"])
                i = g_(SS, lambda oc=oc, qdst=qdst, bcol=bcol: eng.activation(
                    sb[qdst][:, 2048 * oc:2048 * oc + 2048], sb["Lp"][:, :],
                    AF.Identity, bias=sb[bcol][:, oc:oc + 1], scale=1.0))
                inc("s", i)
        vv0 = c["v"]
        for sc in range(SC):
            ph = sc % 2
            ps = sb["Cp"][:, 512 * ph:512 * ph + 512]
            if T and sc >= 2:
                wait("v", vv0 + sc - 1)
            last = None
            for dc in range(DC):
                last = g_(T, lambda dc=dc, sc=sc, ps=ps: eng.matmul(
                    ps,
                    sb["xT"][:, 2048 * dc + 128 * sc:2048 * dc + 128 * sc + 128],
                    sb["wv_sb"][:, 512 * dc:512 * dc + 512],
                    start=(dc == 0), stop=(dc == DC - 1)))
            inc("t", last)
            if V:
                wait("t", c["t"])
            i = g_(V, lambda sc=sc, ps=ps: eng.tensor_tensor(
                sb["vna"][:, 512 * sc:512 * sc + 512], ps, sb["bvb"][:, :], OP.add))
            inc("v", i)
        PROJ_V = c["v"]; PROJ_S = c["s"]
        if upto == 1:
            tapstop()

        head_lgs = {}
        def snk_groups(parity):
            if parity == 0:
                return [sb["Cp"][:, 0:4], sb["Cp"][:, 512:516],
                        sb["S0"][:, 0:4], sb["S1"][:, 0:4]]
            return [sb["Lp"][:, 0:4], sb["Lp"][:, 512:516],
                    sb["Lp"][:, 1024:1028], sb["Lp"][:, 1536:1540]]

        def emit_logits2(h, transposed, av_h=None, p0_h=None):
            oc = h // 2
            p0 = 64 * (h % 2)
            src_l, src_r = ("kt", "qt") if transposed else ("qt", "kt")
            dstbuf = "Tbuf" if transposed else "Abuf"
            LT = c["lgt"]; LS = c["lgs"]
            if av_h is not None:
                AVB = c["avs"]
                hpav = av_h % 2
            if p0_h is not None:
                hp0 = p0_h % 2
                groups0 = snk_groups(0)
                vbase0 = c["snkv"]
                if T:
                    wait("avs", c["avs"])
            last0 = None
            for qc in range(KC):
                if av_h is not None:
                    phav = qc % 2
                    psav = sb["Cp"][:, 512 * phav:512 * phav + 64]
                    if T:
                        if qc == 0:
                            wait("vt", c["vt"])
                        if qc >= 2:
                            wait("avs", AVB + qc - 1)
                    lastav = None
                    for kc in range(KC):
                        lastav = g_(T, lambda kc=kc, qc=qc, psav=psav: eng.matmul(
                            psav,
                            sb["Tbuf"][:, 2048 * kc + 128 * qc:2048 * kc + 128 * qc + 128],
                            sb["vt"][:, 64 * kc:64 * kc + 64],
                            start=(kc == 0), stop=(kc == KC - 1)))
                    inc("avt", lastav)
                    if V:
                        wait("avt", c["avt"])
                    i = g_(V, lambda av_h=av_h, qc=qc, hpav=hpav, psav=psav:
                           eng.tensor_scalar(
                               sb["obuf"][:, 512 * qc + 64 * av_h:512 * qc + 64 * av_h + 64],
                               psav, sb["wcol_f"][hpav][:, qc:qc + 1], None, OP.mult))
                    inc("avs", i)
                if T and qc >= 1:
                    wait("lgs", LS + 2 * (qc - 1) + 1)
                lastmm = None
                for n4 in (0, 1):
                    lastmm = g_(T, lambda n4=n4, qc=qc, oc=oc, p0=p0,
                                src_l=src_l, src_r=src_r: eng.matmul(
                        sb["Lp"][:, 512 * n4:512 * n4 + 512],
                        sb[src_l][p0:p0 + 64,
                                  2048 * oc + 128 * qc:2048 * oc + 128 * qc + 128],
                        sb[src_r][p0:p0 + 64,
                                  2048 * oc + 512 * n4:2048 * oc + 512 * n4 + 512],
                        start=True, stop=True))
                inc("lgt", lastmm)
                if T and qc >= 1:
                    wait("lgs", LS + 2 * (qc - 1) + 2)
                for n4 in (2, 3):
                    lastmm = g_(T, lambda n4=n4, qc=qc, oc=oc, p0=p0,
                                src_l=src_l, src_r=src_r: eng.matmul(
                        sb["Lp"][:, 512 * n4:512 * n4 + 512],
                        sb[src_l][p0:p0 + 64,
                                  2048 * oc + 128 * qc:2048 * oc + 128 * qc + 128],
                        sb[src_r][p0:p0 + 64,
                                  2048 * oc + 512 * n4:2048 * oc + 512 * n4 + 512],
                        start=True, stop=True))
                inc("lgt", lastmm)
                if SS:
                    wait("lgt", LT + 2 * qc + 1)
                    if transposed and qc == 0 and c["avt"] > 0:
                        wait("avt", c["avt"])
                i = g_(SS, lambda h=h, qc=qc, transposed=transposed, dstbuf=dstbuf:
                       eng.activation(
                           sb[dstbuf][:, 2048 * qc:2048 * qc + 1024],
                           sb["Lp"][:, 0:1024], AF.Exp, scale=0.125,
                           accum_out=(None if transposed
                                      else sb["wdenA"][h % 2][:, qc:qc + 1])))
                inc("lgs", i)
                if SS:
                    wait("lgt", LT + 2 * qc + 2)
                i = g_(SS, lambda h=h, qc=qc, transposed=transposed, dstbuf=dstbuf:
                       eng.activation(
                           sb[dstbuf][:, 2048 * qc + 1024:2048 * qc + 2048],
                           sb["Lp"][:, 1024:2048], AF.Exp, scale=0.125,
                           accum_out=(None if transposed
                                      else sb["wdenB"][h % 2][:, qc:qc + 1])))
                inc("lgs", i)
                if p0_h is not None:
                    j = qc
                    gidx = j // 4
                    reg = groups0[gidx]
                    for cc in range(16):
                        if T and j == 0 and cc % 4 == 0:
                            wait("snkv", vbase0 - 4 + (cc // 4) + 1)
                        last0 = g_(T, lambda j=j, cc=cc, reg=reg, hp0=hp0: eng.matmul(
                            reg[:, j % 4:j % 4 + 1],
                            sb["Abuf"][:, 2048 * cc + 128 * j:2048 * cc + 128 * j + 128],
                            sb["wcol_b"][hp0][:, cc:cc + 1],
                            start=(j % 4 == 0 and cc == 0),
                            stop=(j % 4 == 3 and cc == 15),
                            skip_group_check=True))
                    if j % 4 == 3:
                        inc("snkt", last0)
                        if V:
                            wait("snkt", c["snkt"])
                        i = g_(V, lambda hp0=hp0, gidx=gidx, reg=reg: eng.reciprocal(
                            sb["ucol_f"][hp0][:, 4 * gidx:4 * gidx + 4], reg[:, 0:4]))
                        if gidx >= 1:
                            i2 = g_(V, lambda hp0=hp0, g2=gidx - 1: eng.tensor_copy(
                                sb["ucol_b"][hp0][:, 4 * g2:4 * g2 + 4],
                                sb["ucol_f"][hp0][:, 4 * g2:4 * g2 + 4]))
                            inc("snkv", i2)
            if p0_h is not None:
                i3 = g_(V, lambda hp0=hp0: eng.tensor_copy(
                    sb["ucol_b"][hp0][:, 12:16], sb["ucol_f"][hp0][:, 12:16]))
                inc("snkv", i3)

        def emit_w1(h):
            hp = h % 2
            base = c["lgs"]
            def _a(g):
                return g_(V, lambda: eng.tensor_tensor(
                    sb["wsum"][hp][:, 4 * g:4 * g + 4],
                    sb["wdenA"][hp][:, 4 * g:4 * g + 4],
                    sb["wdenB"][hp][:, 4 * g:4 * g + 4], OP.add))
            def _r(g):
                return g_(V, lambda: eng.reciprocal(
                    sb["wcol_f"][hp][:, 4 * g:4 * g + 4],
                    sb["wsum"][hp][:, 4 * g:4 * g + 4]))
            def _c(g):
                return g_(V, lambda: eng.tensor_copy(
                    sb["wcol_b"][hp][:, 4 * g:4 * g + 4],
                    sb["wcol_f"][hp][:, 4 * g:4 * g + 4]))
            if V:
                wait("lgs", base - 32 + 8)
            _a(0)
            if V:
                wait("lgs", base - 32 + 16)
            _a(1)
            _r(0)
            if V:
                wait("lgs", base - 32 + 24)
            _a(2)
            _r(1)
            inc("snkv", _c(0))
            if V:
                wait("lgs", base - 32 + 32)
            _a(3)
            _r(2)
            inc("snkv", _c(1))
            _r(3)
            inc("snkv", _c(2))
            inc("snkv", _c(3))

        def emit_pass(h, p):
            hp = h % 2
            col = (p % 2 == 0)
            mat = "Abuf" if col else "Tbuf"
            src_b = "wcol_b" if col else "ucol_b"
            dst_f = "ucol_f" if col else "wcol_f"
            dst_b = "ucol_b" if col else "wcol_b"
            groups = snk_groups(p % 2)
            vbase = c["snkv"]
            if T and p == 1:
                wait("lgs", head_lgs[h])
            last = None
            for j in range(16):
                gidx = j // 4
                reg = groups[gidx]
                for cc in range(16):
                    if T and j == 0 and cc % 4 == 0:
                        wait("snkv", vbase - 4 + (cc // 4) + 1)
                    last = g_(T, lambda j=j, cc=cc, reg=reg, mat=mat,
                              src_b=src_b, hp=hp, gidx=gidx: eng.matmul(
                        reg[:, j % 4:j % 4 + 1],
                        sb[mat][:, 2048 * cc + 128 * j:2048 * cc + 128 * j + 128],
                        sb[src_b][hp][:, cc:cc + 1],
                        start=(j % 4 == 0 and cc == 0),
                        stop=(j % 4 == 3 and cc == 15),
                        skip_group_check=True))
                if j % 4 == 3:
                    inc("snkt", last)
                    if V:
                        wait("snkt", c["snkt"])
                    i = g_(V, lambda hp=hp, gidx=gidx, reg=reg, dst_f=dst_f:
                           eng.reciprocal(
                               sb[dst_f][hp][:, 4 * gidx:4 * gidx + 4],
                               reg[:, 0:4]))
                    if gidx >= 1:
                        i2 = g_(V, lambda hp=hp, g2=gidx - 1, dst_f=dst_f,
                                dst_b=dst_b: eng.tensor_copy(
                                    sb[dst_b][hp][:, 4 * g2:4 * g2 + 4],
                                    sb[dst_f][hp][:, 4 * g2:4 * g2 + 4]))
                        inc("snkv", i2)
            i3 = g_(V, lambda hp=hp, dst_f=dst_f, dst_b=dst_b: eng.tensor_copy(
                sb[dst_b][hp][:, 12:16], sb[dst_f][hp][:, 12:16]))
            inc("snkv", i3)

        def emit_vtilde(h):
            hp = h % 2
            last = None
            for kc in range(KC):
                last = g_(V, lambda kc=kc, h=h, hp=hp: eng.tensor_scalar(
                    sb["vt"][:, 64 * kc:64 * kc + 64],
                    sb["vna"][:, 512 * kc + 64 * h:512 * kc + 64 * h + 64],
                    sb["ucol_f"][hp][:, kc:kc + 1], None, OP.mult))
            inc("vt", last)

        def emit_av(h):
            hp = h % 2
            av0 = c["avs"]
            for qc in range(KC):
                ph = qc % 2
                ps = sb["Cp"][:, 512 * ph:512 * ph + 64]
                if T:
                    if qc == 0:
                        wait("vt", c["vt"])
                    if qc >= 2:
                        wait("avs", av0 + qc - 1)
                last = None
                for kc in range(KC):
                    last = g_(T, lambda kc=kc, qc=qc, ps=ps: eng.matmul(
                        ps,
                        sb["Tbuf"][:, 2048 * kc + 128 * qc:2048 * kc + 128 * qc + 128],
                        sb["vt"][:, 64 * kc:64 * kc + 64],
                        start=(kc == 0), stop=(kc == KC - 1)))
                inc("avt", last)
                if V:
                    wait("avt", c["avt"])
                i = g_(V, lambda h=h, qc=qc, hp=hp, ps=ps: eng.tensor_scalar(
                    sb["obuf"][:, 512 * qc + 64 * h:512 * qc + 64 * h + 64],
                    ps, sb["wcol_f"][hp][:, qc:qc + 1], None, OP.mult))
                inc("avs", i)

        if T:
            wait("s", PROJ_S)
            wait("v", PROJ_V)
        emit_logits2(0, False)
        emit_w1(0)
        emit_logits2(0, True, p0_h=0)
        head_lgs[0] = c["lgs"]
        for h in range(H):
            for p in range(1, NPASS):
                emit_pass(h, p)
            if upto == 21 and h == 0:
                tapstop()
            if upto == 210 + h:
                tapstop()
            emit_vtilde(h)
            if h + 1 >= H:
                emit_av(h)
            if upto == 2 and h == 0:
                emit_av(h)
                tapstop()
            if h + 1 < H:
                if TSAFE & 1:
                    emit_av(h)
                if SS:
                    wait("snkt", c["snkt"])
                emit_logits2(h + 1, False,
                             av_h=(None if (TSAFE & 1) else h))
                emit_w1(h + 1)
                emit_logits2(h + 1, True,
                             p0_h=(None if (TSAFE & 2) else h + 1))
                head_lgs[h + 1] = c["lgs"]
                if TSAFE & 2:
                    emit_pass(h + 1, 0)
                if upto == 220 + h:
                    tapstop()

        if T:
            wait("avs", c["avs"])
            wait("ldg", LD_ATTN)
        v3 = c["v"]
        idx3 = 0
        for gq in (2, 3, 1, 0):
            for hc in range(4):
                ph = idx3 % 2
                ps = sb["CpB"][:, 1024 * ph:1024 * ph + 512]
                if T and idx3 >= 2:
                    wait("v", v3 + idx3 - 1)
                lastt = None
                for q in range(4):
                    sc = 4 * gq + q
                    lastt = g_(T, lambda sc=sc, hc=hc, q=q, ps=ps: eng.transpose(
                        ps[:, 128 * q:128 * q + 128],
                        sb["obuf"][:, 512 * sc + 128 * hc:512 * sc + 128 * hc + 128],
                        sb["ident"][:, :]))
                inc("t", lastt)
                if V:
                    wait("t", c["t"])
                i = g_(V, lambda hc=hc, gq=gq, ps=ps: eng.tensor_copy(
                    sb["ot"][:, 2048 * hc + 512 * gq:2048 * hc + 512 * gq + 512], ps))
                inc("v", i)
                idx3 += 1
        OT_V = c["v"]

        for wc in range(4):
            if G and wc == 0:
                wait("v", v3 + 4)
            if G and wc == 2:
                wait("v", v3 + 8)
            i = g_(G, lambda wc=wc: eng.dma_start(
                out=sb["wo_sb"][:, 1024 * wc:1024 * wc + 1024],
                in_=dr["wo"][128 * wc:128 * wc + 128, :]))
            inc("ldg", i, 16)
        LD_WO = c["ldg"]
        if G:
            wait("v", v3 + 12)
        i = g_(G, lambda: wload(sb["fut_sb"], dr["fut"], D))
        inc("ldg", i, 16)
        if G:
            wait("v", OT_V)
        i = g_(G, lambda: wload(sb["fdt_sb"], dr["fdt"], HID))
        inc("ldg", i, 16)
        LD_FDT = c["ldg"]
        if G:
            wait("avt", 16 * H)
        i = g_(G, lambda: wload(sb["wet1"], dr["ewt"][0], D))
        inc("wet", i, 16)
        if upto == 29:
            tapstop()

        lsy3 = c["lsy"]
        s_ybf = {}; s_evac = {}; s_sqrt = {}
        v_stt = {}; v_eps = {}; v_mb = {}; t_wo = {}
        v_conv = {}; v_x1t = {}

        def xct_sl(dc, lo, ln):
            buf = sb["xct_a"] if dc < 4 else sb["xct_b"]
            base = 2048 * (dc % 4)
            return buf[:, base + lo:base + lo + ln]

        if V:
            wait("avt", 16 * H)
        i = g_(V, lambda: eng.memset(
            sb["yt"].rearrange("p (dc s) -> p dc s", s=YTS)[:, :, 0:2], 0.0))
        inc("v", i)
        vdrain()

        def ph3_stageA(sc):
            ph = sc % 2
            psW = sb["Lp"][:, 1024 * ph:1024 * ph + 1024]
            t1 = sb["t1f"][sc % 4]
            if SY and sc == 0:
                wait("snkt", c["snkt"])
            if SY and sc >= 4:
                wait("s", s_ybf[sc - 4])
            i = g_(SY, lambda sc=sc, t1=t1: eng.dma_start(
                out=t1[:, :], in_=dr["x"][128 * sc:128 * sc + 128, :]))
            inc("lsy", i, 16)
            if T:
                if sc == 0:
                    wait("v", OT_V)
                    wait("ldg", LD_WO)
                if sc >= 2:
                    wait("v", v_stt[sc - 2])
            last = None
            for n2 in range(2):
                last = g_(T, lambda n2=n2, psW=psW: eng.matmul(
                    psW[:, 512 * n2:512 * n2 + 512], sb["ones1"][:, :],
                    sb["bor_sb"][0:1, 512 * n2:512 * n2 + 512],
                    start=True, stop=False, skip_group_check=True))
                for hcc in range(4):
                    last = g_(T, lambda n2=n2, hcc=hcc, sc=sc, psW=psW: eng.matmul(
                        psW[:, 512 * n2:512 * n2 + 512],
                        sb["ot"][:, 2048 * hcc + 128 * sc:2048 * hcc + 128 * sc + 128],
                        sb["wo_sb"][:, 1024 * hcc + 512 * n2:1024 * hcc + 512 * n2 + 512],
                        start=False, stop=(hcc == 3), skip_group_check=True))
            inc("t", last)
            t_wo[sc] = c["t"]
            rcb = 4 * ph
            if V:
                wait("t", t_wo[sc])
                wait("lsy", lsy3 + (16 if sc == 0 else 32 * sc))
            i = g_(V, lambda t1=t1, psW=psW: eng.scalar_tensor_tensor(
                t1[:, :], psW, 1.0, t1[:, :], OP.mult, OP.add))
            inc("v", i)
            v_stt[sc] = c["v"]
            vdrain()
            i = g_(V, lambda t1=t1, ph=ph: eng.bn_stats(
                sb["bns"][:, 12 * ph:12 * ph + 6], t1[:, 0:512]))
            i = g_(V, lambda t1=t1, ph=ph: eng.bn_stats(
                sb["bns"][:, 12 * ph + 6:12 * ph + 12], t1[:, 512:1024]))
            vdrain()
            i = g_(V, lambda ph=ph: eng.bn_aggr(
                sb["bnagg"][:, 2 * ph:2 * ph + 2],
                sb["bns"][:, 12 * ph:12 * ph + 12]))
            vdrain()
            i = g_(V, lambda rcb=rcb, ph=ph: eng.tensor_scalar(
                sb["rc"][:, rcb:rcb + 1], sb["bnagg"][:, 2 * ph + 1:2 * ph + 2],
                1.0, 1e-5, OP.mult, OP.add))
            inc("v", i)
            v_eps[sc] = c["v"]

        def ph3_stageB(sc):
            rcb = 4 * (sc % 2)
            if SS:
                wait("v", v_eps[sc])
            i = g_(SS, lambda rcb=rcb: eng.activation(
                sb["rc"][:, rcb + 1:rcb + 2], sb["rc"][:, rcb:rcb + 1], AF.Sqrt))
            inc("s", i)
            s_sqrt[sc] = c["s"]

        def ph3_stageC(sc):
            ph = sc % 2
            rcb = 4 * ph
            if V:
                wait("s", s_sqrt[sc])
            i = g_(V, lambda rcb=rcb: eng.reciprocal(
                sb["rc"][:, rcb + 2:rcb + 3], sb["rc"][:, rcb + 1:rcb + 2]))
            vdrain()
            i = g_(V, lambda rcb=rcb, ph=ph: eng.tensor_scalar(
                sb["rc"][:, rcb + 3:rcb + 4], sb["bnagg"][:, 2 * ph:2 * ph + 1],
                sb["rc"][:, rcb + 2:rcb + 3], -1.0, OP.mult, OP.mult))
            inc("v", i)
            v_mb[sc] = c["v"]

        def ph3_stageD(sc):
            ph = sc % 2
            rcb = 4 * ph
            t1 = sb["t1f"][sc % 4]
            if SS:
                wait("v", v_mb[sc])
                if sc >= 2:
                    wait("lsy", lsy3 + 32 * (sc - 2) + 48)
            i = g_(SS, lambda ph=ph, t1=t1, rcb=rcb: eng.activation(
                sb["yf32"][ph][:, :], t1[:, :], AF.Identity,
                scale=sb["rc"][:, rcb + 2:rcb + 3], bias=sb["rc"][:, rcb + 3:rcb + 4]))
            inc("s", i)
            i = g_(SS, lambda sc=sc, t1=t1, rcb=rcb: eng.activation(
                sb["ybf"][sc % 4][:, :], t1[:, :], AF.Identity,
                scale=sb["rc"][:, rcb + 2:rcb + 3], bias=sb["rc"][:, rcb + 3:rcb + 4]))
            inc("s", i)
            s_ybf[sc] = c["s"]
            if SY:
                wait("s", s_ybf[sc])
            i = g_(SY, lambda sc=sc, ph=ph: eng.dma_start(
                out=yspill[128 * sc:128 * sc + 128, :], in_=sb["yf32"][ph][:, :]))
            inc("lsy", i, 16)

        def ph3_back(sc):
            yb = sb["ybf"][sc % 4]
            for half in range(2):
                ps = sb["CpB"][:, 1024 * half:1024 * half + 512]
                if T:
                    wait("s", max(s_ybf[sc], s_evac.get((sc - 1, half), 0)))
                lastt = None
                for q in range(4):
                    dc = 4 * half + q
                    lastt = g_(T, lambda yb=yb, dc=dc, q=q, ps=ps: eng.transpose(
                        ps[:, 128 * q:128 * q + 128],
                        yb[:, 128 * dc:128 * dc + 128],
                        sb["ident"][:, :]))
                inc("t", lastt)
                if SS:
                    wait("t", c["t"])
                i = g_(SS, lambda sc=sc, half=half, ps=ps: eng.activation(
                    sb["yt"].rearrange("p (dc s) -> p dc s", s=YTS)
                    [:, 4 * half:4 * half + 4, 2 + 128 * sc:2 + 128 * sc + 128],
                    ps.rearrange("p (q r) -> p q r", q=4), AF.Copy))
                inc("s", i)
                s_evac[(sc, half)] = c["s"]

        def conv0_part(dc):
            ca = sb["cacc0"]
            y0 = YTS * dc
            if V and dc == 0:
                wait("ldg", LD_SM2)
                wait("s", s_evac[(3, 1)])
            vdrain()
            i = g_(V, lambda ca=ca, y0=y0, dc=dc: eng.tensor_scalar(
                ca[:, :], sb["yt"][:, y0:y0 + 512],
                sb["dwc"][:, 3 * dc:3 * dc + 1], sb["dwbc"][:, dc:dc + 1],
                OP.mult, OP.add))
            vdrain()
            i = g_(V, lambda ca=ca, y0=y0, dc=dc: eng.scalar_tensor_tensor(
                ca[:, :], sb["yt"][:, y0 + 1:y0 + 1 + 512],
                sb["dwc"][:, 3 * dc + 1:3 * dc + 2], ca[:, :], OP.mult, OP.add))
            vdrain()
            i = g_(V, lambda ca=ca, y0=y0, dc=dc: eng.scalar_tensor_tensor(
                xct_sl(dc, 0, 512),
                sb["yt"][:, y0 + 2:y0 + 2 + 512],
                sb["dwc"][:, 3 * dc + 2:3 * dc + 3], ca[:, :], OP.mult, OP.add))
            if dc == DC - 1:
                inc("v", i)
                v_conv[0] = c["v"]

        for sc in range(SC):
            ph3_stageA(sc)
            ph3_stageB(sc)
            if sc >= 1:
                ph3_stageC(sc - 1)
                ph3_stageD(sc - 1)
            if sc >= 8:
                conv0_part(sc - 8)
            if sc >= 3:
                ph3_back(sc - 3)
            if upto == 31 and sc == 3:
                tapstop()
        ph3_stageC(SC - 1)
        ph3_stageD(SC - 1)
        ph3_back(SC - 3)
        ph3_back(SC - 2)
        ph3_back(SC - 1)
        LSY3_END = c["lsy"]
        PH3_V = c["v"]
        if upto == 3:
            tapstop()

        LD_PWT0 = c["ldg"]
        pwt_pos = {}
        for pi, wc in enumerate((0, 2, 4, 6, 1, 3, 5, 7)):
            if G and pi == 0:
                wait("t", t_wo[7])
            if G and pi == 4:
                wait("t", t_wo[SC - 1])
            i = g_(G, lambda wc=wc: eng.dma_start(
                out=sb["pwt_sb"][:, 1024 * wc:1024 * wc + 1024],
                in_=dr["pwt"][128 * wc:128 * wc + 128, :]))
            inc("ldg", i, 16)
            pwt_pos[wc] = c["ldg"]
        LD_PWT = c["ldg"]

        cacc = sb["trash"].bitcast(F32)

        def conv_group(n4):
            lo = 512 * n4
            if V:
                wait("s", s_evac[(4 * n4 + 3, 1)])
            i = None
            for dc in range(DC):
                y0 = YTS * dc + lo
                vdrain()
                i = g_(V, lambda y0=y0, dc=dc: eng.tensor_scalar(
                    cacc[:, :], sb["yt"][:, y0:y0 + 512],
                    sb["dwc"][:, 3 * dc:3 * dc + 1], sb["dwbc"][:, dc:dc + 1],
                    OP.mult, OP.add))
                vdrain()
                i = g_(V, lambda y0=y0, dc=dc: eng.scalar_tensor_tensor(
                    cacc[:, :], sb["yt"][:, y0 + 1:y0 + 1 + 512],
                    sb["dwc"][:, 3 * dc + 1:3 * dc + 2], cacc[:, :],
                    OP.mult, OP.add))
                vdrain()
                i = g_(V, lambda y0=y0, dc=dc, lo=lo: eng.scalar_tensor_tensor(
                    xct_sl(dc, lo, 512),
                    sb["yt"][:, y0 + 2:y0 + 2 + 512],
                    sb["dwc"][:, 3 * dc + 2:3 * dc + 3], cacc[:, :],
                    OP.mult, OP.add))
            inc("v", i)
            v_conv[n4] = c["v"]

        if upto == 41:
            conv_group(1); conv_group(2); conv_group(3)
            tapstop()

        s4 = c["s"]
        for n4 in range(4):
            if n4 + 1 < 4:
                conv_group(n4 + 1)
            for ec in range(DC):
                idx = 8 * n4 + ec
                ph = idx % 2
                ps = sb["Lp"][:, 512 * ph:512 * ph + 512]
                if T:
                    if ec == 0:
                        wait("v", v_conv[n4])
                    if idx >= 2:
                        wait("s", s4 + idx - 1)
                last = None
                for dc in range(DC):
                    if T and n4 == 0:
                        wait("ldg", pwt_pos[dc])
                    last = g_(T, lambda dc=dc, ec=ec, n4=n4, ps=ps: eng.matmul(
                        ps,
                        sb["pwt_sb"][:, 1024 * dc + 128 * ec:1024 * dc + 128 * ec + 128],
                        xct_sl(dc, 512 * n4, 512),
                        start=(dc == 0), stop=(dc == DC - 1)))
                inc("t", last)
                if SS:
                    wait("t", c["t"])
                i = g_(SS, lambda ec=ec, n4=n4, ps=ps: eng.activation(
                    sb["xc2t"][:, 2048 * ec + 512 * n4:2048 * ec + 512 * n4 + 512],
                    ps, AF.Silu, bias=sb["pwbc"][:, ec:ec + 1]))
                inc("s", i)
                if V:
                    wait("s", c["s"])
                i = g_(V, lambda ec=ec, n4=n4: eng.tensor_tensor(
                    sb["x1t"][:, 2048 * ec + 512 * n4:2048 * ec + 512 * n4 + 512],
                    sb["xc2t"][:, 2048 * ec + 512 * n4:2048 * ec + 512 * n4 + 512],
                    sb["yt"][:, YTS * ec + 2 + 512 * n4:YTS * ec + 2 + 512 * n4 + 512],
                    OP.add))
                inc("v", i)
                if ec == DC - 1:
                    v_x1t[n4] = c["v"]
        PW_V = c["v"]
        PW_S = c["s"]
        t_pwend = c["t"]
        if upto == 42:
            tapstop()

        if G:
            wait("t", t_pwend)
        i = g_(G, lambda: wload(sb["wet0"], dr["ewt"][1], D))
        inc("wet", i, 16)

        if V:
            wait("t", t_pwend)
        for pc in range(4):
            i = g_(V, lambda pc=pc: eng.memset(sb["combP"][pc][:, :], 0.0))
            if pc == 3:
                inc("v", i)
        CB0_V = c["v"]

        rtr_inline = (upto == 4)
        T5R = c["t"]
        S5R = c["s"]
        v_sub = {}

        def comb_sl(sc):
            return sb["combP"][sc // 4][:, 1024 * (sc % 4):1024 * (sc % 4) + 1024]

        def router_mm(sc):
            ph = sc % 2
            ps = (sb["S0"] if ph == 0 else sb["S1"])[:, 0:8]
            if T:
                if sc == 0:
                    wait("v", PW_V)
                if sc >= 2:
                    wait("v", CB0_V + 3 * (sc - 2) + 1)
            last = None
            for dc in range(DC):
                last = g_(T, lambda dc=dc, sc=sc, ps=ps: eng.matmul(
                    ps,
                    sb["x1t"][:, 2048 * dc + 128 * sc:2048 * dc + 128 * sc + 128],
                    sb["rwt_sb"][:, 8 * dc:8 * dc + 8],
                    start=(dc == 0), stop=(dc == DC - 1)))
            inc("t", last)

        s4r = c["s"]
        for sc in range(SC):
            ph = sc % 2
            ps = (sb["S0"] if ph == 0 else sb["S1"])[:, 0:8]
            if rtr_inline:
                router_mm(sc)
            if V:
                wait("t", T5R + sc + 1)
            i = g_(V, lambda ps=ps: eng.tensor_tensor(
                sb["rlg"][:, :], ps, sb["rbb"][:, :], OP.add))
            inc("v", i)
            vdrain()
            if RSAFE & 1:
                i = g_(V, lambda: eng.tensor_copy(sb["rmx"][:, :], sb["rlg"][:, :]))
            else:
                i = g_(V, lambda: eng.max(sb["rmx"][:, :], sb["rlg"][:, :]))
            vdrain()
            i = g_(V, lambda: eng.tensor_tensor(
                sb["rc"][:, 8:9], sb["rmx"][:, 0:1], sb["rmx"][:, 1:2], OP.subtract))
            inc("v", i)
            v_sub[sc] = c["v"]
            if rtr_inline:
                if SS:
                    wait("v", v_sub[sc])
                i = g_(SS, lambda: eng.activation(
                    sb["rc"][:, 9:10], sb["rc"][:, 8:9],
                    AF.Identity if (RSAFE & 4) else AF.Sigmoid))
                inc("s", i)
            if V:
                wait("s", c["s"] if rtr_inline else S5R + sc + 1)
            i = g_(V, lambda: eng.tensor_scalar(
                sb["rc"][:, 10:11], sb["rc"][:, 9:10], -1.0, 1.0, OP.mult, OP.add))
            if RSAFE & 2:
                i = g_(V, lambda: eng.tensor_copy(sb["ris"][:, 0:8], sb["rlg"][:, :]))
                i = g_(V, lambda: eng.tensor_copy(sb["ris"][:, 8:16], sb["rlg"][:, :]))
            else:
                i = g_(V, lambda: eng.tensor_scalar(
                    sb["ris"][:, 0:8], sb["rlg"][:, :], sb["rmx"][:, 0:1],
                    sb["rc"][:, 9:10], OP.is_equal, OP.mult))
                i = g_(V, lambda: eng.tensor_scalar(
                    sb["ris"][:, 8:16], sb["rlg"][:, :], sb["rmx"][:, 1:2],
                    sb["rc"][:, 10:11], OP.is_equal, OP.mult))
            vdrain()
            i = g_(V, lambda sc=sc: eng.tensor_tensor(
                sb["rw"][:, 8 * sc:8 * sc + 8], sb["ris"][:, 0:8],
                sb["ris"][:, 8:16], OP.add))
            inc("v", i)
            vdrain()
            if not rtr_inline:
                if V:
                    wait("yes", sc + 1)
                i = g_(V, lambda sc=sc: eng.scalar_tensor_tensor(
                    comb_sl(sc),
                    sb["zbuf"][sc % 2][:, :], sb["rw"][:, 8 * sc:8 * sc + 1],
                    comb_sl(sc), OP.mult, OP.add))
                inc("yev", i)
        if upto == 4:
            tapstop()

        for e in range(E):
            for sc in range(SC):
                it = e * SC + sc
                ph = it % 2
                ps = sb["Lp"][:, 1024 * ph:1024 * ph + 1024]
                if T:
                    if it == 0:
                        wait("s", PW_S)
                        wait("ldg", LD_SM2)
                    if e == 0 and sc % 4 == 0:
                        wait("v", v_x1t[sc // 4])
                    if sc == 0:
                        wait("wet", 16 * (e + 1))
                    if it >= 2:
                        wait("yes", it - 1)
                last = None
                for n2 in range(2):
                    last = g_(T, lambda e=e, n2=n2, ps=ps: eng.matmul(
                        ps[:, 512 * n2:512 * n2 + 512],
                        sb["esel"][:, 128 * e:128 * e + 128],
                        sb["ebr2"][:, 512 * n2:512 * n2 + 512],
                        start=True, stop=False, skip_group_check=True))
                    for dc in range(DC):
                        last = g_(T, lambda e=e, dc=dc, sc=sc, n2=n2, ps=ps: eng.matmul(
                            ps[:, 512 * n2:512 * n2 + 512],
                            sb["x1t"][:, 2048 * dc + 128 * sc:2048 * dc + 128 * sc + 128],
                            sb["wet"][e % 2][:, 1024 * dc + 512 * n2:
                                             1024 * dc + 512 * n2 + 512],
                            start=False, stop=(dc == DC - 1), skip_group_check=True))
                inc("yet", last)
                if e == 0 and not rtr_inline:
                    router_mm(sc)
                    if SS:
                        wait("v", v_sub[sc])
                    i = g_(SS, lambda: eng.activation(
                        sb["rc"][:, 9:10], sb["rc"][:, 8:9],
                        AF.Identity if (RSAFE & 4) else AF.Sigmoid))
                    inc("s", i)
                if SS:
                    wait("yet", c["yet"])
                    if it >= 2:
                        wait("yev", it - 1)
                i = g_(SS, lambda ph=ph, ps=ps: eng.activation(
                    sb["zbuf"][ph][:, :], ps, AF.Silu))
                inc("yes", i)
                if e >= 1 or rtr_inline:
                    if V:
                        wait("yes", c["yes"])
                    i = g_(V, lambda e=e, sc=sc, ph=ph: eng.scalar_tensor_tensor(
                        comb_sl(sc),
                        sb["zbuf"][ph][:, :], sb["rw"][:, 8 * sc + e:8 * sc + e + 1],
                        comb_sl(sc), OP.mult, OP.add))
                    inc("yev", i)
            if e + 2 < E:
                if G:
                    wait("yet", 16 * (e + 1))
                i = g_(G, lambda e=e: wload(sb["wet"][e % 2], dr["ewt"][e + 2], D))
                inc("wet", i, 16)
        EXP_V = c["yev"]

        if upto == 5:
            tapstop()
        LD_OWT0 = c["ldg"]
        for wc in range(DC):
            if G and wc == 0:
                wait("yet", 16 * E)
            i = g_(G, lambda wc=wc: eng.dma_start(
                out=sb["owt_sb"][:, 1024 * wc:1024 * wc + 1024],
                in_=dr["owt"][128 * wc:128 * wc + 128, :]))
            inc("ldg", i, 16)
        LD_OWT = c["ldg"]

        t6 = c["t"]
        for sc in range(SC):
            if V:
                if sc == 0:
                    wait("yev", EXP_V)
                if sc >= 2:
                    wait("t", t6 + 2 * (sc - 2) + 2)
            i = g_(V, lambda sc=sc: eng.tensor_copy(
                sb["cbf"][sc % 2][:, :], comb_sl(sc)))
            inc("v", i)
            for half in range(2):
                ps = sb["CpB"][:, 1024 * half:1024 * half + 512]
                if T:
                    wait("v", c["v"])
                lastt = None
                for q in range(4):
                    dc = 4 * half + q
                    lastt = g_(T, lambda sc=sc, dc=dc, q=q, ps=ps: eng.transpose(
                        ps[:, 128 * q:128 * q + 128],
                        sb["cbf"][sc % 2][:, 128 * dc:128 * dc + 128],
                        sb["ident"][:, :]))
                inc("t", lastt)
                if V:
                    wait("t", c["t"])
                i = g_(V, lambda sc=sc, half=half, ps=ps: eng.tensor_copy(
                    sb["combT"].rearrange("p (dc s) -> p dc s", dc=DC)
                    [:, 4 * half:4 * half + 4, 128 * sc:128 * sc + 128],
                    ps.rearrange("p (q r) -> p q r", q=4)))
                inc("v", i)
        CBT_V = c["v"]

        if V:
            wait("ldg", LD_FDT)
        last = None
        for dc in range(DC):
            last = g_(V, lambda dc=dc: eng.tensor_scalar(
                sb["fdt_sb"][:, 256 * dc:256 * dc + 256],
                sb["fdt_sb"][:, 256 * dc:256 * dc + 256],
                sb["nwc"][:, dc:dc + 1], None, OP.mult))
        inc("v", last)

        s6 = c["s"]
        v6 = c["v"]
        for sc in range(SC):
            ph = sc % 2
            psM = sb["Lp"][:, 1024 * ph:1024 * ph + 1024]
            hp = sb["hpf"][ph]
            rcb = 16 + 4 * ph
            if G:
                if sc == 0:
                    wait("lsy", LSY3_END)
                if sc < 2:
                    wait("v", CBT_V)
                else:
                    wait("s", s6 + 3 * (sc - 2) + 3)
            i = g_(G, lambda hp=hp, sc=sc: eng.dma_start(
                out=hp[:, :], in_=yspill[128 * sc:128 * sc + 128, :]))
            inc("ldy", i, 16)
            if T:
                if sc == 0:
                    wait("v", CBT_V)
                if sc >= 2:
                    wait("v", v6 + 3 * (sc - 2) + 1)
            last = None
            for n2 in range(2):
                last = g_(T, lambda n2=n2, psM=psM: eng.matmul(
                    psM[:, 512 * n2:512 * n2 + 512], sb["ones1"][:, :],
                    sb["obr_sb"][0:1, 512 * n2:512 * n2 + 512],
                    start=True, stop=False, skip_group_check=True))
                for dc in range(DC):
                    if T and sc == 0 and n2 == 0:
                        wait("ldg", LD_OWT0 + 16 * (dc + 1))
                    last = g_(T, lambda dc=dc, sc=sc, n2=n2, psM=psM: eng.matmul(
                        psM[:, 512 * n2:512 * n2 + 512],
                        sb["combT"][:, 2048 * dc + 128 * sc:2048 * dc + 128 * sc + 128],
                        sb["owt_sb"][:, 1024 * dc + 512 * n2:1024 * dc + 512 * n2 + 512],
                        start=False, stop=(dc == DC - 1), skip_group_check=True))
            for dc in range(DC):
                last = g_(T, lambda dc=dc, sc=sc, psM=psM: eng.matmul(
                    psM[:, 128 * dc:128 * dc + 128],
                    sb["xc2t"][:, 2048 * dc + 128 * sc:2048 * dc + 128 * sc + 128],
                    sb["ident"][:, :],
                    start=False, stop=(dc == DC - 1), skip_group_check=True))
            inc("t", last)
            if V:
                wait("t", c["t"])
                wait("ldy", c["ldy"])
            i = g_(V, lambda hp=hp, psM=psM: eng.scalar_tensor_tensor(
                hp[:, :], psM, 1.0, hp[:, :], OP.mult, OP.add))
            inc("v", i)
            if SS:
                wait("v", c["v"])
            i = g_(SS, lambda hp=hp, rcb=rcb: eng.activation(
                sb["trash"][:, :], hp[:, :], AF.Square,
                accum_out=sb["rc"][:, rcb:rcb + 1]))
            inc("s", i)
            if V:
                wait("s", c["s"])
            i = g_(V, lambda rcb=rcb: eng.tensor_scalar(
                sb["rc"][:, rcb + 1:rcb + 2], sb["rc"][:, rcb:rcb + 1],
                1.0 / D, 1e-6, OP.mult, OP.add))
            inc("v", i)
            if SS:
                wait("v", c["v"])
            i = g_(SS, lambda rcb=rcb: eng.activation(
                sb["rc"][:, rcb + 2:rcb + 3], sb["rc"][:, rcb + 1:rcb + 2], AF.Sqrt))
            inc("s", i)
            if V:
                wait("s", c["s"])
            i = g_(V, lambda rcb=rcb: eng.reciprocal(
                sb["rc"][:, rcb + 3:rcb + 4], sb["rc"][:, rcb + 2:rcb + 3]))
            inc("v", i)
            if SS:
                wait("v", c["v"])
            i = g_(SS, lambda sc=sc, hp=hp, rcb=rcb: eng.activation(
                sb["t2bf"][:, 1024 * sc:1024 * sc + 1024], hp[:, :],
                AF.Identity, scale=sb["rc"][:, rcb + 3:rcb + 4]))
            inc("s", i)
        RMS_S = c["s"]

        T6E = c["t"]
        if V:
            wait("t", T6E)
        lastd = None
        for dc in range(DC):
            lastd = g_(V, lambda dc=dc: eng.tensor_scalar(
                sb["nwdiag"][:, 128 * dc:128 * dc + 128], sb["ident"][:, :],
                sb["nwc"][:, dc:dc + 1], None, OP.mult))
        inc("v", lastd)
        NWD_V = c["v"]

        v6t = c["v"]
        for sc in range(SC):
            for half in range(2):
                idx = 2 * sc + half
                ps = sb["CpB"][:, 1024 * half:1024 * half + 512]
                if T:
                    if idx == 0:
                        wait("s", RMS_S)
                    if idx >= 2:
                        wait("v", v6t + idx - 1)
                lastt = None
                for q in range(4):
                    dc = 4 * half + q
                    lastt = g_(T, lambda sc=sc, dc=dc, q=q, ps=ps: eng.transpose(
                        ps[:, 128 * q:128 * q + 128],
                        sb["t2bf"][:, 1024 * sc + 128 * dc:1024 * sc + 128 * dc + 128],
                        sb["ident"][:, :]))
                inc("t", lastt)
                if V:
                    wait("t", c["t"])
                i = g_(V, lambda sc=sc, half=half, ps=ps: eng.tensor_copy(
                    sb["t2t"].rearrange("p (dc s) -> p dc s", dc=DC)
                    [:, 4 * half:4 * half + 4, 128 * sc:128 * sc + 128],
                    ps.rearrange("p (q r) -> p q r", q=4)))
                inc("v", i)
        T2T_V = c["v"]

        if upto == 6:
            tapstop()
        s7 = c["s"]
        for sc in range(SC):
            ph = sc % 2
            psF = (sb["S0"] if ph == 0 else sb["S1"])[:, 0:256]
            if T:
                if sc == 0:
                    wait("v", T2T_V)
                if sc >= 2:
                    wait("s", s7 + sc - 1)
            last = g_(T, lambda psF=psF: eng.matmul(
                psF, sb["ones1"][:, :], sb["fdbr_sb"][0:1, :],
                start=True, stop=False, skip_group_check=True))
            for dc in range(DC):
                last = g_(T, lambda dc=dc, sc=sc, psF=psF: eng.matmul(
                    psF,
                    sb["t2t"][:, 2048 * dc + 128 * sc:2048 * dc + 128 * sc + 128],
                    sb["fdt_sb"][:, 256 * dc:256 * dc + 256],
                    start=False, stop=(dc == DC - 1), skip_group_check=True))
            inc("t", last)
            if SS:
                wait("t", c["t"])
            i = g_(SS, lambda sc=sc, psF=psF: eng.activation(
                sb["ff1"][:, 256 * sc:256 * sc + 256], psF, AF.Gelu))
            inc("s", i)
        FF1_S = c["s"]

        v7 = c["v"]
        for sc in range(SC):
            ph = sc % 2
            ps = sb["CpB"][:, 1024 * ph:1024 * ph + 256]
            if T:
                if sc == 0:
                    wait("s", FF1_S)
                if sc >= 2:
                    wait("v", v7 + sc - 1)
            lastt = None
            for q in range(2):
                lastt = g_(T, lambda sc=sc, q=q, ps=ps: eng.transpose(
                    ps[:, 128 * q:128 * q + 128],
                    sb["ff1"][:, 256 * sc + 128 * q:256 * sc + 128 * q + 128],
                    sb["ident"][:, :]))
            inc("t", lastt)
            if V:
                wait("t", c["t"])
            i = g_(V, lambda sc=sc, ps=ps: eng.tensor_copy(
                sb["ff1t"].rearrange("p (hc s) -> p hc s", hc=2)
                [:, :, 128 * sc:128 * sc + 128],
                ps.rearrange("p (q r) -> p q r", q=2)))
            inc("v", i)
        FF1T_V = c["v"]

        s7b = c["s"]
        v7b = c["v"]
        for sc in range(SC):
            ph = sc % 2
            psZ = sb["Lp"][:, 1024 * ph:1024 * ph + 1024]
            zf = sb["hpf"][ph]
            rcb = 24 + 4 * ph
            if G:
                if sc < 2:
                    wait("s", RMS_S)
                else:
                    wait("s", s7b + 2 * (sc - 2) + 2)
            i = g_(G, lambda zf=zf, sc=sc: eng.dma_start(
                out=zf[:, :], in_=yspill[128 * sc:128 * sc + 128, :]))
            inc("ldy", i, 16)
            if T:
                if sc == 0:
                    wait("v", FF1T_V)
                if sc >= 2:
                    wait("v", v7b + 3 * (sc - 2) + 1)
            last = None
            for n2 in range(2):
                last = g_(T, lambda n2=n2, psZ=psZ: eng.matmul(
                    psZ[:, 512 * n2:512 * n2 + 512], sb["ones1"][:, :],
                    sb["fubr_sb"][0:1, 512 * n2:512 * n2 + 512],
                    start=True, stop=False, skip_group_check=True))
                for hcc in range(2):
                    last = g_(T, lambda hcc=hcc, sc=sc, n2=n2, psZ=psZ: eng.matmul(
                        psZ[:, 512 * n2:512 * n2 + 512],
                        sb["ff1t"][:, 2048 * hcc + 128 * sc:2048 * hcc + 128 * sc + 128],
                        sb["fut_sb"][:, 1024 * hcc + 512 * n2:1024 * hcc + 512 * n2 + 512],
                        start=False, stop=(hcc == 1), skip_group_check=True))
            for dc in range(DC):
                last = g_(T, lambda dc=dc, sc=sc, psZ=psZ: eng.matmul(
                    psZ[:, 128 * dc:128 * dc + 128],
                    sb["t2t"][:, 2048 * dc + 128 * sc:2048 * dc + 128 * sc + 128],
                    sb["nwdiag"][:, 128 * dc:128 * dc + 128],
                    start=False, stop=(dc == DC - 1), skip_group_check=True))
            inc("t", last)
            if V:
                wait("t", c["t"])
                wait("ldy", c["ldy"])
            i = g_(V, lambda zf=zf, psZ=psZ: eng.scalar_tensor_tensor(
                zf[:, :], psZ, 1.0, zf[:, :], OP.mult, OP.add))
            inc("v", i)
            vdrain()
            i = g_(V, lambda zf=zf: eng.bn_stats(sb["bns"][:, 0:6], zf[:, 0:512]))
            i = g_(V, lambda zf=zf: eng.bn_stats(sb["bns"][:, 6:12], zf[:, 512:1024]))
            vdrain()
            i = g_(V, lambda: eng.bn_aggr(sb["bnagg"][:, 0:2], sb["bns"][:, 0:12]))
            vdrain()
            i = g_(V, lambda rcb=rcb: eng.tensor_scalar(
                sb["rc"][:, rcb:rcb + 1], sb["bnagg"][:, 1:2], 1.0, 1e-5,
                OP.mult, OP.add))
            inc("v", i)
            if SS:
                wait("v", c["v"])
            i = g_(SS, lambda rcb=rcb: eng.activation(
                sb["rc"][:, rcb + 1:rcb + 2], sb["rc"][:, rcb:rcb + 1], AF.Sqrt))
            inc("s", i)
            if V:
                wait("s", c["s"])
            i = g_(V, lambda rcb=rcb: eng.reciprocal(
                sb["rc"][:, rcb + 2:rcb + 3], sb["rc"][:, rcb + 1:rcb + 2]))
            vdrain()
            i = g_(V, lambda rcb=rcb: eng.tensor_scalar(
                sb["rc"][:, rcb + 3:rcb + 4], sb["bnagg"][:, 0:1],
                sb["rc"][:, rcb + 2:rcb + 3], -1.0, OP.mult, OP.mult))
            inc("v", i)
            if SS:
                wait("v", c["v"])
                if sc >= 2:
                    wait("sto", 16 * (sc - 1))
            i = g_(SS, lambda ph=ph, zf=zf, rcb=rcb: eng.activation(
                sb["ostg"][ph][:, :], zf[:, :], AF.Identity,
                scale=sb["rc"][:, rcb + 2:rcb + 3], bias=sb["rc"][:, rcb + 3:rcb + 4]))
            inc("s", i)
            if SY:
                wait("s", c["s"])
            i = g_(SY, lambda sc=sc, ph=ph: eng.dma_start(
                out=out_h[128 * sc:128 * sc + 128, :], in_=sb["ostg"][ph][:, :]))
            inc("sto", i, 16)

        if upto == 7:
            tapstop()
        if SY:
            wait("sto", c["sto"])

    import contextlib
    with nc.Block() as block, contextlib.ExitStack() as st:
        sems = {}
        for n in SEM_NAMES:
            sems[n] = st.enter_context(nc.semaphore(n))
        ctx["sems"] = sems

        def sbuf(name, shape, dt):
            ctx[name] = st.enter_context(nc.sbuf_tensor(name, shape, dt))
            return ctx[name]

        def psum(name, shape):
            ctx[name] = st.enter_context(nc.psum_tensor(name, shape, F32))

        sbuf("ident", [128, 128], BF16)
        sbuf("ones1", [1, 128], BF16)
        sbuf("bqc", [128, 4], F32); sbuf("bkc", [128, 4], F32)
        sbuf("bvb", [128, HD], F32)
        sbuf("dwc", [128, DC * 3], F32); sbuf("dwbc", [128, DC], F32)
        sbuf("pwbc", [128, DC], F32); sbuf("rbb", [128, E], F32)
        sbuf("nwc", [128, DC], F32)
        sbuf("ebr2", [8, D], BF16)
        sbuf("esel", [8, E * 128], BF16)
        sbuf("obr_sb", [1, D], BF16)
        sbuf("bor_sb", [1, D], BF16)
        sbuf("fdbr_sb", [1, HID], BF16); sbuf("fubr_sb", [1, D], BF16)
        sbuf("rwt_sb", [128, DC * E], BF16)
        sbuf("rc", [128, 32], F32)
        sbuf("bns", [128, 24], F32); sbuf("bnagg", [128, 4], F32)
        sbuf("rlg", [128, 8], F32); sbuf("rmx", [128, 8], F32)
        sbuf("ris", [128, 16], F32)
        sbuf("rw", [128, 8 * SC], F32)
        sbuf("vt", [128, KC * DH], BF16)
        for par in range(2):
            sbuf(f"wdenA{par}", [128, 16], F32)
            sbuf(f"wdenB{par}", [128, 16], F32)
            sbuf(f"wsum{par}", [128, 16], F32)
            sbuf(f"wcol_f{par}", [128, 16], F32)
            sbuf(f"ucol_f{par}", [128, 16], F32)
            sbuf(f"wcol_b{par}", [128, 16], BF16)
            sbuf(f"ucol_b{par}", [128, 16], BF16)
        for nm in ("wdenA", "wdenB", "wsum", "wcol_f", "ucol_f",
                   "wcol_b", "ucol_b"):
            ctx[nm] = [ctx[f"{nm}0"], ctx[f"{nm}1"]]
        sbuf("yf32_0", [128, D], F32); sbuf("yf32_1", [128, D], F32)
        ctx["yf32"] = [ctx["yf32_0"], ctx["yf32_1"]]

        AR1 = sbuf("AR1", [128, 32768], BF16)
        AR2 = sbuf("AR2", [128, 32832], BF16)
        AR5 = sbuf("AR5", [128, 8192], BF16)
        AR6 = sbuf("AR6", [128, 8192], BF16)
        AR7 = sbuf("AR7", [128, 8192], BF16)

        a1 = AR1.ap(); a2 = AR2.ap(); a5 = AR5.ap(); a6 = AR6.ap(); a7 = AR7.ap()
        ctx["wq_sb"] = a1[:, 0:4096]
        ctx["wk_sb"] = a1[:, 4096:8192]
        ctx["wv_sb"] = a1[:, 8192:12288]
        ctx["Abuf"] = a1[:, 0:32768]
        ctx["convacc"] = [a1[:, 0:4096].bitcast(F32), a1[:, 0:4096].bitcast(F32)]
        ctx["t1f"] = [a1[:, 16384:18432].bitcast(F32),
                      a1[:, 18432:20480].bitcast(F32),
                      a1[:, 20480:22528].bitcast(F32),
                      a1[:, 22528:24576].bitcast(F32)]
        ctx["ybf"] = [a1[:, 28672:29696], a1[:, 29696:30720],
                      a1[:, 30720:31744], a1[:, 31744:32768]]
        ctx["cacc0"] = a1[:, 24576:25600].bitcast(F32)
        ctx["x1t"] = a1[:, 0:16384]
        ctx["xc2t"] = a1[:, 16384:32768]
        ctx["combT"] = a1[:, 0:16384]
        ctx["ff1"] = a1[:, 0:4096]
        ctx["ff1t"] = a1[:, 4096:8192]
        ctx["nwdiag"] = a1[:, 8192:9216]
        ctx["xbf"] = a2[:, 0:16384]
        ctx["xT"] = a2[:, 16384:32768]
        ctx["Tbuf"] = a2[:, 0:32768]
        ctx["yt"] = a2[:, 0:16448]
        ctx["wet1"] = a2[:, 16448:24640]
        ctx["xct_a"] = a2[:, 24640:32832]
        ctx["combP"] = None
        ctx["cbf"] = [a2[:, 16448:17472], a2[:, 17472:18496]]
        ctx["t2bf"] = a2[:, 16448:32832]
        ctx["t2t"] = a2[:, 0:16384]
        ctx["qt"] = a5[:, 0:8192]
        ctx["ot"] = a5[:, 0:8192]
        ctx["pwt_sb"] = a5[:, 0:8192]
        ctx["hpf"] = [a5[:, 0:2048].bitcast(F32), a5[:, 2048:4096].bitcast(F32)]
        ctx["ostg"] = [a5[:, 4096:6144].bitcast(F32), a5[:, 6144:8192].bitcast(F32)]
        ctx["kt"] = a6[:, 0:8192]
        ctx["xct_b"] = a6[:, 0:8192]
        ctx["wet0"] = a6[:, 0:8192]
        ctx["owt_sb"] = a6[:, 0:8192]
        ctx["vna"] = a7[:, 0:8192]
        ctx["obuf"] = a7[:, 0:8192]
        ctx["fdt_sb"] = a7[:, 0:2048]
        ctx["fut_sb"] = a7[:, 2048:4096]
        ctx["wo_sb"] = a7[:, 4096:8192]
        ctx["zbuf"] = [a7[:, 4096:5120], a7[:, 5120:6144]]
        ctx["tmpf"] = a7[:, 4096:6144].bitcast(F32)
        ctx["trash"] = a7[:, 6144:7168]
        ctx["combP"] = [a5[:, 0:8192].bitcast(F32),
                        a2[:, 0:8192].bitcast(F32),
                        a2[:, 8192:16384].bitcast(F32),
                        a2[:, 24640:32832].bitcast(F32)]
        ctx["wet"] = [ctx["wet1"], ctx["wet0"]]

        psum("Lp", [128, 2048])
        psum("Cp", [128, 1024])
        psum("S0", [128, 512])
        psum("S1", [128, 512])
        ctx["CpB"] = ctx["Cp"][:, :].bitcast(BF16)

        def _run(me, eng):
            try:
                emit(me, eng)
            except Exception as ex:
                if type(ex).__name__ != "_Stop":
                    raise

        @block.tensor
        def _(eng):
            _run("tensor", eng)

        @block.vector
        def _(eng):
            _run("vector", eng)

        @block.scalar
        def _(eng):
            _run("scalar", eng)

        @block.gpsimd
        def _(eng):
            _run("gpsimd", eng)

        @block.sync
        def _(eng):
            _run("sync", eng)

    return nc


def _prep_core_inputs(b, inp):
    f = lambda a: np.ascontiguousarray(a, dtype=np.float32)
    d = {
        "x": f(np.asarray(inp["L0_attn_x"])[b]),
        "wq": f(inp["L0_attn_Wq"]), "wk": f(inp["L0_attn_Wk"]),
        "wv": f(inp["L0_attn_Wv"]),
        "bqc": f(np.asarray(inp["L0_attn_bq"]).reshape(4, 128).T),
        "bkc": f(np.asarray(inp["L0_attn_bk"]).reshape(4, 128).T),
        "bvb": f(np.broadcast_to(np.asarray(inp["L0_attn_bv"]), (128, HD))),
        "wo": f(inp["L0_attn_Wo"]),
        "bob": f(np.broadcast_to(np.asarray(inp["L0_attn_bo"]), (128, D))),
        "bor": f(np.asarray(inp["L0_attn_bo"]).reshape(1, D)),
        "dwc": f(np.asarray(inp["L0_ffn_dw_kernel"])
                 .reshape(DC, 128, 3).transpose(1, 0, 2).reshape(128, DC * 3)),
        "dwbc": f(np.asarray(inp["L0_ffn_dw_bias"]).reshape(DC, 128).T),
        "pwt": f(np.asarray(inp["L0_ffn_pw_w"]).T),
        "pwbc": f(np.asarray(inp["L0_ffn_pw_b"]).reshape(DC, 128).T),
        "rwt": f(np.asarray(inp["L0_ffn_router_w"]).T),
        "rbb": f(np.broadcast_to(np.asarray(inp["L0_ffn_router_b"]), (128, E))),
        "ewt": f(np.asarray(inp["L0_ffn_expert_w"]).transpose(0, 2, 1)),
        "ebr": f(inp["L0_ffn_expert_b"]),
        "owt": f(np.asarray(inp["L0_ffn_out_w"]).T),
        "obr": f(np.asarray(inp["L0_ffn_out_b"]).reshape(1, D)),
        "nwc": f(np.asarray(inp["L0_ffn_norm_w"]).reshape(DC, 128).T),
        "nwb": f(np.broadcast_to(np.asarray(inp["L0_ffn_norm_w"]), (128, D))),
        "fdt": f(np.asarray(inp["L0_ffn_ff_down_w"]).T),
        "fdbr": f(np.asarray(inp["L0_ffn_ff_down_b"]).reshape(1, HID)),
        "fut": f(np.asarray(inp["L0_ffn_ff_up_w"]).T),
        "fubr": f(np.asarray(inp["L0_ffn_ff_up_b"]).reshape(1, D)),
        "ident": np.eye(128, dtype=np.float32),
        "ones1": np.ones((1, 128), np.float32),
        "esel": np.kron(np.eye(E, dtype=np.float32), np.ones((1, 128), np.float32)),
    }
    return {"i_" + k: v for k, v in d.items()}


_NC_CACHE = {}


def kernel(**inputs):
    if "prog" not in _NC_CACHE:
        _NC_CACHE["prog"] = build_program()
    nc = _NC_CACHE["prog"]
    in_maps = [_prep_core_inputs(b, inputs) for b in range(8)]
    res = run_bass_kernel_spmd(nc, in_maps, core_ids=list(range(8)))
    return np.stack(
        [np.asarray(res.results[b]["out"], np.float32) for b in range(8)], axis=0)
